# version 10
# baseline (speedup 1.0000x reference)
import contextlib
import numpy as np
import concourse.bass as bass
import concourse.mybir as mybir
from concourse.ap import AP
from concourse.bass import IndirectOffsetOnAxis
from concourse.bass_utils import run_bass_kernel_spmd

F32 = mybir.dt.float32
BF16 = mybir.dt.bfloat16
I32 = mybir.dt.int32
AF = mybir.ActivationFunctionType
ALU = mybir.AluOpType
AX = mybir.AxisListType

S = 2048
DM = 1024
H = 16
E = 8
HF = 4096
NT = S // 128
SCALE = 0.125
CAP = 1024
BIGF = 1.0e6
TWO_PI = float(2.0 * np.pi)
HALF_PI = float(0.5 * np.pi)

ENGINES = ("tensor", "vector", "scalar", "gpsimd", "sync")


class KB:

    def __init__(self, nc, es):
        self.nc = nc
        self.streams = {e: [] for e in ENGINES}
        self.sems = {}
        self.cnt = {}
        self.last_wait = {e: {} for e in ENGINES}
        self._es = es

    def sem(self, name):
        self.sems[name] = self._es.enter_context(self.nc.semaphore(name))
        self.cnt[name] = 0

    def now(self, sem):
        return self.cnt[sem]

    AUTO = {"vector": "dve", "scalar": "act", "gpsimd": "gpc"}

    def op(self, eng, build, waits=(), inc=None, inck=1, unordered=False):
        if getattr(self, "stopped", False):
            return self.cnt.get(inc, 0)
        waits = list(waits)
        auto = self.AUTO.get(eng)
        if build is not None and auto is not None:
            waits.append((auto, self.cnt[auto]))
            if inc is None:
                inc = auto
                inck = 1
        if inc is not None and inck == 16 and not unordered:
            waits.append((inc, self.cnt[inc]))
        wl = []
        for s_, c in waits:
            if c > 0 and c > self.last_wait[eng].get(s_, 0):
                wl.append((s_, c))
                self.last_wait[eng][s_] = c
        if inc is not None:
            self.cnt[inc] += inck
        self.streams[eng].append((wl, build, inc, inck))
        return self.cnt[inc] if inc is not None else None

    def wait(self, eng, *waits):
        self.op(eng, None, waits=waits)

    def fence(self, engs=ENGINES):
        snap = [(s_, c) for s_, c in self.cnt.items() if c > 0]
        for e_ in engs:
            self.op(e_, None, waits=snap)

    def emit(self, blk):
        for eng in ENGINES:
            stream = self.streams[eng]
            if not stream:
                continue

            def make(stream=stream):
                def f(e):
                    for wl, build, inc, inck in stream:
                        for s_, c in wl:
                            e.wait_ge(self.sems[s_], c)
                        ins = build(e) if build is not None else None
                        if inc is not None:
                            if ins is None:
                                ins = e.nop()
                            ins.then_inc(self.sems[inc], inck)
                return f

            getattr(blk, eng)(make())


class _Stop(Exception):
    pass


def build_program(upto=None):
    nc = bass.Bass("TRN2", target_bir_lowering=False)

    d_x = nc.dram_tensor("x", [S, DM], F32, kind="ExternalInput")
    d_qkvw = nc.dram_tensor("qkvw", [3 * DM, DM], F32, kind="ExternalInput")
    d_outw = nc.dram_tensor("outw", [DM, DM], F32, kind="ExternalInput")
    d_bp = nc.dram_tensor("bp", [H], F32, kind="ExternalInput")
    d_ba = nc.dram_tensor("ba", [H], F32, kind="ExternalInput")
    d_freqs = nc.dram_tensor("freqs", [32], F32, kind="ExternalInput")
    d_wg = nc.dram_tensor("wg", [DM, E], F32, kind="ExternalInput")
    d_w1 = nc.dram_tensor("w1", [E, DM, HF], F32, kind="ExternalInput")
    d_b1 = nc.dram_tensor("b1", [E, HF], F32, kind="ExternalInput")
    d_w2 = nc.dram_tensor("w2", [E, HF, DM], F32, kind="ExternalInput")
    d_b2 = nc.dram_tensor("b2", [E, DM], F32, kind="ExternalInput")
    d_out = nc.dram_tensor("out", [S, DM], F32, kind="ExternalOutput")
    d_qkT = nc.dram_tensor("qkT_s", [DM, 2 * DM], BF16, kind="Internal")
    d_aT = nc.dram_tensor("aT_s", [DM, S], BF16, kind="Internal")
    d_band = nc.dram_tensor("band_s", [H, 128, 4096], BF16, kind="Internal")
    d_w1b = nc.dram_tensor("w1b_s", [E * DM * HF], BF16, kind="Internal")
    d_w2b = nc.dram_tensor("w2b_s", [E * HF * DM], BF16, kind="Internal")
    DM16 = DM + 16
    d_x1b = nc.dram_tensor("x1b_s", [S + 1, DM16], BF16, kind="Internal")
    d_idx = nc.dram_tensor("idx_s", [E * CAP + 128, 1], I32, kind="Internal")
    d_y = nc.dram_tensor("y_s", [S + 128, DM], F32, kind="Internal")

    es = contextlib.ExitStack()
    with es:
        kb = KB(nc, es)
        for s_ in ("pe", "act", "dve", "gpc", "out",
                   "dS", "dB2", "dXB", "dQW", "dTS", "dOW", "dBS",
                   "dWSL", "dBBF", "dSW", "dZA", "dRZ", "dAO", "dAOT", "dXR",
                   "dW1", "dW2", "dWC", "dWCo", "dID",
                   "dX1B", "dIX0", "dIX1", "dIDF", "dGX0", "dGX1", "dYZ", "dIL", "dYL",
                   "dY0", "dY1", "dY2", "dY3", "dY4", "dY5", "dY6", "dY7"):
            kb.sem(s_)

        def sb(stack, name, shape, dtype):
            return stack.enter_context(nc.sbuf_tensor(name, shape, dtype))

        P1 = es.enter_context(nc.psum_tensor("P1", [128, 1024], F32))
        P2 = es.enter_context(nc.psum_tensor("P2", [128, 1024], F32))
        P3 = es.enter_context(nc.psum_tensor("P3", [128, 1024], F32))
        P4t = es.enter_context(nc.psum_tensor("P4t", [128, 1024], F32))
        P4 = P4t[:, 0:512]
        P6 = P1[:].bitcast(BF16).rearrange("p (a b c) -> p a b c", a=2, b=8)

        ident = sb(es, "ident", [128, 128], BF16)
        psw = sb(es, "psw", [128, 128], BF16)
        onebf = sb(es, "onebf", [1, 1], BF16)
        ones1f = sb(es, "ones1f", [1, 128], F32)
        ones16 = sb(es, "ones16", [128, NT], F32)
        CONSTS = [0.0, 1.0, -1.0, 1e-5, -0.5, 0.125, HALF_PI]
        csb = sb(es, "csb", [128, len(CONSTS)], F32)
        for ci, cv in enumerate(CONSTS):
            nc.const_aps.register(F32, cv, csb[:, ci:ci + 1])
        b1sb = sb(es, "b1sb", [128, E, 32], F32)
        b2sb = sb(es, "b2sb", [E, DM], BF16)
        wghi = sb(es, "wghi", [128, 8, E], BF16)
        wglo = sb(es, "wglo", [128, 8, E], BF16)
        varall = sb(es, "varall", [128, NT], F32)
        rstd = sb(es, "rstd", [128, NT], F32)
        meancol = sb(es, "meancol", [128, 1], F32)

        episb = sb(es, "episb", [128, DM], F32)
        sqt = episb

        def _maybe_stop(stage):
            if upto != stage:
                return
            kb.fence()
            gm = kb.op("gpsimd", lambda e: e.memset(episb[:], 0.0))
            for tt_ in range(NT):
                kb.op("sync", lambda e, tt_=tt_: e.dma_start(out=d_out.ap()[tt_ * 128:(tt_ + 1) * 128, :], in_=episb[:]),
                      waits=[("gpc", kb.now("gpc"))], inc="out", inck=16)
            kb.wait("sync", ("out", NT * 16))
            kb.stopped = True

        try:
            att_era = es.enter_context(contextlib.ExitStack())
            xT = sb(att_era, "xT", [128, 8, S], BF16)
            V = sb(att_era, "V", [128, NT, H * 65], BF16)
            C4 = sb(att_era, "C4", [128, S], F32)
            S4 = sb(att_era, "S4", [128, S], F32)
            with contextlib.ExitStack() as st1:
                posrow = sb(st1, "posrow", [1, S], F32)
                sgni = sb(st1, "sgni", [128, 1], I32)
                bandabs = sb(st1, "bandabs", [128, 4096], F32)
                qkvwb = sb(st1, "qkvwb", [128, 6, DM], BF16)
                qkvwTv = sb(st1, "qkvwTv", [128, 8, DM], BF16)
                tstage = sb(st1, "tstage", [128, 2, 8, 128], BF16)
                xb = sb(st1, "xb", [128, 2, DM], BF16)
                xbf = sb(st1, "xbf", [128, 2, DM], F32)
                wgf = sb(st1, "wgf", [128, 8, E], F32)
                bparow = sb(st1, "bparow", [1, 2, H], F32)
                paac = sb(st1, "paac", [128, H], F32)
                npac = sb(st1, "npac", [128, H], F32)

                kb.op("sync", lambda e: e.dma_start(out=bparow[:, 0], in_=AP(tensor=d_ba, offset=0, ap=[[1, 1], [1, H]])), inc="dS", inck=16)
                kb.op("sync", lambda e: e.dma_start(out=bparow[:, 1], in_=AP(tensor=d_bp, offset=0, ap=[[1, 1], [1, H]])), inc="dS", inck=16)
                for e_i in range(E):
                    for jc in range(32):
                        kb.op("sync", lambda e, e_i=e_i, jc=jc: e.dma_start(
                            out=b1sb[:, e_i, jc:jc + 1],
                            in_=AP(tensor=d_b1, offset=e_i * HF + jc * 128, ap=[[1, 128], [1, 1]])), inc="dS", inck=16, unordered=True)
                kb.op("sync", lambda e: e.dma_start(out=wgf[:], in_=AP(tensor=d_wg, offset=0, ap=[[E, 128], [128 * E, 8], [1, E]])), inc="dS", inck=16)
                sld_small = kb.now("dS")
                b2_gld = kb.op("gpsimd", lambda e: e.dma_start(out=b2sb[:], in_=d_b2.ap()), inc="dB2", inck=16)

                for ci, cv in enumerate(CONSTS):
                    kb.op("vector", lambda e, ci=ci, cv=cv: e.memset(csb[:, ci:ci + 1], cv))
                imz = kb.op("vector", lambda e: e.memset(ident[:], 0.0))
                ob = kb.op("vector", lambda e: e.memset(onebf[:], 1.0))
                ident_gpc = 0
                for ii in range(128):
                    kb.op("sync", lambda e, ii=ii: e.dma_start(
                        out=AP(tensor=ident, offset=ii * 128 + ii, ap=[[128, 1], [1, 1]]),
                        in_=onebf[:]),
                        waits=[("dve", ob)], inc="dID", inck=16, unordered=True)
                ident_dma = kb.now("dID")
                pm = kb.op("vector", lambda e: e.memset(psw[:], 0.0), inc="dve")
                for (ob, ib) in ((0, 32), (32, 0), (64, 96), (96, 64)):
                    kb.op("sync", lambda e, ob=ob, ib=ib: e.dma_start(
                        out=psw[ib:ib + 32, ob:ob + 32], in_=ident[ib:ib + 32, ib:ib + 32]),
                        waits=[("dID", ident_dma), ("dve", pm)], inc="dSW", inck=16, unordered=True)
                psw_ld = kb.now("dSW")
                kb.op("vector", lambda e: e.memset(ones1f[:], 1.0), inc="dve")
                v16 = kb.op("vector", lambda e: e.memset(ones16[:], 1.0), inc="dve")
                for vh in range(H):
                    kb.op("vector", lambda e, vh=vh: e.tensor_copy(
                        out=AP(tensor=V, offset=vh * 65 + 64, ap=[[NT * H * 65, 128], [H * 65, NT], [1, 1]]),
                        in_=ones16[:]),
                        waits=[("dve", v16)], inc="dve" if vh == H - 1 else None)
                gpc_const = kb.now("gpc")

                _maybe_stop("s1")
                _pc_state = {"i": 0}
                W1_ELEMS = E * DM * HF
                PC_TOTAL = 2 * W1_ELEMS // (128 * 4096)

                def precast_pump(n):
                    for _ in range(n):
                        i = _pc_state["i"]
                        if i >= PC_TOTAL:
                            return
                        _pc_state["i"] = i + 1
                        if i < PC_TOTAL // 2:
                            src_t, dst_t, off = d_w1, d_w1b, i * 128 * 4096
                        else:
                            src_t, dst_t, off = d_w2, d_w2b, (i - PC_TOTAL // 2) * 128 * 4096
                        kb.op("gpsimd", lambda e, src_t=src_t, dst_t=dst_t, off=off: e.dma_start(
                            out=AP(tensor=dst_t, offset=off, ap=[[4096, 128], [1, 4096]]),
                            in_=AP(tensor=src_t, offset=off, ap=[[4096, 128], [1, 4096]])),
                            inc="dWCo", inck=16)

                qkT_sld = [0, 0]
                qk_evs = [0, 0]
                for rnd in range(4):
                    gld_qkvw = kb.op("gpsimd", lambda e, rnd=rnd: e.dma_start(
                        out=qkvwb[:], in_=d_qkvw.ap()[rnd * 6 * 128:(rnd + 1) * 6 * 128, :].rearrange("(c p) d -> p c d", p=128)),
                        waits=[("pe", kb.now("pe") if rnd else 0)], inc="dQW", inck=16)
                    for cc12 in range(6):
                        cc = rnd * 6 + cc12
                        par = cc % 2
                        for dc in range(8):
                            kb.op("tensor", lambda e, cc12=cc12, dc=dc, par=par: e.transpose(P6[:, par, dc], qkvwb[:, cc12, dc * 128:(dc + 1) * 128], ident[:]),
                                  waits=[("dQW", gld_qkvw), ("dID", ident_dma), ("dve", qk_evs[par])],
                                  inc="pe" if dc == 7 else None)
                        if cc < 16:
                            qk_evs[par] = kb.op("vector", lambda e, par=par: e.tensor_copy(out=tstage[:, par], in_=P6[:, par]),
                                  waits=[("pe", kb.now("pe")), ("dTS", qkT_sld[par])], inc="dve")
                            qkT_sld[par] = kb.op("sync", lambda e, cc=cc, par=par: e.dma_start(
                                out=AP(tensor=d_qkT, offset=cc * 128, ap=[[2 * DM, 128], [128 * 2 * DM, 8], [1, 128]]),
                                in_=tstage[:, par]),
                                waits=[("dve", kb.now("dve"))], inc="dTS", inck=16)
                        else:
                            qk_evs[par] = kb.op("vector", lambda e, cc=cc, par=par: e.tensor_copy(
                                out=AP(tensor=qkvwTv, offset=(cc - 16) * 128, ap=[[8 * DM, 128], [DM, 8], [1, 128]]),
                                in_=P6[:, par]),
                                waits=[("pe", kb.now("pe"))], inc="dve")
                qkT_store_sld = kb.now("dTS")
                kb.op("gpsimd", lambda e: e.iota(posrow[:], pattern=[[1, S]], base=0, channel_multiplier=0, allow_small_or_imprecise_dtypes=True), inc="gpc")
                kb.op("gpsimd", lambda e: e.iota(sgni[:], pattern=[[1, 1]], base=0, channel_multiplier=1), inc="gpc")
                kb.op("gpsimd", lambda e: e.iota(bandabs[:], pattern=[[-1, 4096]], base=1920, channel_multiplier=1, allow_small_or_imprecise_dtypes=True), inc="gpc")
                iota_gpc = kb.now("gpc")
                precast_pump(4)
                _maybe_stop("s3")
                xT_evs = [qk_evs[0], qk_evs[1]]
                xc_dve = [0, 0]
                for tt in range(NT):
                    gld_x = kb.op("sync", lambda e, tt=tt: e.dma_start(out=xbf[:, tt % 2], in_=d_x.ap()[tt * 128:(tt + 1) * 128, :]),
                                  waits=[("dve", xc_dve[tt % 2])], inc="dXB", inck=16)
                    xc_dve[tt % 2] = kb.op("vector", lambda e, tt=tt: e.tensor_copy(out=xb[:, tt % 2], in_=xbf[:, tt % 2]),
                                           waits=[("dXB", gld_x), ("pe", kb.now("pe") if tt >= 2 else 0)], inc="dve")
                    for c in range(8):
                        kb.op("tensor", lambda e, tt=tt, c=c: e.transpose(P6[:, tt % 2, c], xb[:, tt % 2, c * 128:(c + 1) * 128], ident[:]),
                              waits=[("dve", xc_dve[tt % 2]), ("dID", ident_dma), ("dve", xT_evs[tt % 2])],
                              inc="pe" if c == 7 else None)
                    xT_evs[tt % 2] = kb.op("vector", lambda e, tt=tt: e.tensor_copy(
                        out=AP(tensor=xT, offset=tt * 128, ap=[[8 * S, 128], [S, 8], [1, 128]]),
                        in_=P6[:, tt % 2]),
                        waits=[("pe", kb.now("pe"))], inc="dve")
                xT_dve = kb.now("dve")
                _maybe_stop("s2")


                for tt in range(NT):
                    for hf in range(2):
                        for d in range(8):
                            kb.op("tensor", lambda e, tt=tt, hf=hf, d=d: e.matmul(
                                P4, xT[:, d, tt * 128:(tt + 1) * 128],
                                qkvwTv[:, d, hf * 512:(hf + 1) * 512],
                                start=(d == 0), stop=(d == 7)),
                                waits=[("dve", xT_dve if (tt == 0 and hf == 0) else kb.now("dve"))],
                                inc="pe" if d == 7 else None)
                        pv = kb.now("pe")
                        kb.op("vector", lambda e, tt=tt, hf=hf: e.tensor_copy(
                            out=AP(tensor=V, offset=tt * H * 65 + hf * 8 * 65, ap=[[NT * H * 65, 128], [65, 8], [1, 64]]),
                            in_=AP(tensor=P4t, offset=0, ap=[[1024, 128], [64, 8], [1, 64]])),
                            waits=[("pe", pv)], inc="dve")
                V_dve = kb.now("dve")
                precast_pump(8)
                _maybe_stop("s4")


                kb.op("vector", lambda e: e.tensor_copy(out=wghi[:], in_=wgf[:]), waits=[("dS", sld_small)], inc="dve")
                kb.op("vector", lambda e: e.scalar_tensor_tensor(out=wglo[:], in0=wghi[:], scalar=-1.0, in1=wgf[:], op0=ALU.mult, op1=ALU.add), inc="dve")

                with contextlib.ExitStack() as st2:
                    freq4 = sb(st2, "freq4", [128, 1], F32)
                    angs = sb(st2, "angs", [128, S], F32)
                    angred = sb(st2, "angred", [128, S], F32)
                    angi = sb(st2, "angi", [128, S], I32)
                    sgnf = sb(st2, "sgnf", [128, 1], F32)
                    sgncol = sb(st2, "sgncol", [128, 1], F32)
                    sgt = sb(st2, "sgt", [128, S], F32)

                    kb.op("sync", lambda e: e.dma_start(out=freq4[:], in_=AP(tensor=d_freqs, offset=0, ap=[[0, 4], [1, 32], [1, 1]])), inc="dS", inck=16)
                    sld_freq = kb.now("dS")
                    gpc_rope = iota_gpc
                    for qt in range(4):
                        kb.op("tensor", lambda e, qt=qt: e.matmul(P4, ones1f[:], posrow[:, qt * 512:(qt + 1) * 512], start=True, stop=True),
                              waits=[("gpc", gpc_rope), ("dve", kb.now("dve")), ("dS", sld_freq)], inc="pe")
                        kb.op("vector", lambda e, qt=qt: e.tensor_scalar(out=angs[:, qt * 512:(qt + 1) * 512], in0=P4, scalar1=freq4[:], scalar2=None, op0=ALU.mult),
                              waits=[("pe", kb.now("pe"))], inc="dve")
                    kb.op("vector", lambda e: e.tensor_scalar(out=angred[:], in0=angs[:], scalar1=1.0 / TWO_PI, scalar2=None, op0=ALU.mult), inc="dve")
                    kb.op("vector", lambda e: e.tensor_copy(out=angi[:], in_=angred[:]), inc="dve")
                    kb.op("vector", lambda e: e.tensor_copy(out=angred[:], in_=angi[:]), inc="dve")
                    kb.op("vector", lambda e: e.scalar_tensor_tensor(out=angred[:], in0=angred[:], scalar=-TWO_PI, in1=angs[:], op0=ALU.mult, op1=ALU.add), inc="dve")
                    kb.op("vector", lambda e: e.tensor_scalar(out=sgt[:], in0=angred[:], scalar1=float(np.pi), scalar2=-TWO_PI, op0=ALU.is_gt, op1=ALU.mult), inc="dve")
                    kb.op("vector", lambda e: e.tensor_add(out=angred[:], in0=angred[:], in1=sgt[:]), inc="dve")
                    kb.op("vector", lambda e: e.tensor_scalar(out=sgt[:], in0=angred[:], scalar1=float(-np.pi), scalar2=TWO_PI, op0=ALU.is_lt, op1=ALU.mult), inc="dve")
                    kb.op("vector", lambda e: e.tensor_add(out=angred[:], in0=angred[:], in1=sgt[:]), inc="dve")
                    dve_ang = kb.now("dve")
                    act_sin = kb.op("scalar", lambda e: e.activation(out=S4[:], in_=angred[:], func=AF.Sin), waits=[("dve", dve_ang)], inc="act")
                    aab = kb.op("scalar", lambda e: e.activation(out=angs[:], in_=angred[:], func=AF.Abs),
                                waits=[("act", act_sin), ("dve", dve_ang)], inc="act")
                    kb.op("scalar", lambda e: e.activation(out=C4[:], in_=angs[:], func=AF.Sin, bias=HALF_PI, scale=-1.0), waits=[("act", aab)], inc="act")
                    kb.op("vector", lambda e: e.tensor_scalar(out=sgni[:], in0=sgni[:], scalar1=32, scalar2=None, op0=ALU.bitwise_and), waits=[("gpc", gpc_rope)], inc="dve")
                    kb.op("vector", lambda e: e.tensor_copy(out=sgnf[:], in_=sgni[:]), inc="dve")
                    kb.op("vector", lambda e: e.tensor_scalar(out=sgncol[:], in0=sgnf[:], scalar1=1.0 / 16.0, scalar2=-1.0, op0=ALU.mult, op1=ALU.add), inc="dve")
                    kb.op("vector", lambda e: e.tensor_scalar(out=S4[:], in0=S4[:], scalar1=sgncol[:], scalar2=None, op0=ALU.mult),
                          waits=[("act", kb.now("act"))], inc="dve")

                    kb.op("tensor", lambda e: e.matmul(P3[:, 0:H], ones1f[:], bparow[:, 0], start=True, stop=True),
                          waits=[("dS", sld_small), ("dve", kb.now("dve"))], inc="pe")
                    kb.op("vector", lambda e: e.tensor_scalar(out=paac[:], in0=P3[:, 0:H], scalar1=0.01, scalar2=None, op0=ALU.max),
                          waits=[("pe", kb.now("pe"))], inc="dve")
                    kb.op("tensor", lambda e: e.matmul(P3[:, 0:H], ones1f[:], bparow[:, 1], start=True, stop=True),
                          waits=[("dve", kb.now("dve"))], inc="pe")
                    kb.op("vector", lambda e: e.tensor_scalar(out=npac[:], in0=P3[:, 0:H], scalar1=0.01, scalar2=-1.0, op0=ALU.max, op1=ALU.mult),
                          waits=[("pe", kb.now("pe"))], inc="dve")

                _maybe_stop("s5")
                kb.fence()
                with contextlib.ExitStack() as st3:
                    bandtmp = sb(st3, "bandtmp", [128, 2, 4096], BF16)
                    bandst = sb(st3, "bandst", [128, 2, 4096], BF16)

                    kb.op("scalar", lambda e: e.activation(out=bandabs[:], in_=bandabs[:], func=AF.Abs),
                          waits=[("gpc", iota_gpc)], inc="act")
                    babs_act = kb.now("act")
                    babs_dve = kb.now("dve")
                    bst_st = [0, 0]
                    bsc_mark = [0, 0]
                    for h in range(H):
                        a_ln = kb.op("scalar", lambda e, h=h: e.activation(out=bandtmp[:, h % 2], in_=bandabs[:], func=AF.Ln, bias=1.0, scale=paac[:, h:h + 1]),
                                     waits=[("act", babs_act), ("dve", max(babs_dve, bsc_mark[h % 2]))], inc="act")
                        d_sc = kb.op("vector", lambda e, h=h: e.tensor_scalar(out=bandst[:, h % 2], in0=bandtmp[:, h % 2], scalar1=npac[:, h:h + 1], scalar2=None, op0=ALU.mult),
                                     waits=[("act", a_ln), ("dBS", bst_st[h % 2])], inc="dve")
                        bsc_mark[h % 2] = d_sc
                        bst_st[h % 2] = kb.op("sync", lambda e, h=h: e.dma_start(out=d_band.ap()[h], in_=bandst[:, h % 2]),
                                              waits=[("dve", d_sc)], inc="dBS", inck=16)
                    band_sld = kb.now("dBS")
                setup_dve = kb.now("dve")
                setup_act = kb.now("act")
                kb.wait("tensor", ("dve", setup_dve), ("act", setup_act), ("dBS", kb.now("dBS")), ("dTS", kb.now("dTS")))
                setup_bar = kb.op("tensor", lambda e: e.nop(), inc="pe")
            _maybe_stop("setup")
            kb.fence()

            with contextlib.ExitStack() as at:
                wsl = sb(at, "wsl", [128, 4, 8, 256], BF16)
                qbf = sb(at, "qbf", [128, S], BF16)
                ctmp = sb(at, "ctmp", [128, S], F32)
                qr = sb(at, "qr", [128, S], BF16)
                kr = sb(at, "kr", [128, S], BF16)
                Pp = sb(at, "Pp", [128, 2, 1024], BF16)
                araw = sb(at, "araw", [64, 2, S], BF16)
                aout = sb(at, "aout", [64, S], BF16)
                bandbf = sb(at, "bandbf", [128, 2, 4096], BF16)
                zasm = sb(at, "zasm", [4, 2, 512], F32)
                rzasm = sb(at, "rzasm", [4, 512], F32)
                rzflat = sb(at, "rzflat", [1, 2, 4, 512], F32)
                ztmp = sb(at, "ztmp", [1, 2, 512], F32)

                psS = (P1, P2)
                psV_banks = (P3, P4t)

                exp_mark = [0, 0]
                kb_mark = [0, 0]
                zrec_mark = 0
                psv_z_p = [0, 0]
                psv_dve_p = [0, 0]
                araw_free_p = [0, 0]
                pending_tail = []
                nonlocal_state = {"zrec_mark": 0, "aout_dma": 0, "ztail_pe": 0}

                qkv_done = [0, 0, 0, 0]
                swp_war = [0, 0]
                for p in range(H // 2):
                    precast_pump(15)
                    war = qkv_done[p % 4]
                    w_ld = kb.op("sync", lambda e, p=p: e.dma_start(
                        out=wsl[:, p % 4, :, 0:128],
                        in_=AP(tensor=d_qkT, offset=p * 128, ap=[[2 * DM, 128], [128 * 2 * DM, 8], [1, 128]])),
                        waits=[("pe", war), ("dTS", qkT_store_sld)], inc="dWSL", inck=16)
                    w_ld = kb.op("sync", lambda e, p=p: e.dma_start(
                        out=wsl[:, p % 4, :, 128:256],
                        in_=AP(tensor=d_qkT, offset=DM + p * 128, ap=[[2 * DM, 128], [128 * 2 * DM, 8], [1, 128]])),
                        inc="dWSL", inck=16)
                    for hh in range(2):
                        h = 2 * p + hh
                        kb.op("sync", lambda e, h=h: e.dma_start(out=bandbf[:, h % 2], in_=d_band.ap()[h]),
                              waits=[("dBS", band_sld), ("pe", kb_mark[h % 2])], inc="dBBF", inck=16)
                    band_ld = kb.now("dBBF")

                    for w, dst, dsc in ((0, qr, SCALE), (1, kr, 1.0)):
                        evac_marks = {}
                        for qt in range(4):
                            g = (w * 4 + qt) % 2
                            psq = P1[:, 0:512] if g == 0 else P2[:, 0:512]
                            pssw = P1[:, 512:1024] if g == 0 else P2[:, 512:1024]
                            for d in range(8):
                                kb.op("tensor", lambda e, p=p, w=w, qt=qt, d=d, psq=psq: e.matmul(
                                    psq, wsl[:, p % 4, d, w * 128:(w + 1) * 128],
                                    xT[:, d, qt * 512:(qt + 1) * 512],
                                    start=(d == 0), stop=(d == 7)),
                                    waits=[("dWSL", w_ld), ("pe", setup_bar),
                                           ("act", max(exp_mark[0], exp_mark[1], evac_marks.get(g, 0))),
                                           ("dve", kb.now("dve"))],
                                    inc="pe" if d == 7 else None)
                            ev = kb.op("scalar", lambda e, qt=qt, dsc=dsc, psq=psq: e.activation(
                                out=qbf[:, qt * 512:(qt + 1) * 512], in_=psq, func=AF.Copy, scale=dsc),
                                waits=[("pe", kb.now("pe"))], inc="act")
                            evac_marks[g] = ev
                            sm = kb.op("tensor", lambda e, qt=qt, pssw=pssw: e.matmul(
                                pssw, psw[:], qbf[:, qt * 512:(qt + 1) * 512], start=True, stop=True),
                                waits=[("act", ev), ("dSW", psw_ld), ("dve", swp_war[g])], inc="pe")
                            kb.op("vector", lambda e, qt=qt: e.tensor_mul(
                                out=ctmp[:, qt * 512:(qt + 1) * 512], in0=qbf[:, qt * 512:(qt + 1) * 512],
                                in1=C4[:, qt * 512:(qt + 1) * 512]),
                                waits=[("act", ev)], inc="dve")
                            kb.op("vector", lambda e, qt=qt, dst=dst, pssw=pssw: e.tensor_mul(
                                out=dst[:, qt * 512:(qt + 1) * 512], in0=pssw,
                                in1=S4[:, qt * 512:(qt + 1) * 512]),
                                waits=[("pe", sm)], inc="dve")
                            swp_war[g] = kb.op("vector", lambda e, qt=qt, dst=dst: e.tensor_add(
                                out=dst[:, qt * 512:(qt + 1) * 512], in0=dst[:, qt * 512:(qt + 1) * 512],
                                in1=ctmp[:, qt * 512:(qt + 1) * 512]), inc="dve")
                    rope_dve = kb.now("dve")
                    qkv_done[p % 4] = kb.now("pe")

                    for hh in range(2):
                        h = 2 * p + hh
                        hb = hh * 64
                        bpar = h % 2
                        vpar = h % 2
                        PV = psV_banks[vpar]
                        for qh in range(2):
                            q0 = qh * 1024
                            evs = [0] * 16

                            def emit_attnV(kt, h=h, PV=PV, vpar=vpar):
                                for j in range(2):
                                    kb.op("tensor", lambda e, kt=kt, h=h, j=j, PV=PV: e.matmul(
                                        PV[0:65, j * 512:(j + 1) * 512], V[:, kt, h * 65:(h + 1) * 65],
                                        Pp[:, kt % 2, j * 512:(j + 1) * 512],
                                        start=(kt == 0), stop=(kt == 15)),
                                        waits=[("act", evs[kt])] +
                                              ([("dZA", psv_z_p[vpar]), ("dve", psv_dve_p[vpar])] if kt == 0 else []),
                                        inc="pe" if (j == 1 and kt == 15) else None)

                            for kt in range(16):
                                par = kt % 2
                                for j in range(2):
                                    kb.op("tensor", lambda e, hb=hb, kt=kt, q0=q0, j=j, par=par: e.matmul(
                                        psS[par][:, j * 512:(j + 1) * 512],
                                        kr[hb:hb + 64, kt * 128:(kt + 1) * 128],
                                        qr[hb:hb + 64, q0 + j * 512:q0 + (j + 1) * 512],
                                        start=True, stop=False),
                                        waits=[("dve", rope_dve), ("dBBF", band_ld),
                                               ("act", exp_mark[par])])
                                for j in range(2):
                                    u0 = q0 + j * 512 + 1920 - kt * 128
                                    r = kb.op("tensor", lambda e, par=par, bpar=bpar, u0=u0, j=j: e.matmul(
                                        psS[par][:, j * 512:(j + 1) * 512],
                                        ident[:], bandbf[:, bpar, u0:u0 + 512],
                                        start=False, stop=True),
                                        inc="pe" if j == 1 else None)
                                    if r is not None:
                                        kb_mark[bpar] = r
                                sdone = kb.now("pe")
                                if kt >= 1:
                                    emit_attnV(kt - 1)
                                if qh == 0 and kt == 8 and pending_tail:
                                    pending_tail.pop(0)()
                                exp_mark[par] = kb.op("scalar", lambda e, par=par: e.activation(
                                    out=Pp[:, par, :], in_=psS[par][:], func=AF.Exp),
                                    waits=[("pe", sdone)], inc="act")
                                evs[kt] = exp_mark[par]
                            emit_attnV(15)
                            pv_done = kb.now("pe")
                            for j in range(2):
                                kb.op("scalar", lambda e, q0=q0, j=j, PV=PV, vpar=vpar: e.activation(
                                    out=araw[:, vpar, q0 + j * 512:q0 + (j + 1) * 512], in_=PV[0:64, j * 512:(j + 1) * 512], func=AF.Copy),
                                    waits=[("pe", pv_done), ("dve", araw_free_p[vpar])], inc="act")
                                za = kb.op("scalar", lambda e, j=j, PV=PV: e.activation(
                                    out=ztmp[:, j, :], in_=PV[64:65, j * 512:(j + 1) * 512], func=AF.Copy),
                                    waits=[("pe", pv_done), ("dZA", kb.now("dZA"))], inc="act")
                                psv_z_p[vpar] = kb.op("sync", lambda e, qh=qh, j=j, vpar=vpar: e.dma_start(
                                    out=zasm[2 * qh + j:2 * qh + j + 1, vpar, :], in_=ztmp[:, j, :]),
                                    waits=[("act", za), ("dve", nonlocal_state["zrec_mark"])], inc="dZA", inck=16)
                        araw_act = kb.now("act")

                        def make_tail(h=h, vpar=vpar, PV=PV, araw_act=araw_act, z_sld=None):
                            z_sld = psv_z_p[vpar]

                            def tail():
                                nonlocal_state["zrec_mark"] = kb.op("vector",
                                    lambda e, vpar=vpar: e.reciprocal(out=rzasm[:], in_=zasm[:, vpar, :]),
                                    waits=[("dZA", z_sld), ("pe", nonlocal_state["ztail_pe"])], inc="dve")
                                rz_ld = kb.op("sync", lambda e, vpar=vpar: e.dma_start(
                                    out=rzflat[0:1, vpar], in_=rzasm[:]),
                                    waits=[("dve", nonlocal_state["zrec_mark"])], inc="dRZ", inck=16)
                                for qj in range(4):
                                    kb.op("tensor", lambda e, qj=qj, PV=PV, vpar=vpar: e.matmul(
                                        PV[0:64, 0:512], ones1f[:, 0:64], rzflat[0:1, vpar, qj, :], start=True, stop=True),
                                        waits=[("dRZ", rz_ld), ("act", araw_act), ("dve", kb.now("dve"))], inc="pe")
                                    pd = kb.op("vector", lambda e, qj=qj, PV=PV, vpar=vpar: e.tensor_tensor(
                                        out=aout[:, qj * 512:(qj + 1) * 512],
                                        in0=araw[:, vpar, qj * 512:(qj + 1) * 512], in1=PV[0:64, 0:512], op=ALU.mult),
                                        waits=[("pe", kb.now("pe")), ("dAO", nonlocal_state["aout_dma"])], inc="dve")
                                    psv_dve_p[vpar] = pd
                                    nonlocal_state["aout_dma"] = kb.op("sync", lambda e, h=h, qj=qj: e.dma_start(
                                        out=d_aT.ap()[h * 64:(h + 1) * 64, qj * 512:(qj + 1) * 512],
                                        in_=aout[:, qj * 512:(qj + 1) * 512]),
                                        waits=[("dve", pd)], inc="dAO", inck=16)
                                araw_free_p[vpar] = psv_dve_p[vpar]
                                nonlocal_state["ztail_pe"] = kb.now("pe")
                            return tail

                        pending_tail.append(make_tail())
                while pending_tail:
                    pending_tail.pop(0)()
                precast_pump(10**9)
                att_sld = kb.now("dAO")
                kb.wait("tensor", ("dAO", att_sld), ("dZA", kb.now("dZA")), ("dRZ", kb.now("dRZ")), ("dSW", kb.now("dSW")), ("dve", kb.now("dve")), ("act", kb.now("act")))
                att_bar = kb.op("tensor", lambda e: e.nop(), inc="pe")
            att_era.close()
            _maybe_stop("attention")
            kb.fence()

            with contextlib.ExitStack() as pr:
                m_all = sb(pr, "m_all", [128, NT, E], F32)
                idxall = sb(pr, "idxall", [128, E, 8], I32)
                mT8 = sb(pr, "mT8", [8, S], BF16)
                x1s = es.enter_context(contextlib.ExitStack())
                x1 = sb(x1s, "x1", [128, NT, DM], F32)
                with contextlib.ExitStack() as pro:
                    outwb = sb(pro, "outwb", [128, 8, DM], BF16)
                    outwT = sb(pro, "outwT", [128, 8, DM], BF16)
                    aoT = sb(pro, "aoT", [128, 2, 8, 128], BF16)
                    xres = sb(pro, "xres", [128, 2, DM], F32)

                    gld_outw = kb.op("gpsimd", lambda e: e.dma_start(out=outwb[:], in_=d_outw.ap().rearrange("(c p) d -> p c d", p=128)),
                                     waits=[("pe", att_bar)], inc="dOW", inck=16)
                    ow_evs = [0, 0]
                    for cc in range(8):
                        for dc in range(8):
                            kb.op("tensor", lambda e, cc=cc, dc=dc: e.transpose(P6[:, cc % 2, dc], outwb[:, cc, dc * 128:(dc + 1) * 128], ident[:]),
                                  waits=[("dOW", gld_outw), ("dve", ow_evs[cc % 2])],
                                  inc="pe" if dc == 7 else None)
                        ow_evs[cc % 2] = kb.op("vector", lambda e, cc=cc: e.tensor_copy(
                            out=AP(tensor=outwT, offset=cc * 128, ap=[[8 * DM, 128], [DM, 8], [1, 128]]),
                            in_=P6[:, cc % 2]),
                            waits=[("pe", kb.now("pe"))], inc="dve")
                    owt_dve = kb.now("dve")

                    for tt in range(NT):
                        war = kb.now("pe") if tt >= 2 else 0
                        ld1 = kb.op("sync", lambda e, tt=tt: e.dma_start(
                            out=aoT[:, tt % 2],
                            in_=AP(tensor=d_aT, offset=tt * 128, ap=[[S, 128], [128 * S, 8], [1, 128]])),
                            waits=[("dAO", att_sld), ("pe", war)], inc="dAOT", inck=16)
                        war2 = kb.now("dve") if tt >= 2 else 0
                        ld2 = kb.op("sync", lambda e, tt=tt: e.dma_start(out=xres[:, tt % 2], in_=d_x.ap()[tt * 128:(tt + 1) * 128, :]),
                                    waits=[("dve", war2)], inc="dXR", inck=16)
                        for dd in range(2):
                            for c in range(8):
                                kb.op("tensor", lambda e, tt=tt, dd=dd, c=c: e.matmul(
                                    P1[:, dd * 512:(dd + 1) * 512], aoT[:, tt % 2, c, :],
                                    outwT[:, c, dd * 512:(dd + 1) * 512],
                                    start=(c == 0), stop=(c == 7)),
                                    waits=[("dAOT", ld1), ("dve", owt_dve), ("dve", kb.now("dve"))],
                                    inc="pe" if c == 7 else None)
                        mm_tt = kb.now("pe")
                        for dd in range(2):
                            kb.op("vector", lambda e, tt=tt, dd=dd: e.tensor_add(
                                out=x1[:, tt, dd * 512:(dd + 1) * 512], in0=xres[:, tt % 2, dd * 512:(dd + 1) * 512],
                                in1=P1[:, dd * 512:(dd + 1) * 512]),
                                waits=[("pe", mm_tt), ("dXR", ld2)], inc="dve")
                        kb.op("vector", lambda e, tt=tt: e.reduce_sum(out=meancol[:], in_=x1[:, tt, :], axis=AX.X), inc="dve")
                        kb.op("vector", lambda e: e.tensor_scalar(out=meancol[:], in0=meancol[:], scalar1=1.0 / DM, scalar2=None, op0=ALU.mult), inc="dve")
                        kb.op("vector", lambda e, tt=tt: e.tensor_scalar(out=x1[:, tt, :], in0=x1[:, tt, :], scalar1=meancol[:], scalar2=None, op0=ALU.subtract), inc="dve")
                        kb.op("scalar", lambda e, tt=tt: e.activation(out=sqt[:], in_=x1[:, tt, :], func=AF.Square, accum_out=varall[:, tt:tt + 1]),
                              waits=[("dve", kb.now("dve")), ("act", kb.now("act"))], inc="act")
                kb.op("vector", lambda e: e.tensor_scalar(out=varall[:], in0=varall[:], scalar1=1.0 / DM, scalar2=None, op0=ALU.mult),
                      waits=[("act", kb.now("act"))], inc="dve")
                a1 = kb.op("scalar", lambda e: e.activation(out=rstd[:], in_=varall[:], func=AF.Ln, bias=1e-5),
                           waits=[("dve", kb.now("dve"))], inc="act")
                kb.op("scalar", lambda e: e.activation(out=rstd[:], in_=rstd[:], func=AF.Exp, scale=-0.5),
                      waits=[("act", a1)], inc="act")
                ln1_act = kb.now("act")
                for tt in range(NT):
                    kb.op("vector", lambda e, tt=tt: e.tensor_scalar(out=x1[:, tt, :], in0=x1[:, tt, :], scalar1=rstd[:, tt:tt + 1], scalar2=None, op0=ALU.mult),
                          waits=[("act", ln1_act)], inc="dve")
                x1_dve = kb.now("dve")

                _maybe_stop("proj")
                kb.fence()
                with contextlib.ExitStack() as gt:
                    x1T = sb(gt, "x1T", [128, 8, S], BF16)
                    x1loT = sb(gt, "x1loT", [128, 8, S], BF16)
                    x1b = sb(gt, "x1b", [128, 4, DM16], BF16)
                    x1lo = sb(gt, "x1lo", [128, 2, DM], BF16)
                    gsb = sb(gt, "gsb", [128, E], F32)
                    mx = sb(gt, "mx", [128, 1], F32)
                    nmx = sb(gt, "nmx", [128, 1], F32)
                    eq = sb(gt, "eq", [128, E], F32)
                    cnt2 = sb(gt, "cnt2", [128, 1], F32)
                    gwo = sb(gt, "gwo", [128, E], F32)
                    second = sb(gt, "second", [128, 1], F32)
                    c2 = sb(gt, "c2", [128, 1], F32)
                    c2i = sb(gt, "c2i", [128, 1], F32)
                    kthv = sb(gt, "kthv", [128, 1], F32)
                    ktha = sb(gt, "ktha", [128, 1], F32)
                    kthb = sb(gt, "kthb", [128, 1], F32)
                    selt = sb(gt, "selt", [128, E], F32)
                    emt = sb(gt, "emt", [128, E], F32)
                    z8 = sb(gt, "z8", [128, 1], F32)
                    rz8 = sb(gt, "rz8", [128, 1], F32)
                    mbf = sb(gt, "mbf", [128, E], BF16)
                    utri = sb(gt, "utri", [128, 128], F32)
                    eoff = sb(gt, "eoff", [128, E], F32)
                    tokf = sb(gt, "tokf", [128, 1], F32)
                    onescol = sb(gt, "onescol", [128, 1], F32)
                    selt2 = sb(gt, "selt2", [128, E], F32)
                    possb = sb(gt, "possb", [128, E], F32)
                    carrow = sb(gt, "carrow", [1, E], F32)
                    totrow = sb(gt, "totrow", [1, E], F32)
                    slf = sb(gt, "slf", [128, E], F32)
                    slint = sb(gt, "slint", [128, 2, 2], I32)
                    tokint = sb(gt, "tokint", [128, 2, 1], I32)
                    ecolB = sb(gt, "ecolB", [128, E], F32)
                    a1sb = sb(gt, "a1sb", [128, E], F32)
                    ohsb = sb(gt, "ohsb", [128, E], F32)
                    tsel = sb(gt, "tsel", [128, E], F32)
                    e1c = sb(gt, "e1c", [128, 1], F32)
                    slout = sb(gt, "slout", [128, 2], F32)
                    mtmp = sb(gt, "mtmp", [128, E], F32)
                    tokft = sb(gt, "tokft", [128, 1], F32)
                    idfill = sb(gt, "idfill", [128, E * CAP // 128], I32)
                    selt3 = sb(gt, "selt3", [128, E], F32)
                    zrowx = sb(gt, "zrowx", [1, DM16], BF16)

                    kb.op("gpsimd", lambda e: e.iota(utri[:], pattern=[[1, 128]], base=0, channel_multiplier=-1, allow_small_or_imprecise_dtypes=True), inc="gpc")
                    kb.op("gpsimd", lambda e: e.iota(eoff[:], pattern=[[CAP, E]], base=0, channel_multiplier=0, allow_small_or_imprecise_dtypes=True), inc="gpc")
                    kb.op("gpsimd", lambda e: e.iota(ecolB[:], pattern=[[1, E]], base=int(BIGF), channel_multiplier=0, allow_small_or_imprecise_dtypes=True), inc="gpc")
                    kb.op("gpsimd", lambda e: e.iota(tokf[:], pattern=[[1, 1]], base=0, channel_multiplier=1, allow_small_or_imprecise_dtypes=True), inc="gpc")
                    gidx_gpc = kb.now("gpc")
                    kb.op("vector", lambda e: e.tensor_scalar(out=utri[:], in0=utri[:], scalar1=0.5, scalar2=None, op0=ALU.is_ge),
                          waits=[("gpc", gidx_gpc)], inc="dve")
                    kb.op("vector", lambda e: e.memset(onescol[:], 1.0), inc="dve")
                    idf = kb.op("vector", lambda e: e.memset(idfill[:], S), inc="dve")
                    gcon_dve = kb.now("dve")
                    kb.op("sync", lambda e: e.dma_start(
                        out=AP(tensor=d_idx, offset=0, ap=[[E * CAP // 128, 128], [1, E * CAP // 128]]),
                        in_=idfill[:]),
                        waits=[("dve", idf)], inc="dIDF", inck=16)
                    zr2 = kb.op("vector", lambda e: e.memset(zrowx[:], 0.0), inc="dve")
                    kb.op("sync", lambda e: e.dma_start(out=d_x1b.ap()[S:S + 1, :], in_=zrowx[:]),
                          waits=[("dve", zr2)], inc="dX1B", inck=16)

                    xb_free = [0, 0, 0, 0]
                    x1b_st = [0, 0, 0, 0]
                    x1T_ev = 0
                    x1loT_ev = 0
                    for tt in range(NT):
                        ca = kb.op("scalar", lambda e, tt=tt: e.activation(out=x1b[:, tt % 4, 0:DM], in_=x1[:, tt, :], func=AF.Copy),
                                   waits=[("dve", x1_dve), ("pe", xb_free[tt % 4]), ("dX1B", x1b_st[tt % 4])], inc="act")
                        cd = kb.op("vector", lambda e, tt=tt: e.scalar_tensor_tensor(
                            out=x1lo[:, tt % 2], in0=x1b[:, tt % 4, 0:DM], scalar=-1.0, in1=x1[:, tt, :], op0=ALU.mult, op1=ALU.add),
                            waits=[("act", ca)], inc="dve")
                        for c in range(8):
                            kb.op("tensor", lambda e, tt=tt, c=c: e.transpose(P6[:, 0, c], x1b[:, tt % 4, c * 128:(c + 1) * 128], ident[:]),
                                  waits=[("act", ca), ("pe", setup_bar), ("dve", x1T_ev)],
                                  inc="pe" if c == 7 else None)
                        x1T_ev = kb.op("vector", lambda e, tt=tt: e.tensor_copy(
                            out=AP(tensor=x1T, offset=tt * 128, ap=[[8 * S, 128], [S, 8], [1, 128]]),
                            in_=P6[:, 0]),
                            waits=[("pe", kb.now("pe"))], inc="dve")
                        for c in range(8):
                            kb.op("tensor", lambda e, tt=tt, c=c: e.transpose(P6[:, 1, c], x1lo[:, tt % 2, c * 128:(c + 1) * 128], ident[:]),
                                  waits=[("dve", cd), ("dve", x1loT_ev)],
                                  inc="pe" if c == 7 else None)
                        x1loT_ev = kb.op("vector", lambda e, tt=tt: e.tensor_copy(
                            out=AP(tensor=x1loT, offset=tt * 128, ap=[[8 * S, 128], [S, 8], [1, 128]]),
                            in_=P6[:, 1]),
                            waits=[("pe", kb.now("pe"))], inc="dve")
                        xb_free[tt % 4] = kb.now("pe")
                        tp_dve = kb.now("dve")
                        for d in range(8):
                            kb.op("tensor", lambda e, tt=tt, d=d: e.matmul(
                                P3[:, 0:E], x1T[:, d, tt * 128:(tt + 1) * 128], wghi[:, d, :],
                                start=(d == 0), stop=False),
                                waits=[("dve", tp_dve)])
                            kb.op("tensor", lambda e, tt=tt, d=d: e.matmul(
                                P3[:, 0:E], x1T[:, d, tt * 128:(tt + 1) * 128], wglo[:, d, :],
                                start=False, stop=False))
                            kb.op("tensor", lambda e, tt=tt, d=d: e.matmul(
                                P3[:, 0:E], x1loT[:, d, tt * 128:(tt + 1) * 128], wghi[:, d, :],
                                start=False, stop=(d == 7)),
                                inc="pe" if d == 7 else None)
                        g_pe = kb.now("pe")
                        kb.op("vector", lambda e: e.tensor_copy(out=gsb[:], in_=P3[:, 0:E]), waits=[("pe", g_pe)], inc="dve")
                        kb.op("vector", lambda e: e.reduce_max(out=mx[:], in_=gsb[:], axis=AX.X), inc="dve")
                        kb.op("vector", lambda e: e.tensor_scalar(out=nmx[:], in0=mx[:], scalar1=-1.0, scalar2=None, op0=ALU.mult), inc="dve")
                        kb.op("vector", lambda e: e.tensor_scalar(out=eq[:], in0=gsb[:], scalar1=mx[:], scalar2=None, op0=ALU.is_equal), inc="dve")
                        kb.op("vector", lambda e: e.reduce_sum(out=cnt2[:], in_=eq[:], axis=AX.X), inc="dve")
                        kb.op("vector", lambda e: e.scalar_tensor_tensor(out=gwo[:], in0=eq[:], scalar=-1e9, in1=gsb[:], op0=ALU.mult, op1=ALU.add), inc="dve")
                        kb.op("vector", lambda e: e.reduce_max(out=second[:], in_=gwo[:], axis=AX.X), inc="dve")
                        kb.op("vector", lambda e: e.tensor_scalar(out=c2[:], in0=cnt2[:], scalar1=1.5, scalar2=None, op0=ALU.is_ge), inc="dve")
                        kb.op("vector", lambda e: e.tensor_scalar(out=c2i[:], in0=c2[:], scalar1=-1.0, scalar2=1.0, op0=ALU.mult, op1=ALU.add), inc="dve")
                        kb.op("vector", lambda e: e.tensor_mul(out=ktha[:], in0=c2[:], in1=mx[:]), inc="dve")
                        kb.op("vector", lambda e: e.tensor_mul(out=kthb[:], in0=c2i[:], in1=second[:]), inc="dve")
                        kb.op("vector", lambda e: e.tensor_add(out=kthv[:], in0=ktha[:], in1=kthb[:]), inc="dve")
                        kb.op("vector", lambda e: e.tensor_scalar(out=selt[:], in0=gsb[:], scalar1=kthv[:], scalar2=None, op0=ALU.is_ge), inc="dve")
                        ae = kb.op("scalar", lambda e: e.activation(out=emt[:], in_=gsb[:], func=AF.Exp, bias=nmx[:]),
                                   waits=[("dve", kb.now("dve"))], inc="act")
                        kb.op("vector", lambda e: e.tensor_mul(out=emt[:], in0=emt[:], in1=selt[:]), waits=[("act", ae)], inc="dve")
                        kb.op("vector", lambda e: e.reduce_sum(out=z8[:], in_=emt[:], axis=AX.X), inc="dve")
                        kb.op("vector", lambda e: e.reciprocal(out=rz8[:], in_=z8[:]), inc="dve")
                        m_dve = kb.op("vector", lambda e, tt=tt: e.tensor_scalar(out=m_all[:, tt, :], in0=emt[:], scalar1=rz8[:], scalar2=1.0 / E, op0=ALU.mult, op1=ALU.mult), inc="dve")
                        kb.op("vector", lambda e, tt=tt: e.tensor_copy(out=x1b[:, tt % 4, DM:DM + 8], in_=m_all[:, tt, :]), inc="dve")
                        kb.op("vector", lambda e, tt=tt: e.tensor_copy(out=mtmp[:], in_=x1b[:, tt % 4, DM:DM + 8]), inc="dve")
                        mlo_dve = kb.op("vector", lambda e, tt=tt: e.tensor_sub(out=x1b[:, tt % 4, DM + 8:DM + 16], in0=m_all[:, tt, :], in1=mtmp[:]), inc="dve")
                        x1b_st[tt % 4] = kb.op("sync", lambda e, tt=tt: e.dma_start(out=d_x1b.ap()[tt * 128:(tt + 1) * 128, :], in_=x1b[:, tt % 4]),
                              waits=[("act", ca), ("dve", mlo_dve), ("pe", kb.now("pe"))], inc="dX1B", inck=16)
                        kb.op("vector", lambda e, tt=tt: e.tensor_scalar(out=selt2[:], in0=m_all[:, tt, :], scalar1=0.0, scalar2=None, op0=ALU.is_gt),
                              waits=[("dve", gcon_dve)], inc="dve")
                        sel_dve = kb.now("dve")
                        if tt == 0:
                            cr = kb.op("vector", lambda e: e.memset(carrow[:], 0.0), inc="dve")
                            tot_ev_prev = 0
                        kb.op("tensor", lambda e: e.matmul(P3[:, 0:E], utri[:], selt2[:], start=True, stop=False),
                              waits=[("dve", sel_dve), ("pe", kb.now("pe"))])
                        pos_pe = kb.op("tensor", lambda e: e.matmul(P3[:, 0:E], ones1f[:], carrow[:], start=False, stop=True),
                                       waits=[("dve", cr)], inc="pe")
                        tot_pe = kb.op("tensor", lambda e: e.matmul(P4t[0:1, 512:512 + E], onescol[:], selt2[:],
                                                                    start=True, stop=True),
                                       waits=[("dve", max(cr, sel_dve, tot_ev_prev))], inc="pe")
                        tot_ev_prev = kb.op("vector", lambda e: e.tensor_copy(out=totrow[:], in_=P4t[0:1, 512:512 + E]),
                                            waits=[("pe", tot_pe)], inc="dve")
                        cr = kb.op("vector", lambda e: e.tensor_add(out=carrow[:], in0=carrow[:], in1=totrow[:]),
                                   waits=[("pe", pos_pe)], inc="dve")
                        pv2 = kb.op("vector", lambda e: e.tensor_copy(out=possb[:], in_=P3[:, 0:E]),
                                    waits=[("pe", pos_pe)], inc="dve")
                        kb.op("vector", lambda e: e.tensor_add(out=slf[:], in0=possb[:], in1=eoff[:]), inc="dve")
                        kb.op("vector", lambda e: e.tensor_scalar(out=slf[:], in0=slf[:], scalar1=BIGF, scalar2=None, op0=ALU.add), inc="dve")
                        kb.op("vector", lambda e: e.scalar_tensor_tensor(out=slf[:], in0=selt2[:], scalar=-BIGF, in1=slf[:], op0=ALU.mult, op1=ALU.add), inc="dve")
                        kb.op("vector", lambda e: e.tensor_scalar(out=selt3[:], in0=possb[:], scalar1=float(CAP) - 0.5, scalar2=BIGF, op0=ALU.is_ge, op1=ALU.mult), inc="dve")
                        kb.op("vector", lambda e: e.tensor_add(out=slf[:], in0=slf[:], in1=selt3[:]), inc="dve")
                        kb.op("vector", lambda e: e.scalar_tensor_tensor(out=a1sb[:], in0=selt2[:], scalar=-BIGF, in1=ecolB[:], op0=ALU.mult, op1=ALU.add), inc="dve")
                        kb.op("vector", lambda e: e.tensor_reduce(out=e1c[:], in_=a1sb[:], axis=AX.X, op=ALU.min), inc="dve")
                        kb.op("vector", lambda e: e.tensor_scalar(out=ohsb[:], in0=a1sb[:], scalar1=e1c[:], scalar2=None, op0=ALU.is_equal), inc="dve")
                        kb.op("vector", lambda e: e.tensor_mul(out=tsel[:], in0=slf[:], in1=ohsb[:]), inc="dve")
                        kb.op("vector", lambda e: e.reduce_sum(out=slout[:, 0:1], in_=tsel[:], axis=AX.X), inc="dve")
                        kb.op("vector", lambda e: e.scalar_tensor_tensor(out=a1sb[:], in0=ohsb[:], scalar=BIGF, in1=a1sb[:], op0=ALU.mult, op1=ALU.add), inc="dve")
                        kb.op("vector", lambda e: e.tensor_reduce(out=e1c[:], in_=a1sb[:], axis=AX.X, op=ALU.min), inc="dve")
                        kb.op("vector", lambda e: e.tensor_scalar(out=ohsb[:], in0=a1sb[:], scalar1=e1c[:], scalar2=None, op0=ALU.is_equal), inc="dve")
                        kb.op("vector", lambda e: e.tensor_mul(out=tsel[:], in0=slf[:], in1=ohsb[:]), inc="dve")
                        kb.op("vector", lambda e: e.reduce_sum(out=slout[:, 1:2], in_=tsel[:], axis=AX.X), inc="dve")
                        kb.op("vector", lambda e, tt=tt: e.tensor_scalar(out=slout[:, 0:1], in0=slout[:, 0:1], scalar1=float(E * CAP + tt * 2), scalar2=None, op0=ALU.min), inc="dve")
                        kb.op("vector", lambda e, tt=tt: e.tensor_scalar(out=slout[:, 1:2], in0=slout[:, 1:2], scalar1=float(E * CAP + tt * 2 + 1), scalar2=None, op0=ALU.min), inc="dve")
                        kb.op("vector", lambda e, tt=tt: e.tensor_copy(out=slint[:, tt % 2], in_=slout[:]),
                              waits=([("dIX%d" % (tt % 2), (tt // 2) * 2 * 16)] if tt >= 2 else []), inc="dve")
                        kb.op("vector", lambda e, tt=tt: e.tensor_scalar(out=tokft[:], in0=tokf[:], scalar1=float(tt * 128), scalar2=None, op0=ALU.add), inc="dve")
                        ti_dve = kb.op("vector", lambda e, tt=tt: e.tensor_copy(out=tokint[:, tt % 2], in_=tokft[:]), inc="dve")
                        for e_x in range(2):
                            kb.op("gpsimd", lambda e, e_x=e_x, tt=tt: e.indirect_dma_start(
                                out=d_idx.ap(),
                                out_offset=IndirectOffsetOnAxis(slint[:, tt % 2, e_x:e_x + 1], 0),
                                in_=tokint[:, tt % 2], in_offset=None),
                                waits=[("dve", ti_dve), ("dIDF", 16)], inc="dIX%d" % (tt % 2), inck=16, unordered=True)
                        kb.op("vector", lambda e, tt=tt: e.tensor_copy(out=mbf[:], in_=m_all[:, tt, :]), inc="dve")
                        mb_dve = kb.now("dve")
                        kb.op("tensor", lambda e, tt=tt: e.transpose(P6[0:8, 0, 0], mbf[:], ident[:]),
                              waits=[("dve", mb_dve)], inc="pe")
                        kb.op("vector", lambda e, tt=tt: e.tensor_copy(out=mT8[:, tt * 128:(tt + 1) * 128], in_=P6[0:8, 0, 0]),
                              waits=[("pe", kb.now("pe"))], inc="dve")
                        x1T_ev = kb.now("dve")
                    mask_dve = kb.now("dve")

                    b2ev = 0
                    for tt in range(NT):
                        for dd in range(2):
                            bm = kb.op("tensor", lambda e, tt=tt, dd=dd: e.matmul(
                                P4, mT8[:, tt * 128:(tt + 1) * 128], b2sb[:, dd * 512:(dd + 1) * 512],
                                start=True, stop=True),
                                waits=[("dve", max(mask_dve, b2ev)), ("dB2", b2_gld)], inc="pe")
                            b2ev = kb.op("vector", lambda e, tt=tt, dd=dd: e.tensor_add(
                                out=x1[:, tt, dd * 512:(dd + 1) * 512], in0=x1[:, tt, dd * 512:(dd + 1) * 512], in1=P4),
                                waits=[("pe", bm)], inc="dve")
                    b2_dve = kb.now("dve")

                    yzm = kb.op("vector", lambda e: e.memset(episb[:], 0.0), inc="dve")
                    w_init = kb.now("dve")
                    for tt in range(NT):
                        kb.op("sync", lambda e, tt=tt: e.dma_start(out=d_y.ap()[tt * 128:(tt + 1) * 128, :], in_=x1[:, tt, :]),
                              waits=[("dve", w_init)], inc="dYZ", inck=16, unordered=True)
                    kb.op("sync", lambda e: e.dma_start(out=d_y.ap()[S:S + 128, :], in_=episb[:]),
                          waits=[("dve", yzm)], inc="dYZ", inck=16, unordered=True)

                    for e_i2 in range(E):
                        for sbk in range(8):
                            kb.op("sync", lambda e, e_i2=e_i2, sbk=sbk: e.dma_start(
                                out=idxall[:, e_i2, sbk:sbk + 1],
                                in_=AP(tensor=d_idx, offset=e_i2 * CAP + sbk * 128, ap=[[1, 128], [1, 1]])),
                                waits=[("dIX0", (NT // 2) * 2 * 16), ("dIX1", (NT // 2) * 2 * 16)], inc="dIL", inck=16, unordered=True)

                _maybe_stop("gate")
                kb.fence()
                x1s.close()
                fx = es.enter_context(contextlib.ExitStack())
                hT = sb(fx, "hT", [128, 32, CAP], BF16)
                w1sb = sb(fx, "w1sb", [128, 2, 8, 256], BF16)
                w2sb = sb(fx, "w2sb", [128, 3, 8, 512], BF16)
                xg = sb(fx, "xg", [128, 8, DM16], BF16)
                xgT = sb(fx, "xgT", [128, 8, CAP], BF16)
                mgcol = sb(fx, "mgcol", [128, 2, 8], F32)
                idxsc = sb(fx, "idxsc", [128, 16], I32)
                idxeq = sb(fx, "idxeq", [128, 16], I32)
                ctrash = sb(fx, "ctrash", [128, 16], I32)
                ysc = sb(fx, "ysc", [128, 4, 512], F32)
                psY = [P1[:, 0:512], P1[:, 512:1024], P2[:, 0:512], P2[:, 512:1024],
                       P3[:, 0:512], P3[:, 512:1024], P4t[:, 0:512], P4t[:, 512:1024]]

                idx_done = (NT // 2) * 2 * 16
                x1b_done = kb.now("dX1B")
                yz_done = kb.now("dYZ")

                il_total = kb.now("dIL")

                w1_free = [0, 0]
                w2_free = [0, 0, 0]
                psh_ev = [("dve", 0), ("dve", 0)]
                y_ev = 0
                yev_bank = [0, 0]
                yev_first4 = 0
                mg_war = [0, 0]
                relu_alt = 0
                w1_last_pe = 0
                tp_done_prev = 0
                gx_total = [0] * E
                ys_hist = []
                ys_prev_expert = {}

                def issue_xgathers(e_i, war_pe, war_dve):
                    for sbk in range(8):
                        kb.op("gpsimd", lambda e, e_i=e_i, sbk=sbk: e.indirect_dma_start(
                            out=xg[:, sbk], out_offset=None,
                            in_=d_x1b.ap(), in_offset=IndirectOffsetOnAxis(idxall[:, e_i, sbk:sbk + 1], 0)),
                            waits=[("dIL", il_total), ("dX1B", x1b_done), ("pe", war_pe), ("dve", war_dve)],
                            inc="dGX%d" % (e_i % 2), inck=16, unordered=True)
                    return kb.now("dGX%d" % (e_i % 2))

                kb.op("gpsimd", lambda e: e.iota(ctrash[:], pattern=[[1, 16]], base=0, channel_multiplier=0), inc="gpc")
                ct_gpc = kb.now("gpc")
                gx_total[0] = issue_xgathers(0, 0, 0)
                for e_i in range(E):
                    ep = e_i % 2
                    kb.op("vector", lambda e, e_i=e_i: e.tensor_copy(out=idxsc[:, 0:8], in_=idxall[:, e_i, :]),
                          waits=[("gpc", ct_gpc), ("dIL", il_total)] + [(sn, cv) for sn, cv in ys_prev_expert.items()], inc="dve")
                    kb.op("vector", lambda e, e_i=e_i: e.tensor_copy(out=idxsc[:, 8:16], in_=idxall[:, e_i, :]), inc="dve")
                    kb.op("vector", lambda e, e_i=e_i: e.tensor_scalar(out=idxeq[:, 0:8], in0=idxall[:, e_i, :], scalar1=S, scalar2=None, op0=ALU.is_equal),
                          inc="dve")
                    kb.op("vector", lambda e: e.tensor_copy(out=idxeq[:, 8:16], in_=idxeq[:, 0:8]), inc="dve")
                    kb.op("vector", lambda e: e.tensor_mul(out=idxeq[:], in0=idxeq[:], in1=ctrash[:]), inc="dve")
                    idxsc_dve = kb.op("vector", lambda e: e.tensor_add(out=idxsc[:], in0=idxsc[:], in1=idxeq[:]), inc="dve")
                    xgt_evs = [0, 0]
                    for sbk in range(8):
                        for c in range(8):
                            kb.op("tensor", lambda e, sbk=sbk, c=c, e_i=e_i: e.transpose(P6[:, sbk % 2, c], xg[:, sbk, c * 128:(c + 1) * 128], ident[:]),
                                  waits=[("dGX%d" % (e_i % 2), gx_total[e_i]), ("dve", xgt_evs[sbk % 2]), ("dve", yev_bank[sbk % 2])],
                                  inc="pe" if c == 7 else None)
                        xgt_evs[sbk % 2] = kb.op("vector", lambda e, sbk=sbk: e.tensor_copy(
                            out=AP(tensor=xgT, offset=sbk * 128, ap=[[8 * CAP, 128], [CAP, 8], [1, 128]]),
                            in_=P6[:, sbk % 2]),
                            waits=[("pe", kb.now("pe")), ("pe", w1_last_pe)], inc="dve")
                    for sbk in range(8):
                        kb.op("vector", lambda e, e_i=e_i, sbk=sbk, ep=ep: e.tensor_tensor(
                            out=mgcol[:, ep, sbk:sbk + 1], in0=xg[:, sbk, DM + e_i:DM + e_i + 1],
                            in1=xg[:, sbk, DM + 8 + e_i:DM + 9 + e_i], op=ALU.add),
                            waits=[("dGX%d" % (e_i % 2), gx_total[e_i])], inc="dve")
                    xgt_dve = kb.now("dve")
                    tp_done = kb.now("pe")
                    if e_i + 1 < E:
                        gx_total[e_i + 1] = issue_xgathers(e_i + 1, tp_done, xgt_dve)
                    for jg in range(16):
                        w1_ld = kb.op("sync", lambda e, e_i=e_i, jg=jg: e.dma_start(
                            out=w1sb[:, jg % 2],
                            in_=AP(tensor=d_w1b, offset=e_i * DM * HF + jg * 256,
                                   ap=[[HF, 128], [128 * HF, 8], [1, 256]])),
                            waits=[("pe", w1_free[jg % 2]), ("dWCo", (PC_TOTAL // 2) * 16)],
                            inc="dW1", inck=16)
                        for js in range(2):
                            jc = jg * 2 + js
                            pp = jc % 2
                            psH = (P1, P2)[pp]
                            for th in range(2):
                                for d in range(8):
                                    kb.op("tensor", lambda e, jg=jg, js=js, d=d, th=th, psH=psH: e.matmul(
                                        psH[:, th * 512:(th + 1) * 512],
                                        w1sb[:, jg % 2, d, js * 128:(js + 1) * 128],
                                        xgT[:, d, th * 512:(th + 1) * 512],
                                        start=(d == 0), stop=(d == 7)),
                                        waits=[("dW1", w1_ld), ("dve", max(xgt_dve, yev_first4)), psh_ev[pp]],
                                        inc="pe" if (d == 7 and th == 1) else None)
                            h_pe = kb.now("pe")
                            if relu_alt == 0:
                                m = kb.op("scalar", lambda e, e_i=e_i, jc=jc, psH=psH: e.activation(
                                    out=hT[:, jc, :], in_=psH[:], func=AF.Relu, bias=b1sb[:, e_i, jc:jc + 1]),
                                    waits=[("pe", h_pe)], inc="act")
                                psh_ev[pp] = ("act", m)
                            else:
                                m = kb.op("vector", lambda e, e_i=e_i, jc=jc, psH=psH: e.tensor_scalar(
                                    out=hT[:, jc, :], in0=psH[:],
                                    scalar1=b1sb[:, e_i, jc:jc + 1], scalar2=0.0,
                                    op0=ALU.add, op1=ALU.max),
                                    waits=[("pe", h_pe)], inc="dve")
                                psh_ev[pp] = ("dve", m)
                            relu_alt ^= 1
                        w1_free[jg % 2] = kb.now("pe")
                    w1_last_pe = kb.now("pe")
                    h_act = kb.now("act")
                    h_dve = kb.now("dve")
                    def emit_w2_load(k2, e_i=e_i):
                        ddh, jcg = divmod(k2, 4)
                        return kb.op("scalar", lambda e, e_i=e_i, jcg=jcg, ddh=ddh, k2=k2: e.dma_start(
                            out=w2sb[:, k2 % 3],
                            in_=AP(tensor=d_w2b, offset=e_i * HF * DM + jcg * 8 * 128 * DM + ddh * 512,
                                   ap=[[DM, 128], [128 * DM, 8], [1, 512]])),
                            waits=[("pe", w2_free[k2 % 3]), ("dWCo", PC_TOTAL * 16)],
                            inc="dW2", inck=16)
                    w2_marks = {}
                    for k2 in range(2):
                        w2_marks[k2] = emit_w2_load(k2)
                    for ddh in range(2):
                        for jcg in range(4):
                            k2 = ddh * 4 + jcg
                            if k2 + 2 < 8:
                                w2_marks[k2 + 2] = emit_w2_load(k2 + 2)
                            w2_ld = w2_marks[k2]
                            for ji in range(8):
                                jc = jcg * 8 + ji
                                for ts in range(8):
                                    kb.op("tensor", lambda e, jc=jc, ji=ji, k2=k2, ts=ts: e.matmul(
                                        psY[ts], hT[:, jc, ts * 128:(ts + 1) * 128],
                                        w2sb[:, k2 % 3, ji, :],
                                        start=(jc == 0), stop=(jc == 31)),
                                        waits=[("dW2", w2_ld), ("act", h_act), ("dve", h_dve)] +
                                              ([("dve", y_ev)] if jc == 0 else []),
                                        inc="pe" if ts == 7 else None)
                            w2_free[k2 % 3] = kb.now("pe")
                        y_pe = kb.now("pe")
                        for ts in range(8):
                            k = len(ys_hist)
                            war = ys_hist[k - 4][1] if k >= 4 else 0
                            war_sem = ys_hist[k - 4][0] if k >= 4 else "dY0"
                            y_ev = kb.op("vector", lambda e, ts=ts, ep=ep: e.tensor_scalar(
                                out=ysc[:, ts % 4, :], in0=psY[ts],
                                scalar1=mgcol[:, ep, ts:ts + 1], scalar2=None, op0=ALU.mult),
                                waits=[("pe", y_pe), (war_sem, war)], inc="dve")
                            sname = "dY%d" % ts
                            jcol = ddh * 8 + ts
                            cnt = kb.op("gpsimd", lambda e, ts=ts, ddh=ddh, ep=ep, jcol=jcol: e.indirect_dma_start(
                                out=d_y.ap(), out_offset=IndirectOffsetOnAxis(idxsc[:, jcol:jcol + 1], 0),
                                in_=ysc[:, ts % 4, :], in_offset=None,
                                element_offset=ddh * 512,
                                compute_op=ALU.add),
                                waits=[("dve", y_ev), ("dYZ", yz_done)] +
                                      [(sn, cv) for sn, cv in ys_prev_expert.items()],
                                inc=sname, inck=16, unordered=True)
                            ys_hist.append((sname, cnt))
                            if ddh == 1 and ts == 1:
                                yev_bank = [None, None]
                                yev_bank[0] = y_ev - 1
                                yev_bank[1] = y_ev
                            if ddh == 1 and ts == 3:
                                yev_first4 = y_ev
                    mg_war[ep] = y_ev
                    tp_done_prev = tp_done
                    ys_prev_expert = {("dY%d" % t): kb.now("dY%d" % t) for t in range(8)}
                ffn_dve = kb.now("dve")

                fx.close()
                kb.fence()
                outsb = sb(pr, "outsb", [128, 2, DM], F32)
                yld = sb(pr, "yld", [128, 2, DM], F32)
                meanall = sb(pr, "meanall", [128, NT], F32)
                msq = sb(pr, "msq", [128, NT], F32)
                ys_totals = [("dY%d" % t, kb.now("dY%d" % t)) for t in range(8)]
                war_d = [0, 0]
                war_a = [0, 0]
                for tt in range(NT):
                    ly = kb.op("sync", lambda e, tt=tt: e.dma_start(out=yld[:, tt % 2], in_=d_y.ap()[tt * 128:(tt + 1) * 128, :]),
                               waits=ys_totals + [("dve", war_d[tt % 2]), ("act", war_a[tt % 2])], inc="dYL", inck=16)
                    war_d[tt % 2] = kb.op("vector", lambda e, tt=tt: e.reduce_sum(out=meanall[:, tt:tt + 1], in_=yld[:, tt % 2], axis=AX.X),
                                          waits=[("dYL", ly)], inc="dve")
                    war_a[tt % 2] = kb.op("scalar", lambda e, tt=tt: e.activation(out=sqt[:], in_=yld[:, tt % 2], func=AF.Square, accum_out=varall[:, tt:tt + 1]),
                          waits=[("dYL", ly), ("act", kb.now("act"))] + ([("dYZ", (NT + 1) * 16)] if tt == 0 else []), inc="act")
                kb.op("vector", lambda e: e.tensor_scalar(out=meanall[:], in0=meanall[:], scalar1=1.0 / DM, scalar2=None, op0=ALU.mult), inc="dve")
                kb.op("vector", lambda e: e.tensor_mul(out=msq[:], in0=meanall[:], in1=meanall[:]), inc="dve")
                kb.op("vector", lambda e: e.tensor_scalar(out=varall[:], in0=varall[:], scalar1=1.0 / DM, scalar2=None, op0=ALU.mult),
                      waits=[("act", kb.now("act"))], inc="dve")
                kb.op("vector", lambda e: e.tensor_sub(out=varall[:], in0=varall[:], in1=msq[:]), inc="dve")
                a2 = kb.op("scalar", lambda e: e.activation(out=rstd[:], in_=varall[:], func=AF.Ln, bias=1e-5),
                           waits=[("dve", kb.now("dve"))], inc="act")
                kb.op("scalar", lambda e: e.activation(out=rstd[:], in_=rstd[:], func=AF.Exp, scale=-0.5),
                      waits=[("act", a2)], inc="act")
                ln2_act = kb.now("act")
                for tt in range(NT):
                    ly = kb.op("sync", lambda e, tt=tt: e.dma_start(out=yld[:, tt % 2], in_=d_y.ap()[tt * 128:(tt + 1) * 128, :]),
                               waits=[("act", ln2_act), ("dve", war_d[tt % 2])], inc="dYL", inck=16)
                    od = kb.op("vector", lambda e, tt=tt: e.tensor_scalar(out=outsb[:, tt % 2], in0=yld[:, tt % 2],
                                                                          scalar1=meanall[:, tt:tt + 1], scalar2=rstd[:, tt:tt + 1],
                                                                          op0=ALU.subtract, op1=ALU.mult),
                               waits=[("dYL", ly), ("out", max(0, (tt - 1) * 16))], inc="dve")
                    war_d[tt % 2] = od
                    kb.op("sync", lambda e, tt=tt: e.dma_start(out=d_out.ap()[tt * 128:(tt + 1) * 128, :], in_=outsb[:, tt % 2]),
                          waits=[("dve", od)], inc="out", inck=16)
                kb.wait("sync", ("out", NT * 16))


        except _Stop:
            pass

        blk = es.enter_context(nc.Block())
        kb.emit(blk)

    return nc


_NC_CACHE = None


def kernel(L0_attn_x, L0_attn_qkv_w, L0_attn_out_w, L0_attn_bias_p, L0_attn_bias_a,
           L0_attn_rope_freqs, L0_ffn_W_gate, L0_ffn_W1, L0_ffn_b1, L0_ffn_W2, L0_ffn_b2):
    global _NC_CACHE
    if _NC_CACHE is None:
        _NC_CACHE = build_program()
    nc = _NC_CACHE
    weights = {
        "qkvw": np.ascontiguousarray(np.asarray(L0_attn_qkv_w, dtype=np.float32)),
        "outw": np.ascontiguousarray(np.asarray(L0_attn_out_w, dtype=np.float32)),
        "bp": np.ascontiguousarray(np.asarray(L0_attn_bias_p, dtype=np.float32).reshape(H)),
        "ba": np.ascontiguousarray(np.asarray(L0_attn_bias_a, dtype=np.float32).reshape(H)),
        "freqs": np.ascontiguousarray(np.asarray(L0_attn_rope_freqs, dtype=np.float32)),
        "wg": np.ascontiguousarray(np.asarray(L0_ffn_W_gate, dtype=np.float32)),
        "w1": np.ascontiguousarray(np.asarray(L0_ffn_W1, dtype=np.float32)),
        "b1": np.ascontiguousarray(np.asarray(L0_ffn_b1, dtype=np.float32)),
        "w2": np.ascontiguousarray(np.asarray(L0_ffn_W2, dtype=np.float32)),
        "b2": np.ascontiguousarray(np.asarray(L0_ffn_b2, dtype=np.float32)),
    }
    x = np.asarray(L0_attn_x, dtype=np.float32)
    in_maps = []
    for i in range(8):
        m = {"x": np.ascontiguousarray(x[i])}
        m.update(weights)
        in_maps.append(m)
    res = run_bass_kernel_spmd(nc, in_maps, core_ids=list(range(8)))
    return np.stack([np.asarray(r["out"], dtype=np.float32) for r in res.results], axis=0)


# version 11
# speedup vs baseline: 1.0267x; 1.0267x over previous
import contextlib
import numpy as np
import concourse.bass as bass
import concourse.mybir as mybir
from concourse.ap import AP
from concourse.bass import IndirectOffsetOnAxis
from concourse.bass_utils import run_bass_kernel_spmd

F32 = mybir.dt.float32
BF16 = mybir.dt.bfloat16
I32 = mybir.dt.int32
AF = mybir.ActivationFunctionType
ALU = mybir.AluOpType
AX = mybir.AxisListType

S = 2048
DM = 1024
H = 16
E = 8
HF = 4096
NT = S // 128
SCALE = 0.125
CAP = 1024
BIGF = 1.0e6
TWO_PI = float(2.0 * np.pi)
HALF_PI = float(0.5 * np.pi)

ENGINES = ("tensor", "vector", "scalar", "gpsimd", "sync")


class KB:

    def __init__(self, nc, es):
        self.nc = nc
        self.streams = {e: [] for e in ENGINES}
        self.sems = {}
        self.cnt = {}
        self.last_wait = {e: {} for e in ENGINES}
        self._es = es

    def sem(self, name):
        self.sems[name] = self._es.enter_context(self.nc.semaphore(name))
        self.cnt[name] = 0

    def now(self, sem):
        return self.cnt[sem]

    AUTO = {"vector": "dve", "scalar": "act", "gpsimd": "gpc"}

    def op(self, eng, build, waits=(), inc=None, inck=1, unordered=False):
        if getattr(self, "stopped", False):
            return self.cnt.get(inc, 0)
        waits = list(waits)
        auto = self.AUTO.get(eng)
        if build is not None and auto is not None:
            waits.append((auto, self.cnt[auto]))
            if inc is None:
                inc = auto
                inck = 1
        if inc is not None and inck == 16 and not unordered:
            waits.append((inc, self.cnt[inc]))
        wl = []
        for s_, c in waits:
            if c > 0 and c > self.last_wait[eng].get(s_, 0):
                wl.append((s_, c))
                self.last_wait[eng][s_] = c
        if inc is not None:
            self.cnt[inc] += inck
        self.streams[eng].append((wl, build, inc, inck))
        return self.cnt[inc] if inc is not None else None

    def wait(self, eng, *waits):
        self.op(eng, None, waits=waits)

    def fence(self, engs=ENGINES):
        snap = [(s_, c) for s_, c in self.cnt.items() if c > 0]
        for e_ in engs:
            self.op(e_, None, waits=snap)

    def emit(self, blk):
        for eng in ENGINES:
            stream = self.streams[eng]
            if not stream:
                continue

            def make(stream=stream):
                def f(e):
                    for wl, build, inc, inck in stream:
                        for s_, c in wl:
                            e.wait_ge(self.sems[s_], c)
                        ins = build(e) if build is not None else None
                        if inc is not None:
                            if ins is None:
                                ins = e.nop()
                            ins.then_inc(self.sems[inc], inck)
                return f

            getattr(blk, eng)(make())


class _Stop(Exception):
    pass


def build_program(upto=None):
    nc = bass.Bass("TRN2", target_bir_lowering=False)

    d_x = nc.dram_tensor("x", [S, DM], F32, kind="ExternalInput")
    d_qkvw = nc.dram_tensor("qkvw", [3 * DM, DM], F32, kind="ExternalInput")
    d_outw = nc.dram_tensor("outw", [DM, DM], F32, kind="ExternalInput")
    d_bp = nc.dram_tensor("bp", [H], F32, kind="ExternalInput")
    d_ba = nc.dram_tensor("ba", [H], F32, kind="ExternalInput")
    d_freqs = nc.dram_tensor("freqs", [32], F32, kind="ExternalInput")
    d_wg = nc.dram_tensor("wg", [DM, E], F32, kind="ExternalInput")
    d_w1 = nc.dram_tensor("w1", [E, DM, HF], F32, kind="ExternalInput")
    d_b1 = nc.dram_tensor("b1", [E, HF], F32, kind="ExternalInput")
    d_w2 = nc.dram_tensor("w2", [E, HF, DM], F32, kind="ExternalInput")
    d_b2 = nc.dram_tensor("b2", [E, DM], F32, kind="ExternalInput")
    d_out = nc.dram_tensor("out", [S, DM], F32, kind="ExternalOutput")
    d_qkT = nc.dram_tensor("qkT_s", [DM, 2 * DM], BF16, kind="Internal")
    d_aT = nc.dram_tensor("aT_s", [DM, S], BF16, kind="Internal")
    d_band = nc.dram_tensor("band_s", [H, 128, 4096], BF16, kind="Internal")
    d_w1b = nc.dram_tensor("w1b_s", [E * DM * HF], BF16, kind="Internal")
    d_w2b = nc.dram_tensor("w2b_s", [E * HF * DM], BF16, kind="Internal")
    DM16 = DM + 16
    d_x1b = nc.dram_tensor("x1b_s", [S + 1, DM16], BF16, kind="Internal")
    d_idx = nc.dram_tensor("idx_s", [E * CAP + 128, 1], I32, kind="Internal")
    d_y = nc.dram_tensor("y_s", [S + 128, DM], F32, kind="Internal")

    es = contextlib.ExitStack()
    with es:
        kb = KB(nc, es)
        for s_ in ("pe", "act", "dve", "gpc", "out",
                   "dS", "dB2", "dXB", "dQW", "dTS", "dOW", "dBS",
                   "dWSL", "dBBF", "dSW", "dZA", "dRZ", "dAO", "dAOT", "dXR",
                   "dW1", "dW2", "dWC", "dWCo", "dID",
                   "dX1B", "dIX0", "dIX1", "dIDF", "dGX0", "dGX1", "dYZ", "dIL", "dYL",
                   "dY0", "dY1", "dY2", "dY3", "dY4", "dY5", "dY6", "dY7"):
            kb.sem(s_)

        def sb(stack, name, shape, dtype):
            return stack.enter_context(nc.sbuf_tensor(name, shape, dtype))

        P1 = es.enter_context(nc.psum_tensor("P1", [128, 1024], F32))
        P2 = es.enter_context(nc.psum_tensor("P2", [128, 1024], F32))
        P3 = es.enter_context(nc.psum_tensor("P3", [128, 1024], F32))
        P4t = es.enter_context(nc.psum_tensor("P4t", [128, 1024], F32))
        P4 = P4t[:, 0:512]
        P6 = P1[:].bitcast(BF16).rearrange("p (a b c) -> p a b c", a=2, b=8)

        ident = sb(es, "ident", [128, 128], BF16)
        psw = sb(es, "psw", [128, 128], BF16)
        onebf = sb(es, "onebf", [1, 1], BF16)
        ones1f = sb(es, "ones1f", [1, 128], F32)
        ones16 = sb(es, "ones16", [128, NT], F32)
        CONSTS = [0.0, 1.0, -1.0, 1e-5, -0.5, 0.125, HALF_PI]
        csb = sb(es, "csb", [128, len(CONSTS)], F32)
        for ci, cv in enumerate(CONSTS):
            nc.const_aps.register(F32, cv, csb[:, ci:ci + 1])
        b1sb = sb(es, "b1sb", [128, E, 32], F32)
        b2sb = sb(es, "b2sb", [E, DM], BF16)
        wghi = sb(es, "wghi", [128, 8, E], BF16)
        wglo = sb(es, "wglo", [128, 8, E], BF16)
        varall = sb(es, "varall", [128, NT], F32)
        rstd = sb(es, "rstd", [128, NT], F32)
        meancol = sb(es, "meancol", [128, 1], F32)

        episb = sb(es, "episb", [128, DM], F32)
        sqt = episb

        def _maybe_stop(stage):
            if upto != stage:
                return
            kb.fence()
            gm = kb.op("gpsimd", lambda e: e.memset(episb[:], 0.0))
            for tt_ in range(NT):
                kb.op("sync", lambda e, tt_=tt_: e.dma_start(out=d_out.ap()[tt_ * 128:(tt_ + 1) * 128, :], in_=episb[:]),
                      waits=[("gpc", kb.now("gpc"))], inc="out", inck=16)
            kb.wait("sync", ("out", NT * 16))
            kb.stopped = True

        try:
            att_era = es.enter_context(contextlib.ExitStack())
            xT = sb(att_era, "xT", [128, 8, S], BF16)
            V = sb(att_era, "V", [128, NT, H * 65], BF16)
            C4 = sb(att_era, "C4", [128, S], F32)
            S4 = sb(att_era, "S4", [128, S], F32)
            with contextlib.ExitStack() as st1:
                posrow = sb(st1, "posrow", [1, S], F32)
                sgni = sb(st1, "sgni", [128, 1], I32)
                bandabs = sb(st1, "bandabs", [128, 4096], F32)
                qkvwb = sb(st1, "qkvwb", [128, 6, DM], BF16)
                qkvwTv = sb(st1, "qkvwTv", [128, 8, DM], BF16)
                tstage = sb(st1, "tstage", [128, 2, 8, 128], BF16)
                xb = sb(st1, "xb", [128, 2, DM], BF16)
                xbf = sb(st1, "xbf", [128, 2, DM], F32)
                wgf = sb(st1, "wgf", [128, 8, E], F32)
                bparow = sb(st1, "bparow", [1, 2, H], F32)
                paac = sb(st1, "paac", [128, H], F32)
                npac = sb(st1, "npac", [128, H], F32)

                kb.op("sync", lambda e: e.dma_start(out=bparow[:, 0], in_=AP(tensor=d_ba, offset=0, ap=[[1, 1], [1, H]])), inc="dS", inck=16)
                kb.op("sync", lambda e: e.dma_start(out=bparow[:, 1], in_=AP(tensor=d_bp, offset=0, ap=[[1, 1], [1, H]])), inc="dS", inck=16)
                for e_i in range(E):
                    for jc in range(32):
                        kb.op("sync", lambda e, e_i=e_i, jc=jc: e.dma_start(
                            out=b1sb[:, e_i, jc:jc + 1],
                            in_=AP(tensor=d_b1, offset=e_i * HF + jc * 128, ap=[[1, 128], [1, 1]])), inc="dS", inck=16, unordered=True)
                kb.op("sync", lambda e: e.dma_start(out=wgf[:], in_=AP(tensor=d_wg, offset=0, ap=[[E, 128], [128 * E, 8], [1, E]])), inc="dS", inck=16)
                sld_small = kb.now("dS")
                b2_gld = kb.op("gpsimd", lambda e: e.dma_start(out=b2sb[:], in_=d_b2.ap()), inc="dB2", inck=16)

                for ci, cv in enumerate(CONSTS):
                    kb.op("vector", lambda e, ci=ci, cv=cv: e.memset(csb[:, ci:ci + 1], cv))
                imz = kb.op("vector", lambda e: e.memset(ident[:], 0.0))
                ob = kb.op("vector", lambda e: e.memset(onebf[:], 1.0))
                ident_gpc = 0
                for ii in range(128):
                    kb.op("sync", lambda e, ii=ii: e.dma_start(
                        out=AP(tensor=ident, offset=ii * 128 + ii, ap=[[128, 1], [1, 1]]),
                        in_=onebf[:]),
                        waits=[("dve", ob)], inc="dID", inck=16, unordered=True)
                ident_dma = kb.now("dID")
                pm = kb.op("vector", lambda e: e.memset(psw[:], 0.0), inc="dve")
                for (ob, ib) in ((0, 32), (32, 0), (64, 96), (96, 64)):
                    kb.op("sync", lambda e, ob=ob, ib=ib: e.dma_start(
                        out=psw[ib:ib + 32, ob:ob + 32], in_=ident[ib:ib + 32, ib:ib + 32]),
                        waits=[("dID", ident_dma), ("dve", pm)], inc="dSW", inck=16, unordered=True)
                psw_ld = kb.now("dSW")
                kb.op("vector", lambda e: e.memset(ones1f[:], 1.0), inc="dve")
                v16 = kb.op("vector", lambda e: e.memset(ones16[:], 1.0), inc="dve")
                for vh in range(H):
                    kb.op("vector", lambda e, vh=vh: e.tensor_copy(
                        out=AP(tensor=V, offset=vh * 65 + 64, ap=[[NT * H * 65, 128], [H * 65, NT], [1, 1]]),
                        in_=ones16[:]),
                        waits=[("dve", v16)], inc="dve" if vh == H - 1 else None)
                gpc_const = kb.now("gpc")

                _maybe_stop("s1")
                _pc_state = {"i": 0}
                W1_ELEMS = E * DM * HF
                PC_TOTAL = 2 * W1_ELEMS // (128 * 4096)

                def precast_pump(n):
                    for _ in range(n):
                        i = _pc_state["i"]
                        if i >= PC_TOTAL:
                            return
                        _pc_state["i"] = i + 1
                        if i < PC_TOTAL // 2:
                            src_t, dst_t, off = d_w1, d_w1b, i * 128 * 4096
                        else:
                            src_t, dst_t, off = d_w2, d_w2b, (i - PC_TOTAL // 2) * 128 * 4096
                        kb.op("gpsimd", lambda e, src_t=src_t, dst_t=dst_t, off=off: e.dma_start(
                            out=AP(tensor=dst_t, offset=off, ap=[[4096, 128], [1, 4096]]),
                            in_=AP(tensor=src_t, offset=off, ap=[[4096, 128], [1, 4096]])),
                            inc="dWCo", inck=16)

                qkT_sld = [0, 0]
                qk_evs = [0, 0]
                for rnd in range(4):
                    gld_qkvw = kb.op("gpsimd", lambda e, rnd=rnd: e.dma_start(
                        out=qkvwb[:], in_=d_qkvw.ap()[rnd * 6 * 128:(rnd + 1) * 6 * 128, :].rearrange("(c p) d -> p c d", p=128)),
                        waits=[("pe", kb.now("pe") if rnd else 0)], inc="dQW", inck=16)
                    for cc12 in range(6):
                        cc = rnd * 6 + cc12
                        par = cc % 2
                        for dc in range(8):
                            kb.op("tensor", lambda e, cc12=cc12, dc=dc, par=par: e.transpose(P6[:, par, dc], qkvwb[:, cc12, dc * 128:(dc + 1) * 128], ident[:]),
                                  waits=[("dQW", gld_qkvw), ("dID", ident_dma), ("dve", qk_evs[par])],
                                  inc="pe" if dc == 7 else None)
                        if cc < 16:
                            qk_evs[par] = kb.op("vector", lambda e, par=par: e.tensor_copy(out=tstage[:, par], in_=P6[:, par]),
                                  waits=[("pe", kb.now("pe")), ("dTS", qkT_sld[par])], inc="dve")
                            qkT_sld[par] = kb.op("sync", lambda e, cc=cc, par=par: e.dma_start(
                                out=AP(tensor=d_qkT, offset=cc * 128, ap=[[2 * DM, 128], [128 * 2 * DM, 8], [1, 128]]),
                                in_=tstage[:, par]),
                                waits=[("dve", kb.now("dve"))], inc="dTS", inck=16)
                        else:
                            qk_evs[par] = kb.op("vector", lambda e, cc=cc, par=par: e.tensor_copy(
                                out=AP(tensor=qkvwTv, offset=(cc - 16) * 128, ap=[[8 * DM, 128], [DM, 8], [1, 128]]),
                                in_=P6[:, par]),
                                waits=[("pe", kb.now("pe"))], inc="dve")
                qkT_store_sld = kb.now("dTS")
                kb.op("gpsimd", lambda e: e.iota(posrow[:], pattern=[[1, S]], base=0, channel_multiplier=0, allow_small_or_imprecise_dtypes=True), inc="gpc")
                kb.op("gpsimd", lambda e: e.iota(sgni[:], pattern=[[1, 1]], base=0, channel_multiplier=1), inc="gpc")
                kb.op("gpsimd", lambda e: e.iota(bandabs[:], pattern=[[-1, 4096]], base=1920, channel_multiplier=1, allow_small_or_imprecise_dtypes=True), inc="gpc")
                iota_gpc = kb.now("gpc")
                precast_pump(4)
                _maybe_stop("s3")
                xT_evs = [qk_evs[0], qk_evs[1]]
                xc_dve = [0, 0]
                for tt in range(NT):
                    gld_x = kb.op("sync", lambda e, tt=tt: e.dma_start(out=xbf[:, tt % 2], in_=d_x.ap()[tt * 128:(tt + 1) * 128, :]),
                                  waits=[("dve", xc_dve[tt % 2])], inc="dXB", inck=16)
                    xc_dve[tt % 2] = kb.op("vector", lambda e, tt=tt: e.tensor_copy(out=xb[:, tt % 2], in_=xbf[:, tt % 2]),
                                           waits=[("dXB", gld_x), ("pe", kb.now("pe") if tt >= 2 else 0)], inc="dve")
                    for c in range(8):
                        kb.op("tensor", lambda e, tt=tt, c=c: e.transpose(P6[:, tt % 2, c], xb[:, tt % 2, c * 128:(c + 1) * 128], ident[:]),
                              waits=[("dve", xc_dve[tt % 2]), ("dID", ident_dma), ("dve", xT_evs[tt % 2])],
                              inc="pe" if c == 7 else None)
                    xT_evs[tt % 2] = kb.op("vector", lambda e, tt=tt: e.tensor_copy(
                        out=AP(tensor=xT, offset=tt * 128, ap=[[8 * S, 128], [S, 8], [1, 128]]),
                        in_=P6[:, tt % 2]),
                        waits=[("pe", kb.now("pe"))], inc="dve")
                xT_dve = kb.now("dve")
                _maybe_stop("s2")


                for tt in range(NT):
                    for hf in range(2):
                        for d in range(8):
                            kb.op("tensor", lambda e, tt=tt, hf=hf, d=d: e.matmul(
                                P4, xT[:, d, tt * 128:(tt + 1) * 128],
                                qkvwTv[:, d, hf * 512:(hf + 1) * 512],
                                start=(d == 0), stop=(d == 7)),
                                waits=[("dve", xT_dve if (tt == 0 and hf == 0) else kb.now("dve"))],
                                inc="pe" if d == 7 else None)
                        pv = kb.now("pe")
                        kb.op("vector", lambda e, tt=tt, hf=hf: e.tensor_copy(
                            out=AP(tensor=V, offset=tt * H * 65 + hf * 8 * 65, ap=[[NT * H * 65, 128], [65, 8], [1, 64]]),
                            in_=AP(tensor=P4t, offset=0, ap=[[1024, 128], [64, 8], [1, 64]])),
                            waits=[("pe", pv)], inc="dve")
                V_dve = kb.now("dve")
                precast_pump(8)
                _maybe_stop("s4")


                kb.op("vector", lambda e: e.tensor_copy(out=wghi[:], in_=wgf[:]), waits=[("dS", sld_small)], inc="dve")
                kb.op("vector", lambda e: e.scalar_tensor_tensor(out=wglo[:], in0=wghi[:], scalar=-1.0, in1=wgf[:], op0=ALU.mult, op1=ALU.add), inc="dve")

                with contextlib.ExitStack() as st2:
                    freq4 = sb(st2, "freq4", [128, 1], F32)
                    angs = sb(st2, "angs", [128, S], F32)
                    angred = sb(st2, "angred", [128, S], F32)
                    angi = sb(st2, "angi", [128, S], I32)
                    sgnf = sb(st2, "sgnf", [128, 1], F32)
                    sgncol = sb(st2, "sgncol", [128, 1], F32)
                    sgt = sb(st2, "sgt", [128, S], F32)

                    kb.op("sync", lambda e: e.dma_start(out=freq4[:], in_=AP(tensor=d_freqs, offset=0, ap=[[0, 4], [1, 32], [1, 1]])), inc="dS", inck=16)
                    sld_freq = kb.now("dS")
                    gpc_rope = iota_gpc
                    for qt in range(4):
                        kb.op("tensor", lambda e, qt=qt: e.matmul(P4, ones1f[:], posrow[:, qt * 512:(qt + 1) * 512], start=True, stop=True),
                              waits=[("gpc", gpc_rope), ("dve", kb.now("dve")), ("dS", sld_freq)], inc="pe")
                        kb.op("vector", lambda e, qt=qt: e.tensor_scalar(out=angs[:, qt * 512:(qt + 1) * 512], in0=P4, scalar1=freq4[:], scalar2=None, op0=ALU.mult),
                              waits=[("pe", kb.now("pe"))], inc="dve")
                    kb.op("vector", lambda e: e.tensor_scalar(out=angred[:], in0=angs[:], scalar1=1.0 / TWO_PI, scalar2=None, op0=ALU.mult), inc="dve")
                    kb.op("vector", lambda e: e.tensor_copy(out=angi[:], in_=angred[:]), inc="dve")
                    kb.op("vector", lambda e: e.tensor_copy(out=angred[:], in_=angi[:]), inc="dve")
                    kb.op("vector", lambda e: e.scalar_tensor_tensor(out=angred[:], in0=angred[:], scalar=-TWO_PI, in1=angs[:], op0=ALU.mult, op1=ALU.add), inc="dve")
                    kb.op("vector", lambda e: e.tensor_scalar(out=sgt[:], in0=angred[:], scalar1=float(np.pi), scalar2=-TWO_PI, op0=ALU.is_gt, op1=ALU.mult), inc="dve")
                    kb.op("vector", lambda e: e.tensor_add(out=angred[:], in0=angred[:], in1=sgt[:]), inc="dve")
                    kb.op("vector", lambda e: e.tensor_scalar(out=sgt[:], in0=angred[:], scalar1=float(-np.pi), scalar2=TWO_PI, op0=ALU.is_lt, op1=ALU.mult), inc="dve")
                    kb.op("vector", lambda e: e.tensor_add(out=angred[:], in0=angred[:], in1=sgt[:]), inc="dve")
                    dve_ang = kb.now("dve")
                    act_sin = kb.op("scalar", lambda e: e.activation(out=S4[:], in_=angred[:], func=AF.Sin), waits=[("dve", dve_ang)], inc="act")
                    aab = kb.op("scalar", lambda e: e.activation(out=angs[:], in_=angred[:], func=AF.Abs),
                                waits=[("act", act_sin), ("dve", dve_ang)], inc="act")
                    kb.op("scalar", lambda e: e.activation(out=C4[:], in_=angs[:], func=AF.Sin, bias=HALF_PI, scale=-1.0), waits=[("act", aab)], inc="act")
                    kb.op("vector", lambda e: e.tensor_scalar(out=sgni[:], in0=sgni[:], scalar1=32, scalar2=None, op0=ALU.bitwise_and), waits=[("gpc", gpc_rope)], inc="dve")
                    kb.op("vector", lambda e: e.tensor_copy(out=sgnf[:], in_=sgni[:]), inc="dve")
                    kb.op("vector", lambda e: e.tensor_scalar(out=sgncol[:], in0=sgnf[:], scalar1=1.0 / 16.0, scalar2=-1.0, op0=ALU.mult, op1=ALU.add), inc="dve")
                    kb.op("vector", lambda e: e.tensor_scalar(out=S4[:], in0=S4[:], scalar1=sgncol[:], scalar2=None, op0=ALU.mult),
                          waits=[("act", kb.now("act"))], inc="dve")

                    kb.op("tensor", lambda e: e.matmul(P3[:, 0:H], ones1f[:], bparow[:, 0], start=True, stop=True),
                          waits=[("dS", sld_small), ("dve", kb.now("dve"))], inc="pe")
                    kb.op("vector", lambda e: e.tensor_scalar(out=paac[:], in0=P3[:, 0:H], scalar1=0.01, scalar2=None, op0=ALU.max),
                          waits=[("pe", kb.now("pe"))], inc="dve")
                    kb.op("tensor", lambda e: e.matmul(P3[:, 0:H], ones1f[:], bparow[:, 1], start=True, stop=True),
                          waits=[("dve", kb.now("dve"))], inc="pe")
                    kb.op("vector", lambda e: e.tensor_scalar(out=npac[:], in0=P3[:, 0:H], scalar1=0.01, scalar2=-1.0, op0=ALU.max, op1=ALU.mult),
                          waits=[("pe", kb.now("pe"))], inc="dve")

                _maybe_stop("s5")
                kb.fence()
                with contextlib.ExitStack() as st3:
                    bandtmp = sb(st3, "bandtmp", [128, 2, 4096], BF16)
                    bandst = sb(st3, "bandst", [128, 2, 4096], BF16)

                    kb.op("scalar", lambda e: e.activation(out=bandabs[:], in_=bandabs[:], func=AF.Abs),
                          waits=[("gpc", iota_gpc)], inc="act")
                    babs_act = kb.now("act")
                    babs_dve = kb.now("dve")
                    bst_st = [0, 0]
                    bsc_mark = [0, 0]
                    for h in range(H):
                        a_ln = kb.op("scalar", lambda e, h=h: e.activation(out=bandtmp[:, h % 2], in_=bandabs[:], func=AF.Ln, bias=1.0, scale=paac[:, h:h + 1]),
                                     waits=[("act", babs_act), ("dve", max(babs_dve, bsc_mark[h % 2]))], inc="act")
                        d_sc = kb.op("vector", lambda e, h=h: e.tensor_scalar(out=bandst[:, h % 2], in0=bandtmp[:, h % 2], scalar1=npac[:, h:h + 1], scalar2=None, op0=ALU.mult),
                                     waits=[("act", a_ln), ("dBS", bst_st[h % 2])], inc="dve")
                        bsc_mark[h % 2] = d_sc
                        bst_st[h % 2] = kb.op("sync", lambda e, h=h: e.dma_start(out=d_band.ap()[h], in_=bandst[:, h % 2]),
                                              waits=[("dve", d_sc)], inc="dBS", inck=16)
                    band_sld = kb.now("dBS")
                setup_dve = kb.now("dve")
                setup_act = kb.now("act")
                kb.wait("tensor", ("dve", setup_dve), ("act", setup_act), ("dBS", kb.now("dBS")), ("dTS", kb.now("dTS")))
                setup_bar = kb.op("tensor", lambda e: e.nop(), inc="pe")
            _maybe_stop("setup")
            kb.fence()

            with contextlib.ExitStack() as at:
                wsl = sb(at, "wsl", [128, 4, 8, 256], BF16)
                qbf = sb(at, "qbf", [128, S], BF16)
                ctmp = sb(at, "ctmp", [128, S], F32)
                qr = sb(at, "qr", [128, S], BF16)
                kr = sb(at, "kr", [128, S], BF16)
                Pp = sb(at, "Pp", [128, 2, 1024], BF16)
                araw = sb(at, "araw", [64, 2, S], BF16)
                aout = sb(at, "aout", [64, S], BF16)
                bandbf = sb(at, "bandbf", [128, 2, 4096], BF16)
                zasm = sb(at, "zasm", [4, 2, 512], F32)
                rzasm = sb(at, "rzasm", [4, 512], F32)
                rzflat = sb(at, "rzflat", [1, 2, 4, 512], F32)
                ztmp = sb(at, "ztmp", [1, 2, 512], F32)

                psS = (P1, P2)
                psV_banks = (P3, P4t)

                exp_mark = [0, 0]
                kb_mark = [0, 0]
                zrec_mark = 0
                psv_z_p = [0, 0]
                psv_dve_p = [0, 0]
                araw_free_p = [0, 0]
                pending_tail = []
                nonlocal_state = {"zrec_mark": 0, "aout_dma": 0}
                ztail_pe_p = [0, 0]
                rz_ld_p = [0, 0]

                qkv_done = [0, 0, 0, 0]
                swp_war = [0, 0]
                qbf_war = [0, 0, 0, 0]
                for p in range(H // 2):
                    precast_pump(15)
                    war = qkv_done[p % 4]
                    w_ld = kb.op("sync", lambda e, p=p: e.dma_start(
                        out=wsl[:, p % 4, :, 0:128],
                        in_=AP(tensor=d_qkT, offset=p * 128, ap=[[2 * DM, 128], [128 * 2 * DM, 8], [1, 128]])),
                        waits=[("pe", war), ("dTS", qkT_store_sld)], inc="dWSL", inck=16)
                    w_ld = kb.op("sync", lambda e, p=p: e.dma_start(
                        out=wsl[:, p % 4, :, 128:256],
                        in_=AP(tensor=d_qkT, offset=DM + p * 128, ap=[[2 * DM, 128], [128 * 2 * DM, 8], [1, 128]])),
                        inc="dWSL", inck=16)
                    for hh in range(2):
                        h = 2 * p + hh
                        kb.op("sync", lambda e, h=h: e.dma_start(out=bandbf[:, h % 2], in_=d_band.ap()[h]),
                              waits=[("dBS", band_sld), ("pe", kb_mark[h % 2])], inc="dBBF", inck=16)
                    band_ld = kb.now("dBBF")

                    for w, dst, dsc in ((0, qr, SCALE), (1, kr, 1.0)):
                        evac_marks = {}
                        for qt in range(4):
                            g = (w * 4 + qt) % 2
                            psq = P1[:, 0:512] if g == 0 else P2[:, 0:512]
                            pssw = P1[:, 512:1024] if g == 0 else P2[:, 512:1024]
                            for d in range(8):
                                kb.op("tensor", lambda e, p=p, w=w, qt=qt, d=d, psq=psq: e.matmul(
                                    psq, wsl[:, p % 4, d, w * 128:(w + 1) * 128],
                                    xT[:, d, qt * 512:(qt + 1) * 512],
                                    start=(d == 0), stop=(d == 7)),
                                    waits=[("dWSL", w_ld), ("pe", setup_bar),
                                           ("act", max(exp_mark[0], exp_mark[1], evac_marks.get(g, 0)))],
                                    inc="pe" if d == 7 else None)
                            ev = kb.op("scalar", lambda e, qt=qt, dsc=dsc, psq=psq: e.activation(
                                out=qbf[:, qt * 512:(qt + 1) * 512], in_=psq, func=AF.Copy, scale=dsc),
                                waits=[("pe", kb.now("pe")), ("dve", qbf_war[qt])], inc="act")
                            evac_marks[g] = ev
                            sm = kb.op("tensor", lambda e, qt=qt, pssw=pssw: e.matmul(
                                pssw, psw[:], qbf[:, qt * 512:(qt + 1) * 512], start=True, stop=True),
                                waits=[("act", ev), ("dSW", psw_ld), ("dve", swp_war[g])], inc="pe")
                            qbf_war[qt] = kb.op("vector", lambda e, qt=qt: e.tensor_mul(
                                out=ctmp[:, qt * 512:(qt + 1) * 512], in0=qbf[:, qt * 512:(qt + 1) * 512],
                                in1=C4[:, qt * 512:(qt + 1) * 512]),
                                waits=[("act", ev)], inc="dve")
                            kb.op("vector", lambda e, qt=qt, dst=dst, pssw=pssw: e.tensor_mul(
                                out=dst[:, qt * 512:(qt + 1) * 512], in0=pssw,
                                in1=S4[:, qt * 512:(qt + 1) * 512]),
                                waits=[("pe", sm)], inc="dve")
                            swp_war[g] = kb.op("vector", lambda e, qt=qt, dst=dst: e.tensor_add(
                                out=dst[:, qt * 512:(qt + 1) * 512], in0=dst[:, qt * 512:(qt + 1) * 512],
                                in1=ctmp[:, qt * 512:(qt + 1) * 512]), inc="dve")
                    rope_dve = kb.now("dve")
                    qkv_done[p % 4] = kb.now("pe")

                    for hh in range(2):
                        h = 2 * p + hh
                        hb = hh * 64
                        bpar = h % 2
                        vpar = h % 2
                        PV = psV_banks[vpar]
                        for qh in range(2):
                            q0 = qh * 1024
                            evs = [0] * 16

                            def emit_attnV(kt, h=h, PV=PV, vpar=vpar):
                                for j in range(2):
                                    kb.op("tensor", lambda e, kt=kt, h=h, j=j, PV=PV: e.matmul(
                                        PV[0:65, j * 512:(j + 1) * 512], V[:, kt, h * 65:(h + 1) * 65],
                                        Pp[:, kt % 2, j * 512:(j + 1) * 512],
                                        start=(kt == 0), stop=(kt == 15)),
                                        waits=[("act", evs[kt])] +
                                              ([("dZA", psv_z_p[vpar]), ("dve", psv_dve_p[vpar])] if kt == 0 else []),
                                        inc="pe" if (j == 1 and kt == 15) else None)

                            for kt in range(16):
                                par = kt % 2
                                for j in range(2):
                                    kb.op("tensor", lambda e, hb=hb, kt=kt, q0=q0, j=j, par=par: e.matmul(
                                        psS[par][:, j * 512:(j + 1) * 512],
                                        kr[hb:hb + 64, kt * 128:(kt + 1) * 128],
                                        qr[hb:hb + 64, q0 + j * 512:q0 + (j + 1) * 512],
                                        start=True, stop=False),
                                        waits=[("dve", rope_dve), ("dBBF", band_ld),
                                               ("act", exp_mark[par])])
                                for j in range(2):
                                    u0 = q0 + j * 512 + 1920 - kt * 128
                                    r = kb.op("tensor", lambda e, par=par, bpar=bpar, u0=u0, j=j: e.matmul(
                                        psS[par][:, j * 512:(j + 1) * 512],
                                        ident[:], bandbf[:, bpar, u0:u0 + 512],
                                        start=False, stop=True),
                                        inc="pe" if j == 1 else None)
                                    if r is not None:
                                        kb_mark[bpar] = r
                                sdone = kb.now("pe")
                                if kt >= 1:
                                    emit_attnV(kt - 1)
                                if qh == 0 and kt == 8 and pending_tail:
                                    pending_tail.pop(0)()
                                exp_mark[par] = kb.op("scalar", lambda e, par=par: e.activation(
                                    out=Pp[:, par, :], in_=psS[par][:], func=AF.Exp),
                                    waits=[("pe", sdone)], inc="act")
                                evs[kt] = exp_mark[par]
                            emit_attnV(15)
                            pv_done = kb.now("pe")
                            for j in range(2):
                                kb.op("vector", lambda e, q0=q0, j=j, PV=PV, vpar=vpar: e.tensor_copy(
                                    out=araw[:, vpar, q0 + j * 512:q0 + (j + 1) * 512], in_=PV[0:64, j * 512:(j + 1) * 512]),
                                    waits=[("pe", pv_done), ("dve", araw_free_p[vpar])], inc="dve")
                                za = kb.op("vector", lambda e, j=j, PV=PV: e.tensor_copy(
                                    out=ztmp[:, j, :], in_=PV[64:65, j * 512:(j + 1) * 512]),
                                    waits=[("pe", pv_done), ("dZA", kb.now("dZA"))], inc="dve")
                                psv_z_p[vpar] = kb.op("sync", lambda e, qh=qh, j=j, vpar=vpar: e.dma_start(
                                    out=zasm[2 * qh + j:2 * qh + j + 1, vpar, :], in_=ztmp[:, j, :]),
                                    waits=[("dve", za)], inc="dZA", inck=16)
                        araw_act = kb.now("dve")
                        nonlocal_state["zrec_mark"] = kb.op("vector",
                            lambda e, vpar=vpar: e.reciprocal(out=rzasm[:], in_=zasm[:, vpar, :]),
                            waits=[("dZA", psv_z_p[vpar]), ("dRZ", kb.now("dRZ")), ("pe", ztail_pe_p[vpar])], inc="dve")
                        rz_ld_p[vpar] = kb.op("sync", lambda e, vpar=vpar: e.dma_start(
                            out=rzflat[0:1, vpar], in_=rzasm[:]),
                            waits=[("dve", nonlocal_state["zrec_mark"])], inc="dRZ", inck=16)

                        def make_tail(h=h, vpar=vpar, PV=PV, araw_act=araw_act, rz_ld=None):
                            rz_ld = rz_ld_p[vpar]

                            def tail():
                                pd_prev = araw_act
                                for qj in range(4):
                                    kb.op("tensor", lambda e, qj=qj, PV=PV, vpar=vpar: e.matmul(
                                        PV[0:64, 0:512], ones1f[:, 0:64], rzflat[0:1, vpar, qj, :], start=True, stop=True),
                                        waits=[("dRZ", rz_ld), ("dve", pd_prev)], inc="pe")
                                    pd = kb.op("vector", lambda e, qj=qj, PV=PV, vpar=vpar: e.tensor_tensor(
                                        out=aout[:, qj * 512:(qj + 1) * 512],
                                        in0=araw[:, vpar, qj * 512:(qj + 1) * 512], in1=PV[0:64, 0:512], op=ALU.mult),
                                        waits=[("pe", kb.now("pe")), ("dAO", nonlocal_state["aout_dma"])], inc="dve")
                                    psv_dve_p[vpar] = pd
                                    pd_prev = pd
                                    nonlocal_state["aout_dma"] = kb.op("sync", lambda e, h=h, qj=qj: e.dma_start(
                                        out=d_aT.ap()[h * 64:(h + 1) * 64, qj * 512:(qj + 1) * 512],
                                        in_=aout[:, qj * 512:(qj + 1) * 512]),
                                        waits=[("dve", pd)], inc="dAO", inck=16)
                                araw_free_p[vpar] = psv_dve_p[vpar]
                                ztail_pe_p[vpar] = kb.now("pe")
                            return tail

                        pending_tail.append(make_tail())
                while pending_tail:
                    pending_tail.pop(0)()
                precast_pump(10**9)
                att_sld = kb.now("dAO")
                kb.wait("tensor", ("dAO", att_sld), ("dZA", kb.now("dZA")), ("dRZ", kb.now("dRZ")), ("dSW", kb.now("dSW")), ("dve", kb.now("dve")), ("act", kb.now("act")))
                att_bar = kb.op("tensor", lambda e: e.nop(), inc="pe")
            att_era.close()
            _maybe_stop("attention")
            kb.fence()

            with contextlib.ExitStack() as pr:
                m_all = sb(pr, "m_all", [128, NT, E], F32)
                idxall = sb(pr, "idxall", [128, E, 8], I32)
                mT8 = sb(pr, "mT8", [8, S], BF16)
                x1s = es.enter_context(contextlib.ExitStack())
                x1 = sb(x1s, "x1", [128, NT, DM], F32)
                with contextlib.ExitStack() as pro:
                    outwb = sb(pro, "outwb", [128, 8, DM], BF16)
                    outwT = sb(pro, "outwT", [128, 8, DM], BF16)
                    aoT = sb(pro, "aoT", [128, 2, 8, 128], BF16)
                    xres = sb(pro, "xres", [128, 2, DM], F32)

                    gld_outw = kb.op("gpsimd", lambda e: e.dma_start(out=outwb[:], in_=d_outw.ap().rearrange("(c p) d -> p c d", p=128)),
                                     waits=[("pe", att_bar)], inc="dOW", inck=16)
                    ow_evs = [0, 0]
                    for cc in range(8):
                        for dc in range(8):
                            kb.op("tensor", lambda e, cc=cc, dc=dc: e.transpose(P6[:, cc % 2, dc], outwb[:, cc, dc * 128:(dc + 1) * 128], ident[:]),
                                  waits=[("dOW", gld_outw), ("dve", ow_evs[cc % 2])],
                                  inc="pe" if dc == 7 else None)
                        ow_evs[cc % 2] = kb.op("vector", lambda e, cc=cc: e.tensor_copy(
                            out=AP(tensor=outwT, offset=cc * 128, ap=[[8 * DM, 128], [DM, 8], [1, 128]]),
                            in_=P6[:, cc % 2]),
                            waits=[("pe", kb.now("pe"))], inc="dve")
                    owt_dve = kb.now("dve")

                    for tt in range(NT):
                        war = kb.now("pe") if tt >= 2 else 0
                        ld1 = kb.op("sync", lambda e, tt=tt: e.dma_start(
                            out=aoT[:, tt % 2],
                            in_=AP(tensor=d_aT, offset=tt * 128, ap=[[S, 128], [128 * S, 8], [1, 128]])),
                            waits=[("dAO", att_sld), ("pe", war)], inc="dAOT", inck=16)
                        war2 = kb.now("dve") if tt >= 2 else 0
                        ld2 = kb.op("sync", lambda e, tt=tt: e.dma_start(out=xres[:, tt % 2], in_=d_x.ap()[tt * 128:(tt + 1) * 128, :]),
                                    waits=[("dve", war2)], inc="dXR", inck=16)
                        for dd in range(2):
                            for c in range(8):
                                kb.op("tensor", lambda e, tt=tt, dd=dd, c=c: e.matmul(
                                    P1[:, dd * 512:(dd + 1) * 512], aoT[:, tt % 2, c, :],
                                    outwT[:, c, dd * 512:(dd + 1) * 512],
                                    start=(c == 0), stop=(c == 7)),
                                    waits=[("dAOT", ld1), ("dve", owt_dve), ("dve", kb.now("dve"))],
                                    inc="pe" if c == 7 else None)
                        mm_tt = kb.now("pe")
                        for dd in range(2):
                            kb.op("vector", lambda e, tt=tt, dd=dd: e.tensor_add(
                                out=x1[:, tt, dd * 512:(dd + 1) * 512], in0=xres[:, tt % 2, dd * 512:(dd + 1) * 512],
                                in1=P1[:, dd * 512:(dd + 1) * 512]),
                                waits=[("pe", mm_tt), ("dXR", ld2)], inc="dve")
                        kb.op("vector", lambda e, tt=tt: e.reduce_sum(out=meancol[:], in_=x1[:, tt, :], axis=AX.X), inc="dve")
                        kb.op("vector", lambda e: e.tensor_scalar(out=meancol[:], in0=meancol[:], scalar1=1.0 / DM, scalar2=None, op0=ALU.mult), inc="dve")
                        kb.op("vector", lambda e, tt=tt: e.tensor_scalar(out=x1[:, tt, :], in0=x1[:, tt, :], scalar1=meancol[:], scalar2=None, op0=ALU.subtract), inc="dve")
                        kb.op("scalar", lambda e, tt=tt: e.activation(out=sqt[:], in_=x1[:, tt, :], func=AF.Square, accum_out=varall[:, tt:tt + 1]),
                              waits=[("dve", kb.now("dve")), ("act", kb.now("act"))], inc="act")
                kb.op("vector", lambda e: e.tensor_scalar(out=varall[:], in0=varall[:], scalar1=1.0 / DM, scalar2=None, op0=ALU.mult),
                      waits=[("act", kb.now("act"))], inc="dve")
                a1 = kb.op("scalar", lambda e: e.activation(out=rstd[:], in_=varall[:], func=AF.Ln, bias=1e-5),
                           waits=[("dve", kb.now("dve"))], inc="act")
                kb.op("scalar", lambda e: e.activation(out=rstd[:], in_=rstd[:], func=AF.Exp, scale=-0.5),
                      waits=[("act", a1)], inc="act")
                ln1_act = kb.now("act")
                for tt in range(NT):
                    kb.op("vector", lambda e, tt=tt: e.tensor_scalar(out=x1[:, tt, :], in0=x1[:, tt, :], scalar1=rstd[:, tt:tt + 1], scalar2=None, op0=ALU.mult),
                          waits=[("act", ln1_act)], inc="dve")
                x1_dve = kb.now("dve")

                _maybe_stop("proj")
                kb.fence()
                with contextlib.ExitStack() as gt:
                    x1T = sb(gt, "x1T", [128, 8, S], BF16)
                    x1loT = sb(gt, "x1loT", [128, 8, S], BF16)
                    x1b = sb(gt, "x1b", [128, 4, DM16], BF16)
                    x1lo = sb(gt, "x1lo", [128, 2, DM], BF16)
                    gsb = sb(gt, "gsb", [128, E], F32)
                    mx = sb(gt, "mx", [128, 1], F32)
                    nmx = sb(gt, "nmx", [128, 1], F32)
                    eq = sb(gt, "eq", [128, E], F32)
                    cnt2 = sb(gt, "cnt2", [128, 1], F32)
                    gwo = sb(gt, "gwo", [128, E], F32)
                    second = sb(gt, "second", [128, 1], F32)
                    c2 = sb(gt, "c2", [128, 1], F32)
                    c2i = sb(gt, "c2i", [128, 1], F32)
                    kthv = sb(gt, "kthv", [128, 1], F32)
                    ktha = sb(gt, "ktha", [128, 1], F32)
                    kthb = sb(gt, "kthb", [128, 1], F32)
                    selt = sb(gt, "selt", [128, E], F32)
                    emt = sb(gt, "emt", [128, E], F32)
                    z8 = sb(gt, "z8", [128, 1], F32)
                    rz8 = sb(gt, "rz8", [128, 1], F32)
                    mbf = sb(gt, "mbf", [128, E], BF16)
                    utri = sb(gt, "utri", [128, 128], F32)
                    eoff = sb(gt, "eoff", [128, E], F32)
                    tokf = sb(gt, "tokf", [128, 1], F32)
                    onescol = sb(gt, "onescol", [128, 1], F32)
                    selt2 = sb(gt, "selt2", [128, E], F32)
                    possb = sb(gt, "possb", [128, E], F32)
                    carrow = sb(gt, "carrow", [1, E], F32)
                    totrow = sb(gt, "totrow", [1, E], F32)
                    slf = sb(gt, "slf", [128, E], F32)
                    slint = sb(gt, "slint", [128, 2, 2], I32)
                    tokint = sb(gt, "tokint", [128, 2, 1], I32)
                    ecolB = sb(gt, "ecolB", [128, E], F32)
                    a1sb = sb(gt, "a1sb", [128, E], F32)
                    ohsb = sb(gt, "ohsb", [128, E], F32)
                    tsel = sb(gt, "tsel", [128, E], F32)
                    e1c = sb(gt, "e1c", [128, 1], F32)
                    slout = sb(gt, "slout", [128, 2], F32)
                    mtmp = sb(gt, "mtmp", [128, E], F32)
                    tokft = sb(gt, "tokft", [128, 1], F32)
                    idfill = sb(gt, "idfill", [128, E * CAP // 128], I32)
                    selt3 = sb(gt, "selt3", [128, E], F32)
                    zrowx = sb(gt, "zrowx", [1, DM16], BF16)

                    kb.op("gpsimd", lambda e: e.iota(utri[:], pattern=[[1, 128]], base=0, channel_multiplier=-1, allow_small_or_imprecise_dtypes=True), inc="gpc")
                    kb.op("gpsimd", lambda e: e.iota(eoff[:], pattern=[[CAP, E]], base=0, channel_multiplier=0, allow_small_or_imprecise_dtypes=True), inc="gpc")
                    kb.op("gpsimd", lambda e: e.iota(ecolB[:], pattern=[[1, E]], base=int(BIGF), channel_multiplier=0, allow_small_or_imprecise_dtypes=True), inc="gpc")
                    kb.op("gpsimd", lambda e: e.iota(tokf[:], pattern=[[1, 1]], base=0, channel_multiplier=1, allow_small_or_imprecise_dtypes=True), inc="gpc")
                    gidx_gpc = kb.now("gpc")
                    kb.op("vector", lambda e: e.tensor_scalar(out=utri[:], in0=utri[:], scalar1=0.5, scalar2=None, op0=ALU.is_ge),
                          waits=[("gpc", gidx_gpc)], inc="dve")
                    kb.op("vector", lambda e: e.memset(onescol[:], 1.0), inc="dve")
                    idf = kb.op("vector", lambda e: e.memset(idfill[:], S), inc="dve")
                    gcon_dve = kb.now("dve")
                    kb.op("sync", lambda e: e.dma_start(
                        out=AP(tensor=d_idx, offset=0, ap=[[E * CAP // 128, 128], [1, E * CAP // 128]]),
                        in_=idfill[:]),
                        waits=[("dve", idf)], inc="dIDF", inck=16)
                    zr2 = kb.op("vector", lambda e: e.memset(zrowx[:], 0.0), inc="dve")
                    kb.op("sync", lambda e: e.dma_start(out=d_x1b.ap()[S:S + 1, :], in_=zrowx[:]),
                          waits=[("dve", zr2)], inc="dX1B", inck=16)

                    xb_free = [0, 0, 0, 0]
                    x1b_st = [0, 0, 0, 0]
                    x1T_ev = 0
                    x1loT_ev = 0
                    for tt in range(NT):
                        ca = kb.op("scalar", lambda e, tt=tt: e.activation(out=x1b[:, tt % 4, 0:DM], in_=x1[:, tt, :], func=AF.Copy),
                                   waits=[("dve", x1_dve), ("pe", xb_free[tt % 4]), ("dX1B", x1b_st[tt % 4])], inc="act")
                        cd = kb.op("vector", lambda e, tt=tt: e.scalar_tensor_tensor(
                            out=x1lo[:, tt % 2], in0=x1b[:, tt % 4, 0:DM], scalar=-1.0, in1=x1[:, tt, :], op0=ALU.mult, op1=ALU.add),
                            waits=[("act", ca)], inc="dve")
                        for c in range(8):
                            kb.op("tensor", lambda e, tt=tt, c=c: e.transpose(P6[:, 0, c], x1b[:, tt % 4, c * 128:(c + 1) * 128], ident[:]),
                                  waits=[("act", ca), ("pe", setup_bar), ("dve", x1T_ev)],
                                  inc="pe" if c == 7 else None)
                        x1T_ev = kb.op("vector", lambda e, tt=tt: e.tensor_copy(
                            out=AP(tensor=x1T, offset=tt * 128, ap=[[8 * S, 128], [S, 8], [1, 128]]),
                            in_=P6[:, 0]),
                            waits=[("pe", kb.now("pe"))], inc="dve")
                        for c in range(8):
                            kb.op("tensor", lambda e, tt=tt, c=c: e.transpose(P6[:, 1, c], x1lo[:, tt % 2, c * 128:(c + 1) * 128], ident[:]),
                                  waits=[("dve", cd), ("dve", x1loT_ev)],
                                  inc="pe" if c == 7 else None)
                        x1loT_ev = kb.op("vector", lambda e, tt=tt: e.tensor_copy(
                            out=AP(tensor=x1loT, offset=tt * 128, ap=[[8 * S, 128], [S, 8], [1, 128]]),
                            in_=P6[:, 1]),
                            waits=[("pe", kb.now("pe"))], inc="dve")
                        xb_free[tt % 4] = kb.now("pe")
                        tp_dve = kb.now("dve")
                        for d in range(8):
                            kb.op("tensor", lambda e, tt=tt, d=d: e.matmul(
                                P3[:, 0:E], x1T[:, d, tt * 128:(tt + 1) * 128], wghi[:, d, :],
                                start=(d == 0), stop=False),
                                waits=[("dve", tp_dve)])
                            kb.op("tensor", lambda e, tt=tt, d=d: e.matmul(
                                P3[:, 0:E], x1T[:, d, tt * 128:(tt + 1) * 128], wglo[:, d, :],
                                start=False, stop=False))
                            kb.op("tensor", lambda e, tt=tt, d=d: e.matmul(
                                P3[:, 0:E], x1loT[:, d, tt * 128:(tt + 1) * 128], wghi[:, d, :],
                                start=False, stop=(d == 7)),
                                inc="pe" if d == 7 else None)
                        g_pe = kb.now("pe")
                        kb.op("vector", lambda e: e.tensor_copy(out=gsb[:], in_=P3[:, 0:E]), waits=[("pe", g_pe)], inc="dve")
                        kb.op("vector", lambda e: e.reduce_max(out=mx[:], in_=gsb[:], axis=AX.X), inc="dve")
                        kb.op("vector", lambda e: e.tensor_scalar(out=nmx[:], in0=mx[:], scalar1=-1.0, scalar2=None, op0=ALU.mult), inc="dve")
                        kb.op("vector", lambda e: e.tensor_scalar(out=eq[:], in0=gsb[:], scalar1=mx[:], scalar2=None, op0=ALU.is_equal), inc="dve")
                        kb.op("vector", lambda e: e.reduce_sum(out=cnt2[:], in_=eq[:], axis=AX.X), inc="dve")
                        kb.op("vector", lambda e: e.scalar_tensor_tensor(out=gwo[:], in0=eq[:], scalar=-1e9, in1=gsb[:], op0=ALU.mult, op1=ALU.add), inc="dve")
                        kb.op("vector", lambda e: e.reduce_max(out=second[:], in_=gwo[:], axis=AX.X), inc="dve")
                        kb.op("vector", lambda e: e.tensor_scalar(out=c2[:], in0=cnt2[:], scalar1=1.5, scalar2=None, op0=ALU.is_ge), inc="dve")
                        kb.op("vector", lambda e: e.tensor_scalar(out=c2i[:], in0=c2[:], scalar1=-1.0, scalar2=1.0, op0=ALU.mult, op1=ALU.add), inc="dve")
                        kb.op("vector", lambda e: e.tensor_mul(out=ktha[:], in0=c2[:], in1=mx[:]), inc="dve")
                        kb.op("vector", lambda e: e.tensor_mul(out=kthb[:], in0=c2i[:], in1=second[:]), inc="dve")
                        kb.op("vector", lambda e: e.tensor_add(out=kthv[:], in0=ktha[:], in1=kthb[:]), inc="dve")
                        kb.op("vector", lambda e: e.tensor_scalar(out=selt[:], in0=gsb[:], scalar1=kthv[:], scalar2=None, op0=ALU.is_ge), inc="dve")
                        ae = kb.op("scalar", lambda e: e.activation(out=emt[:], in_=gsb[:], func=AF.Exp, bias=nmx[:]),
                                   waits=[("dve", kb.now("dve"))], inc="act")
                        kb.op("vector", lambda e: e.tensor_mul(out=emt[:], in0=emt[:], in1=selt[:]), waits=[("act", ae)], inc="dve")
                        kb.op("vector", lambda e: e.reduce_sum(out=z8[:], in_=emt[:], axis=AX.X), inc="dve")
                        kb.op("vector", lambda e: e.reciprocal(out=rz8[:], in_=z8[:]), inc="dve")
                        m_dve = kb.op("vector", lambda e, tt=tt: e.tensor_scalar(out=m_all[:, tt, :], in0=emt[:], scalar1=rz8[:], scalar2=1.0 / E, op0=ALU.mult, op1=ALU.mult), inc="dve")
                        kb.op("vector", lambda e, tt=tt: e.tensor_copy(out=x1b[:, tt % 4, DM:DM + 8], in_=m_all[:, tt, :]), inc="dve")
                        kb.op("vector", lambda e, tt=tt: e.tensor_copy(out=mtmp[:], in_=x1b[:, tt % 4, DM:DM + 8]), inc="dve")
                        mlo_dve = kb.op("vector", lambda e, tt=tt: e.tensor_sub(out=x1b[:, tt % 4, DM + 8:DM + 16], in0=m_all[:, tt, :], in1=mtmp[:]), inc="dve")
                        x1b_st[tt % 4] = kb.op("sync", lambda e, tt=tt: e.dma_start(out=d_x1b.ap()[tt * 128:(tt + 1) * 128, :], in_=x1b[:, tt % 4]),
                              waits=[("act", ca), ("dve", mlo_dve), ("pe", kb.now("pe"))], inc="dX1B", inck=16)
                        kb.op("vector", lambda e, tt=tt: e.tensor_scalar(out=selt2[:], in0=m_all[:, tt, :], scalar1=0.0, scalar2=None, op0=ALU.is_gt),
                              waits=[("dve", gcon_dve)], inc="dve")
                        sel_dve = kb.now("dve")
                        if tt == 0:
                            cr = kb.op("vector", lambda e: e.memset(carrow[:], 0.0), inc="dve")
                            tot_ev_prev = 0
                        kb.op("tensor", lambda e: e.matmul(P3[:, 0:E], utri[:], selt2[:], start=True, stop=False),
                              waits=[("dve", sel_dve), ("pe", kb.now("pe"))])
                        pos_pe = kb.op("tensor", lambda e: e.matmul(P3[:, 0:E], ones1f[:], carrow[:], start=False, stop=True),
                                       waits=[("dve", cr)], inc="pe")
                        tot_pe = kb.op("tensor", lambda e: e.matmul(P4t[0:1, 512:512 + E], onescol[:], selt2[:],
                                                                    start=True, stop=True),
                                       waits=[("dve", max(cr, sel_dve, tot_ev_prev))], inc="pe")
                        tot_ev_prev = kb.op("vector", lambda e: e.tensor_copy(out=totrow[:], in_=P4t[0:1, 512:512 + E]),
                                            waits=[("pe", tot_pe)], inc="dve")
                        cr = kb.op("vector", lambda e: e.tensor_add(out=carrow[:], in0=carrow[:], in1=totrow[:]),
                                   waits=[("pe", pos_pe)], inc="dve")
                        pv2 = kb.op("vector", lambda e: e.tensor_copy(out=possb[:], in_=P3[:, 0:E]),
                                    waits=[("pe", pos_pe)], inc="dve")
                        kb.op("vector", lambda e: e.tensor_add(out=slf[:], in0=possb[:], in1=eoff[:]), inc="dve")
                        kb.op("vector", lambda e: e.tensor_scalar(out=slf[:], in0=slf[:], scalar1=BIGF, scalar2=None, op0=ALU.add), inc="dve")
                        kb.op("vector", lambda e: e.scalar_tensor_tensor(out=slf[:], in0=selt2[:], scalar=-BIGF, in1=slf[:], op0=ALU.mult, op1=ALU.add), inc="dve")
                        kb.op("vector", lambda e: e.tensor_scalar(out=selt3[:], in0=possb[:], scalar1=float(CAP) - 0.5, scalar2=BIGF, op0=ALU.is_ge, op1=ALU.mult), inc="dve")
                        kb.op("vector", lambda e: e.tensor_add(out=slf[:], in0=slf[:], in1=selt3[:]), inc="dve")
                        kb.op("vector", lambda e: e.scalar_tensor_tensor(out=a1sb[:], in0=selt2[:], scalar=-BIGF, in1=ecolB[:], op0=ALU.mult, op1=ALU.add), inc="dve")
                        kb.op("vector", lambda e: e.tensor_reduce(out=e1c[:], in_=a1sb[:], axis=AX.X, op=ALU.min), inc="dve")
                        kb.op("vector", lambda e: e.tensor_scalar(out=ohsb[:], in0=a1sb[:], scalar1=e1c[:], scalar2=None, op0=ALU.is_equal), inc="dve")
                        kb.op("vector", lambda e: e.tensor_mul(out=tsel[:], in0=slf[:], in1=ohsb[:]), inc="dve")
                        kb.op("vector", lambda e: e.reduce_sum(out=slout[:, 0:1], in_=tsel[:], axis=AX.X), inc="dve")
                        kb.op("vector", lambda e: e.scalar_tensor_tensor(out=a1sb[:], in0=ohsb[:], scalar=BIGF, in1=a1sb[:], op0=ALU.mult, op1=ALU.add), inc="dve")
                        kb.op("vector", lambda e: e.tensor_reduce(out=e1c[:], in_=a1sb[:], axis=AX.X, op=ALU.min), inc="dve")
                        kb.op("vector", lambda e: e.tensor_scalar(out=ohsb[:], in0=a1sb[:], scalar1=e1c[:], scalar2=None, op0=ALU.is_equal), inc="dve")
                        kb.op("vector", lambda e: e.tensor_mul(out=tsel[:], in0=slf[:], in1=ohsb[:]), inc="dve")
                        kb.op("vector", lambda e: e.reduce_sum(out=slout[:, 1:2], in_=tsel[:], axis=AX.X), inc="dve")
                        kb.op("vector", lambda e, tt=tt: e.tensor_scalar(out=slout[:, 0:1], in0=slout[:, 0:1], scalar1=float(E * CAP + tt * 2), scalar2=None, op0=ALU.min), inc="dve")
                        kb.op("vector", lambda e, tt=tt: e.tensor_scalar(out=slout[:, 1:2], in0=slout[:, 1:2], scalar1=float(E * CAP + tt * 2 + 1), scalar2=None, op0=ALU.min), inc="dve")
                        kb.op("vector", lambda e, tt=tt: e.tensor_copy(out=slint[:, tt % 2], in_=slout[:]),
                              waits=([("dIX%d" % (tt % 2), (tt // 2) * 2 * 16)] if tt >= 2 else []), inc="dve")
                        kb.op("vector", lambda e, tt=tt: e.tensor_scalar(out=tokft[:], in0=tokf[:], scalar1=float(tt * 128), scalar2=None, op0=ALU.add), inc="dve")
                        ti_dve = kb.op("vector", lambda e, tt=tt: e.tensor_copy(out=tokint[:, tt % 2], in_=tokft[:]), inc="dve")
                        for e_x in range(2):
                            kb.op("gpsimd", lambda e, e_x=e_x, tt=tt: e.indirect_dma_start(
                                out=d_idx.ap(),
                                out_offset=IndirectOffsetOnAxis(slint[:, tt % 2, e_x:e_x + 1], 0),
                                in_=tokint[:, tt % 2], in_offset=None),
                                waits=[("dve", ti_dve), ("dIDF", 16)], inc="dIX%d" % (tt % 2), inck=16, unordered=True)
                        kb.op("vector", lambda e, tt=tt: e.tensor_copy(out=mbf[:], in_=m_all[:, tt, :]), inc="dve")
                        mb_dve = kb.now("dve")
                        kb.op("tensor", lambda e, tt=tt: e.transpose(P6[0:8, 0, 0], mbf[:], ident[:]),
                              waits=[("dve", mb_dve)], inc="pe")
                        kb.op("vector", lambda e, tt=tt: e.tensor_copy(out=mT8[:, tt * 128:(tt + 1) * 128], in_=P6[0:8, 0, 0]),
                              waits=[("pe", kb.now("pe"))], inc="dve")
                        x1T_ev = kb.now("dve")
                    mask_dve = kb.now("dve")

                    b2ev = 0
                    for tt in range(NT):
                        for dd in range(2):
                            bm = kb.op("tensor", lambda e, tt=tt, dd=dd: e.matmul(
                                P4, mT8[:, tt * 128:(tt + 1) * 128], b2sb[:, dd * 512:(dd + 1) * 512],
                                start=True, stop=True),
                                waits=[("dve", max(mask_dve, b2ev)), ("dB2", b2_gld)], inc="pe")
                            b2ev = kb.op("vector", lambda e, tt=tt, dd=dd: e.tensor_add(
                                out=x1[:, tt, dd * 512:(dd + 1) * 512], in0=x1[:, tt, dd * 512:(dd + 1) * 512], in1=P4),
                                waits=[("pe", bm)], inc="dve")
                    b2_dve = kb.now("dve")

                    yzm = kb.op("vector", lambda e: e.memset(episb[:], 0.0), inc="dve")
                    w_init = kb.now("dve")
                    for tt in range(NT):
                        kb.op("sync", lambda e, tt=tt: e.dma_start(out=d_y.ap()[tt * 128:(tt + 1) * 128, :], in_=x1[:, tt, :]),
                              waits=[("dve", w_init)], inc="dYZ", inck=16, unordered=True)
                    kb.op("sync", lambda e: e.dma_start(out=d_y.ap()[S:S + 128, :], in_=episb[:]),
                          waits=[("dve", yzm)], inc="dYZ", inck=16, unordered=True)

                    for e_i2 in range(E):
                        for sbk in range(8):
                            kb.op("sync", lambda e, e_i2=e_i2, sbk=sbk: e.dma_start(
                                out=idxall[:, e_i2, sbk:sbk + 1],
                                in_=AP(tensor=d_idx, offset=e_i2 * CAP + sbk * 128, ap=[[1, 128], [1, 1]])),
                                waits=[("dIX0", (NT // 2) * 2 * 16), ("dIX1", (NT // 2) * 2 * 16)], inc="dIL", inck=16, unordered=True)

                _maybe_stop("gate")
                kb.fence()
                x1s.close()
                fx = es.enter_context(contextlib.ExitStack())
                hT = sb(fx, "hT", [128, 32, CAP], BF16)
                w1sb = sb(fx, "w1sb", [128, 2, 8, 256], BF16)
                w2sb = sb(fx, "w2sb", [128, 3, 8, 512], BF16)
                xg = sb(fx, "xg", [128, 8, DM16], BF16)
                xgT = sb(fx, "xgT", [128, 8, CAP], BF16)
                mgcol = sb(fx, "mgcol", [128, 2, 8], F32)
                idxsc = sb(fx, "idxsc", [128, 16], I32)
                idxeq = sb(fx, "idxeq", [128, 16], I32)
                ctrash = sb(fx, "ctrash", [128, 16], I32)
                ysc = sb(fx, "ysc", [128, 4, 512], F32)
                psY = [P1[:, 0:512], P1[:, 512:1024], P2[:, 0:512], P2[:, 512:1024],
                       P3[:, 0:512], P3[:, 512:1024], P4t[:, 0:512], P4t[:, 512:1024]]

                idx_done = (NT // 2) * 2 * 16
                x1b_done = kb.now("dX1B")
                yz_done = kb.now("dYZ")

                il_total = kb.now("dIL")

                w1_free = [0, 0]
                w2_free = [0, 0, 0]
                psh_ev = [("dve", 0), ("dve", 0)]
                y_ev = 0
                yev_bank = [0, 0]
                yev_first4 = 0
                mg_war = [0, 0]
                relu_alt = 0
                w1_last_pe = 0
                tp_done_prev = 0
                gx_total = [0] * E
                ys_hist = []
                ys_prev_expert = {}

                def issue_xgathers(e_i, war_pe, war_dve):
                    for sbk in range(8):
                        kb.op("gpsimd", lambda e, e_i=e_i, sbk=sbk: e.indirect_dma_start(
                            out=xg[:, sbk], out_offset=None,
                            in_=d_x1b.ap(), in_offset=IndirectOffsetOnAxis(idxall[:, e_i, sbk:sbk + 1], 0)),
                            waits=[("dIL", il_total), ("dX1B", x1b_done), ("pe", war_pe), ("dve", war_dve)],
                            inc="dGX%d" % (e_i % 2), inck=16, unordered=True)
                    return kb.now("dGX%d" % (e_i % 2))

                kb.op("gpsimd", lambda e: e.iota(ctrash[:], pattern=[[1, 16]], base=0, channel_multiplier=0), inc="gpc")
                ct_gpc = kb.now("gpc")
                gx_total[0] = issue_xgathers(0, 0, 0)
                for e_i in range(E):
                    ep = e_i % 2
                    kb.op("vector", lambda e, e_i=e_i: e.tensor_copy(out=idxsc[:, 0:8], in_=idxall[:, e_i, :]),
                          waits=[("gpc", ct_gpc), ("dIL", il_total)] + [(sn, cv) for sn, cv in ys_prev_expert.items()], inc="dve")
                    kb.op("vector", lambda e, e_i=e_i: e.tensor_copy(out=idxsc[:, 8:16], in_=idxall[:, e_i, :]), inc="dve")
                    kb.op("vector", lambda e, e_i=e_i: e.tensor_scalar(out=idxeq[:, 0:8], in0=idxall[:, e_i, :], scalar1=S, scalar2=None, op0=ALU.is_equal),
                          inc="dve")
                    kb.op("vector", lambda e: e.tensor_copy(out=idxeq[:, 8:16], in_=idxeq[:, 0:8]), inc="dve")
                    kb.op("vector", lambda e: e.tensor_mul(out=idxeq[:], in0=idxeq[:], in1=ctrash[:]), inc="dve")
                    idxsc_dve = kb.op("vector", lambda e: e.tensor_add(out=idxsc[:], in0=idxsc[:], in1=idxeq[:]), inc="dve")
                    xgt_evs = [0, 0]
                    for sbk in range(8):
                        for c in range(8):
                            kb.op("tensor", lambda e, sbk=sbk, c=c, e_i=e_i: e.transpose(P6[:, sbk % 2, c], xg[:, sbk, c * 128:(c + 1) * 128], ident[:]),
                                  waits=[("dGX%d" % (e_i % 2), gx_total[e_i]), ("dve", xgt_evs[sbk % 2]), ("dve", yev_bank[sbk % 2])],
                                  inc="pe" if c == 7 else None)
                        xgt_evs[sbk % 2] = kb.op("vector", lambda e, sbk=sbk: e.tensor_copy(
                            out=AP(tensor=xgT, offset=sbk * 128, ap=[[8 * CAP, 128], [CAP, 8], [1, 128]]),
                            in_=P6[:, sbk % 2]),
                            waits=[("pe", kb.now("pe")), ("pe", w1_last_pe)], inc="dve")
                    for sbk in range(8):
                        kb.op("vector", lambda e, e_i=e_i, sbk=sbk, ep=ep: e.tensor_tensor(
                            out=mgcol[:, ep, sbk:sbk + 1], in0=xg[:, sbk, DM + e_i:DM + e_i + 1],
                            in1=xg[:, sbk, DM + 8 + e_i:DM + 9 + e_i], op=ALU.add),
                            waits=[("dGX%d" % (e_i % 2), gx_total[e_i])], inc="dve")
                    xgt_dve = kb.now("dve")
                    tp_done = kb.now("pe")
                    if e_i + 1 < E:
                        gx_total[e_i + 1] = issue_xgathers(e_i + 1, tp_done, xgt_dve)
                    for jg in range(16):
                        w1_ld = kb.op("sync", lambda e, e_i=e_i, jg=jg: e.dma_start(
                            out=w1sb[:, jg % 2],
                            in_=AP(tensor=d_w1b, offset=e_i * DM * HF + jg * 256,
                                   ap=[[HF, 128], [128 * HF, 8], [1, 256]])),
                            waits=[("pe", w1_free[jg % 2]), ("dWCo", (PC_TOTAL // 2) * 16)],
                            inc="dW1", inck=16)
                        for js in range(2):
                            jc = jg * 2 + js
                            pp = jc % 2
                            psH = (P1, P2)[pp]
                            for th in range(2):
                                for d in range(8):
                                    kb.op("tensor", lambda e, jg=jg, js=js, d=d, th=th, psH=psH: e.matmul(
                                        psH[:, th * 512:(th + 1) * 512],
                                        w1sb[:, jg % 2, d, js * 128:(js + 1) * 128],
                                        xgT[:, d, th * 512:(th + 1) * 512],
                                        start=(d == 0), stop=(d == 7)),
                                        waits=[("dW1", w1_ld), ("dve", max(xgt_dve, yev_first4)), psh_ev[pp]],
                                        inc="pe" if (d == 7 and th == 1) else None)
                            h_pe = kb.now("pe")
                            if relu_alt == 0:
                                m = kb.op("scalar", lambda e, e_i=e_i, jc=jc, psH=psH: e.activation(
                                    out=hT[:, jc, :], in_=psH[:], func=AF.Relu, bias=b1sb[:, e_i, jc:jc + 1]),
                                    waits=[("pe", h_pe)], inc="act")
                                psh_ev[pp] = ("act", m)
                            else:
                                m = kb.op("vector", lambda e, e_i=e_i, jc=jc, psH=psH: e.tensor_scalar(
                                    out=hT[:, jc, :], in0=psH[:],
                                    scalar1=b1sb[:, e_i, jc:jc + 1], scalar2=0.0,
                                    op0=ALU.add, op1=ALU.max),
                                    waits=[("pe", h_pe)], inc="dve")
                                psh_ev[pp] = ("dve", m)
                            relu_alt ^= 1
                        w1_free[jg % 2] = kb.now("pe")
                    w1_last_pe = kb.now("pe")
                    h_act = kb.now("act")
                    h_dve = kb.now("dve")
                    def emit_w2_load(k2, e_i=e_i):
                        ddh, jcg = divmod(k2, 4)
                        return kb.op("scalar", lambda e, e_i=e_i, jcg=jcg, ddh=ddh, k2=k2: e.dma_start(
                            out=w2sb[:, k2 % 3],
                            in_=AP(tensor=d_w2b, offset=e_i * HF * DM + jcg * 8 * 128 * DM + ddh * 512,
                                   ap=[[DM, 128], [128 * DM, 8], [1, 512]])),
                            waits=[("pe", w2_free[k2 % 3]), ("dWCo", PC_TOTAL * 16)],
                            inc="dW2", inck=16)
                    w2_marks = {}
                    for k2 in range(2):
                        w2_marks[k2] = emit_w2_load(k2)
                    for ddh in range(2):
                        for jcg in range(4):
                            k2 = ddh * 4 + jcg
                            if k2 + 2 < 8:
                                w2_marks[k2 + 2] = emit_w2_load(k2 + 2)
                            w2_ld = w2_marks[k2]
                            for ji in range(8):
                                jc = jcg * 8 + ji
                                for ts in range(8):
                                    kb.op("tensor", lambda e, jc=jc, ji=ji, k2=k2, ts=ts: e.matmul(
                                        psY[ts], hT[:, jc, ts * 128:(ts + 1) * 128],
                                        w2sb[:, k2 % 3, ji, :],
                                        start=(jc == 0), stop=(jc == 31)),
                                        waits=[("dW2", w2_ld), ("act", h_act), ("dve", h_dve)] +
                                              ([("dve", y_ev)] if jc == 0 else []),
                                        inc="pe" if ts == 7 else None)
                            w2_free[k2 % 3] = kb.now("pe")
                        y_pe = kb.now("pe")
                        for ts in range(8):
                            k = len(ys_hist)
                            war = ys_hist[k - 4][1] if k >= 4 else 0
                            war_sem = ys_hist[k - 4][0] if k >= 4 else "dY0"
                            y_ev = kb.op("vector", lambda e, ts=ts, ep=ep: e.tensor_scalar(
                                out=ysc[:, ts % 4, :], in0=psY[ts],
                                scalar1=mgcol[:, ep, ts:ts + 1], scalar2=None, op0=ALU.mult),
                                waits=[("pe", y_pe), (war_sem, war)], inc="dve")
                            sname = "dY%d" % ts
                            jcol = ddh * 8 + ts
                            cnt = kb.op("gpsimd", lambda e, ts=ts, ddh=ddh, ep=ep, jcol=jcol: e.indirect_dma_start(
                                out=d_y.ap(), out_offset=IndirectOffsetOnAxis(idxsc[:, jcol:jcol + 1], 0),
                                in_=ysc[:, ts % 4, :], in_offset=None,
                                element_offset=ddh * 512,
                                compute_op=ALU.add),
                                waits=[("dve", y_ev), ("dYZ", yz_done)] +
                                      [(sn, cv) for sn, cv in ys_prev_expert.items()],
                                inc=sname, inck=16, unordered=True)
                            ys_hist.append((sname, cnt))
                            if ddh == 1 and ts == 1:
                                yev_bank = [None, None]
                                yev_bank[0] = y_ev - 1
                                yev_bank[1] = y_ev
                            if ddh == 1 and ts == 3:
                                yev_first4 = y_ev
                    mg_war[ep] = y_ev
                    tp_done_prev = tp_done
                    ys_prev_expert = {("dY%d" % t): kb.now("dY%d" % t) for t in range(8)}
                ffn_dve = kb.now("dve")

                fx.close()
                kb.fence()
                outsb = sb(pr, "outsb", [128, 2, DM], F32)
                yld = sb(pr, "yld", [128, 2, DM], F32)
                meanall = sb(pr, "meanall", [128, NT], F32)
                msq = sb(pr, "msq", [128, NT], F32)
                ys_totals = [("dY%d" % t, kb.now("dY%d" % t)) for t in range(8)]
                war_d = [0, 0]
                war_a = [0, 0]
                for tt in range(NT):
                    ly = kb.op("sync", lambda e, tt=tt: e.dma_start(out=yld[:, tt % 2], in_=d_y.ap()[tt * 128:(tt + 1) * 128, :]),
                               waits=ys_totals + [("dve", war_d[tt % 2]), ("act", war_a[tt % 2])], inc="dYL", inck=16)
                    war_d[tt % 2] = kb.op("vector", lambda e, tt=tt: e.reduce_sum(out=meanall[:, tt:tt + 1], in_=yld[:, tt % 2], axis=AX.X),
                                          waits=[("dYL", ly)], inc="dve")
                    war_a[tt % 2] = kb.op("scalar", lambda e, tt=tt: e.activation(out=sqt[:], in_=yld[:, tt % 2], func=AF.Square, accum_out=varall[:, tt:tt + 1]),
                          waits=[("dYL", ly), ("act", kb.now("act"))] + ([("dYZ", (NT + 1) * 16)] if tt == 0 else []), inc="act")
                kb.op("vector", lambda e: e.tensor_scalar(out=meanall[:], in0=meanall[:], scalar1=1.0 / DM, scalar2=None, op0=ALU.mult), inc="dve")
                kb.op("vector", lambda e: e.tensor_mul(out=msq[:], in0=meanall[:], in1=meanall[:]), inc="dve")
                kb.op("vector", lambda e: e.tensor_scalar(out=varall[:], in0=varall[:], scalar1=1.0 / DM, scalar2=None, op0=ALU.mult),
                      waits=[("act", kb.now("act"))], inc="dve")
                kb.op("vector", lambda e: e.tensor_sub(out=varall[:], in0=varall[:], in1=msq[:]), inc="dve")
                a2 = kb.op("scalar", lambda e: e.activation(out=rstd[:], in_=varall[:], func=AF.Ln, bias=1e-5),
                           waits=[("dve", kb.now("dve"))], inc="act")
                kb.op("scalar", lambda e: e.activation(out=rstd[:], in_=rstd[:], func=AF.Exp, scale=-0.5),
                      waits=[("act", a2)], inc="act")
                ln2_act = kb.now("act")
                for tt in range(NT):
                    ly = kb.op("sync", lambda e, tt=tt: e.dma_start(out=yld[:, tt % 2], in_=d_y.ap()[tt * 128:(tt + 1) * 128, :]),
                               waits=[("act", ln2_act), ("dve", war_d[tt % 2])], inc="dYL", inck=16)
                    od = kb.op("vector", lambda e, tt=tt: e.tensor_scalar(out=outsb[:, tt % 2], in0=yld[:, tt % 2],
                                                                          scalar1=meanall[:, tt:tt + 1], scalar2=rstd[:, tt:tt + 1],
                                                                          op0=ALU.subtract, op1=ALU.mult),
                               waits=[("dYL", ly), ("out", max(0, (tt - 1) * 16))], inc="dve")
                    war_d[tt % 2] = od
                    kb.op("sync", lambda e, tt=tt: e.dma_start(out=d_out.ap()[tt * 128:(tt + 1) * 128, :], in_=outsb[:, tt % 2]),
                          waits=[("dve", od)], inc="out", inck=16)
                kb.wait("sync", ("out", NT * 16))


        except _Stop:
            pass

        blk = es.enter_context(nc.Block())
        kb.emit(blk)

    return nc


_NC_CACHE = None


def kernel(L0_attn_x, L0_attn_qkv_w, L0_attn_out_w, L0_attn_bias_p, L0_attn_bias_a,
           L0_attn_rope_freqs, L0_ffn_W_gate, L0_ffn_W1, L0_ffn_b1, L0_ffn_W2, L0_ffn_b2):
    global _NC_CACHE
    if _NC_CACHE is None:
        _NC_CACHE = build_program()
    nc = _NC_CACHE
    weights = {
        "qkvw": np.ascontiguousarray(np.asarray(L0_attn_qkv_w, dtype=np.float32)),
        "outw": np.ascontiguousarray(np.asarray(L0_attn_out_w, dtype=np.float32)),
        "bp": np.ascontiguousarray(np.asarray(L0_attn_bias_p, dtype=np.float32).reshape(H)),
        "ba": np.ascontiguousarray(np.asarray(L0_attn_bias_a, dtype=np.float32).reshape(H)),
        "freqs": np.ascontiguousarray(np.asarray(L0_attn_rope_freqs, dtype=np.float32)),
        "wg": np.ascontiguousarray(np.asarray(L0_ffn_W_gate, dtype=np.float32)),
        "w1": np.ascontiguousarray(np.asarray(L0_ffn_W1, dtype=np.float32)),
        "b1": np.ascontiguousarray(np.asarray(L0_ffn_b1, dtype=np.float32)),
        "w2": np.ascontiguousarray(np.asarray(L0_ffn_W2, dtype=np.float32)),
        "b2": np.ascontiguousarray(np.asarray(L0_ffn_b2, dtype=np.float32)),
    }
    x = np.asarray(L0_attn_x, dtype=np.float32)
    in_maps = []
    for i in range(8):
        m = {"x": np.ascontiguousarray(x[i])}
        m.update(weights)
        in_maps.append(m)
    res = run_bass_kernel_spmd(nc, in_maps, core_ids=list(range(8)))
    return np.stack([np.asarray(r["out"], dtype=np.float32) for r in res.results], axis=0)


# version 12
# speedup vs baseline: 1.0322x; 1.0053x over previous
import contextlib
import numpy as np
import concourse.bass as bass
import concourse.mybir as mybir
from concourse.ap import AP
from concourse.bass import IndirectOffsetOnAxis
from concourse.bass_utils import run_bass_kernel_spmd

F32 = mybir.dt.float32
BF16 = mybir.dt.bfloat16
I32 = mybir.dt.int32
AF = mybir.ActivationFunctionType
ALU = mybir.AluOpType
AX = mybir.AxisListType

S = 2048
DM = 1024
H = 16
E = 8
HF = 4096
NT = S // 128
SCALE = 0.125
CAP = 1024
BIGF = 1.0e6
TWO_PI = float(2.0 * np.pi)
HALF_PI = float(0.5 * np.pi)

ENGINES = ("tensor", "vector", "scalar", "gpsimd", "sync")


class KB:

    def __init__(self, nc, es):
        self.nc = nc
        self.streams = {e: [] for e in ENGINES}
        self.sems = {}
        self.cnt = {}
        self.last_wait = {e: {} for e in ENGINES}
        self._es = es

    def sem(self, name):
        self.sems[name] = self._es.enter_context(self.nc.semaphore(name))
        self.cnt[name] = 0

    def now(self, sem):
        return self.cnt[sem]

    AUTO = {"vector": "dve", "scalar": "act", "gpsimd": "gpc"}

    def op(self, eng, build, waits=(), inc=None, inck=1, unordered=False):
        if getattr(self, "stopped", False):
            return self.cnt.get(inc, 0)
        waits = list(waits)
        auto = self.AUTO.get(eng)
        if build is not None and auto is not None:
            waits.append((auto, self.cnt[auto]))
            if inc is None:
                inc = auto
                inck = 1
        if inc is not None and inck == 16 and not unordered:
            waits.append((inc, self.cnt[inc]))
        wl = []
        for s_, c in waits:
            if c > 0 and c > self.last_wait[eng].get(s_, 0):
                wl.append((s_, c))
                self.last_wait[eng][s_] = c
        if inc is not None:
            self.cnt[inc] += inck
        self.streams[eng].append((wl, build, inc, inck))
        return self.cnt[inc] if inc is not None else None

    def wait(self, eng, *waits):
        self.op(eng, None, waits=waits)

    def fence(self, engs=ENGINES):
        snap = [(s_, c) for s_, c in self.cnt.items() if c > 0]
        for e_ in engs:
            self.op(e_, None, waits=snap)

    def emit(self, blk):
        for eng in ENGINES:
            stream = self.streams[eng]
            if not stream:
                continue

            def make(stream=stream):
                def f(e):
                    for wl, build, inc, inck in stream:
                        for s_, c in wl:
                            e.wait_ge(self.sems[s_], c)
                        ins = build(e) if build is not None else None
                        if inc is not None:
                            if ins is None:
                                ins = e.nop()
                            ins.then_inc(self.sems[inc], inck)
                return f

            getattr(blk, eng)(make())


class _Stop(Exception):
    pass


def build_program(upto=None):
    nc = bass.Bass("TRN2", target_bir_lowering=False)

    d_x = nc.dram_tensor("x", [S, DM], F32, kind="ExternalInput")
    d_qkvw = nc.dram_tensor("qkvw", [3 * DM, DM], F32, kind="ExternalInput")
    d_outw = nc.dram_tensor("outw", [DM, DM], F32, kind="ExternalInput")
    d_bp = nc.dram_tensor("bp", [H], F32, kind="ExternalInput")
    d_ba = nc.dram_tensor("ba", [H], F32, kind="ExternalInput")
    d_freqs = nc.dram_tensor("freqs", [32], F32, kind="ExternalInput")
    d_wg = nc.dram_tensor("wg", [DM, E], F32, kind="ExternalInput")
    d_w1 = nc.dram_tensor("w1", [E, DM, HF], F32, kind="ExternalInput")
    d_b1 = nc.dram_tensor("b1", [E, HF], F32, kind="ExternalInput")
    d_w2 = nc.dram_tensor("w2", [E, HF, DM], F32, kind="ExternalInput")
    d_b2 = nc.dram_tensor("b2", [E, DM], F32, kind="ExternalInput")
    d_out = nc.dram_tensor("out", [S, DM], F32, kind="ExternalOutput")
    d_qkT = nc.dram_tensor("qkT_s", [DM, 2 * DM], BF16, kind="Internal")
    d_aT = nc.dram_tensor("aT_s", [DM, S], BF16, kind="Internal")
    d_band = nc.dram_tensor("band_s", [H, 128, 4096], BF16, kind="Internal")
    d_w1b = nc.dram_tensor("w1b_s", [E * DM * HF], BF16, kind="Internal")
    d_w2b = nc.dram_tensor("w2b_s", [E * HF * DM], BF16, kind="Internal")
    DM16 = DM + 16
    d_x1b = nc.dram_tensor("x1b_s", [S + 1, DM16], BF16, kind="Internal")
    d_idx = nc.dram_tensor("idx_s", [E * CAP + 128, 1], I32, kind="Internal")
    d_y = nc.dram_tensor("y_s", [S + 128, DM], F32, kind="Internal")

    es = contextlib.ExitStack()
    with es:
        kb = KB(nc, es)
        for s_ in ("pe", "act", "dve", "gpc", "out",
                   "dS", "dB2", "dXB", "dQW", "dTS", "dOW", "dBS",
                   "dWSL", "dBBF", "dSW", "dZA", "dRZ", "dAO", "dAOT", "dXR",
                   "dW1", "dW2", "dWC", "dWCo", "dID",
                   "dX1B", "dIX0", "dIX1", "dIDF", "dGX0", "dGX1", "dYZ", "dIL", "dYL",
                   "dY0", "dY1", "dY2", "dY3", "dY4", "dY5", "dY6", "dY7"):
            kb.sem(s_)

        def sb(stack, name, shape, dtype):
            return stack.enter_context(nc.sbuf_tensor(name, shape, dtype))

        P1 = es.enter_context(nc.psum_tensor("P1", [128, 1024], F32))
        P2 = es.enter_context(nc.psum_tensor("P2", [128, 1024], F32))
        P3 = es.enter_context(nc.psum_tensor("P3", [128, 1024], F32))
        P4t = es.enter_context(nc.psum_tensor("P4t", [128, 1024], F32))
        P4 = P4t[:, 0:512]
        P6 = P1[:].bitcast(BF16).rearrange("p (a b c) -> p a b c", a=2, b=8)

        ident = sb(es, "ident", [128, 128], BF16)
        psw = sb(es, "psw", [128, 128], BF16)
        onebf = sb(es, "onebf", [1, 1], BF16)
        ones1f = sb(es, "ones1f", [1, 128], F32)
        ones16 = sb(es, "ones16", [128, NT], F32)
        CONSTS = [0.0, 1.0, -1.0, 1e-5, -0.5, 0.125, HALF_PI]
        csb = sb(es, "csb", [128, len(CONSTS)], F32)
        for ci, cv in enumerate(CONSTS):
            nc.const_aps.register(F32, cv, csb[:, ci:ci + 1])
        b1sb = sb(es, "b1sb", [128, E, 32], F32)
        b2sb = sb(es, "b2sb", [E, DM], BF16)
        wghi = sb(es, "wghi", [128, 8, E], BF16)
        wglo = sb(es, "wglo", [128, 8, E], BF16)
        varall = sb(es, "varall", [128, NT], F32)
        rstd = sb(es, "rstd", [128, NT], F32)
        meancol = sb(es, "meancol", [128, 1], F32)

        episb = sb(es, "episb", [128, DM], F32)
        sqt = episb

        def _maybe_stop(stage):
            if upto != stage:
                return
            kb.fence()
            gm = kb.op("gpsimd", lambda e: e.memset(episb[:], 0.0))
            for tt_ in range(NT):
                kb.op("sync", lambda e, tt_=tt_: e.dma_start(out=d_out.ap()[tt_ * 128:(tt_ + 1) * 128, :], in_=episb[:]),
                      waits=[("gpc", kb.now("gpc"))], inc="out", inck=16)
            kb.wait("sync", ("out", NT * 16))
            kb.stopped = True

        try:
            att_era = es.enter_context(contextlib.ExitStack())
            xT = sb(att_era, "xT", [128, 8, S], BF16)
            V = sb(att_era, "V", [128, NT, H * 65], BF16)
            C4 = sb(att_era, "C4", [128, S], F32)
            S4 = sb(att_era, "S4", [128, S], F32)
            with contextlib.ExitStack() as st1:
                posrow = sb(st1, "posrow", [1, S], F32)
                sgni = sb(st1, "sgni", [128, 1], I32)
                bandabs = sb(st1, "bandabs", [128, 4096], F32)
                qkvwb = sb(st1, "qkvwb", [128, 6, DM], BF16)
                qkvwTv = sb(st1, "qkvwTv", [128, 8, DM], BF16)
                tstage = sb(st1, "tstage", [128, 2, 8, 128], BF16)
                xb = sb(st1, "xb", [128, 2, DM], BF16)
                xbf = sb(st1, "xbf", [128, 2, DM], F32)
                wgf = sb(st1, "wgf", [128, 8, E], F32)
                bparow = sb(st1, "bparow", [1, 2, H], F32)
                paac = sb(st1, "paac", [128, H], F32)
                npac = sb(st1, "npac", [128, H], F32)

                kb.op("sync", lambda e: e.dma_start(out=bparow[:, 0], in_=AP(tensor=d_ba, offset=0, ap=[[1, 1], [1, H]])), inc="dS", inck=16)
                kb.op("sync", lambda e: e.dma_start(out=bparow[:, 1], in_=AP(tensor=d_bp, offset=0, ap=[[1, 1], [1, H]])), inc="dS", inck=16)
                for e_i in range(E):
                    for jc in range(32):
                        kb.op("sync", lambda e, e_i=e_i, jc=jc: e.dma_start(
                            out=b1sb[:, e_i, jc:jc + 1],
                            in_=AP(tensor=d_b1, offset=e_i * HF + jc * 128, ap=[[1, 128], [1, 1]])), inc="dS", inck=16, unordered=True)
                kb.op("sync", lambda e: e.dma_start(out=wgf[:], in_=AP(tensor=d_wg, offset=0, ap=[[E, 128], [128 * E, 8], [1, E]])), inc="dS", inck=16)
                sld_small = kb.now("dS")
                b2_gld = kb.op("gpsimd", lambda e: e.dma_start(out=b2sb[:], in_=d_b2.ap()), inc="dB2", inck=16)

                for ci, cv in enumerate(CONSTS):
                    kb.op("vector", lambda e, ci=ci, cv=cv: e.memset(csb[:, ci:ci + 1], cv))
                imz = kb.op("vector", lambda e: e.memset(ident[:], 0.0))
                ob = kb.op("vector", lambda e: e.memset(onebf[:], 1.0))
                ident_gpc = 0
                for ii in range(128):
                    kb.op("sync", lambda e, ii=ii: e.dma_start(
                        out=AP(tensor=ident, offset=ii * 128 + ii, ap=[[128, 1], [1, 1]]),
                        in_=onebf[:]),
                        waits=[("dve", ob)], inc="dID", inck=16, unordered=True)
                ident_dma = kb.now("dID")
                pm = kb.op("vector", lambda e: e.memset(psw[:], 0.0), inc="dve")
                for (ob, ib) in ((0, 32), (32, 0), (64, 96), (96, 64)):
                    kb.op("sync", lambda e, ob=ob, ib=ib: e.dma_start(
                        out=psw[ib:ib + 32, ob:ob + 32], in_=ident[ib:ib + 32, ib:ib + 32]),
                        waits=[("dID", ident_dma), ("dve", pm)], inc="dSW", inck=16, unordered=True)
                psw_ld = kb.now("dSW")
                kb.op("vector", lambda e: e.memset(ones1f[:], 1.0), inc="dve")
                v16 = kb.op("vector", lambda e: e.memset(ones16[:], 1.0), inc="dve")
                for vh in range(H):
                    kb.op("vector", lambda e, vh=vh: e.tensor_copy(
                        out=AP(tensor=V, offset=vh * 65 + 64, ap=[[NT * H * 65, 128], [H * 65, NT], [1, 1]]),
                        in_=ones16[:]),
                        waits=[("dve", v16)], inc="dve" if vh == H - 1 else None)
                gpc_const = kb.now("gpc")

                _maybe_stop("s1")
                _pc_state = {"i": 0}
                W1_ELEMS = E * DM * HF
                PC_TOTAL = 2 * W1_ELEMS // (128 * 4096)

                def precast_pump(n):
                    for _ in range(n):
                        i = _pc_state["i"]
                        if i >= PC_TOTAL:
                            return
                        _pc_state["i"] = i + 1
                        if i < PC_TOTAL // 2:
                            src_t, dst_t, off = d_w1, d_w1b, i * 128 * 4096
                        else:
                            src_t, dst_t, off = d_w2, d_w2b, (i - PC_TOTAL // 2) * 128 * 4096
                        kb.op("gpsimd", lambda e, src_t=src_t, dst_t=dst_t, off=off: e.dma_start(
                            out=AP(tensor=dst_t, offset=off, ap=[[4096, 128], [1, 4096]]),
                            in_=AP(tensor=src_t, offset=off, ap=[[4096, 128], [1, 4096]])),
                            inc="dWCo", inck=16)

                qkT_sld = [0, 0]
                qk_evs = [0, 0]
                for rnd in range(4):
                    gld_qkvw = kb.op("gpsimd", lambda e, rnd=rnd: e.dma_start(
                        out=qkvwb[:], in_=d_qkvw.ap()[rnd * 6 * 128:(rnd + 1) * 6 * 128, :].rearrange("(c p) d -> p c d", p=128)),
                        waits=[("pe", kb.now("pe") if rnd else 0)], inc="dQW", inck=16)
                    for cc12 in range(6):
                        cc = rnd * 6 + cc12
                        par = cc % 2
                        for dc in range(8):
                            kb.op("tensor", lambda e, cc12=cc12, dc=dc, par=par: e.transpose(P6[:, par, dc], qkvwb[:, cc12, dc * 128:(dc + 1) * 128], ident[:]),
                                  waits=[("dQW", gld_qkvw), ("dID", ident_dma), ("dve", qk_evs[par])],
                                  inc="pe" if dc == 7 else None)
                        if cc < 16:
                            qk_evs[par] = kb.op("vector", lambda e, par=par: e.tensor_copy(out=tstage[:, par], in_=P6[:, par]),
                                  waits=[("pe", kb.now("pe")), ("dTS", qkT_sld[par])], inc="dve")
                            qkT_sld[par] = kb.op("sync", lambda e, cc=cc, par=par: e.dma_start(
                                out=AP(tensor=d_qkT, offset=cc * 128, ap=[[2 * DM, 128], [128 * 2 * DM, 8], [1, 128]]),
                                in_=tstage[:, par]),
                                waits=[("dve", kb.now("dve"))], inc="dTS", inck=16)
                        else:
                            qk_evs[par] = kb.op("vector", lambda e, cc=cc, par=par: e.tensor_copy(
                                out=AP(tensor=qkvwTv, offset=(cc - 16) * 128, ap=[[8 * DM, 128], [DM, 8], [1, 128]]),
                                in_=P6[:, par]),
                                waits=[("pe", kb.now("pe"))], inc="dve")
                qkT_store_sld = kb.now("dTS")
                kb.op("gpsimd", lambda e: e.iota(posrow[:], pattern=[[1, S]], base=0, channel_multiplier=0, allow_small_or_imprecise_dtypes=True), inc="gpc")
                kb.op("gpsimd", lambda e: e.iota(sgni[:], pattern=[[1, 1]], base=0, channel_multiplier=1), inc="gpc")
                kb.op("gpsimd", lambda e: e.iota(bandabs[:], pattern=[[-1, 4096]], base=1920, channel_multiplier=1, allow_small_or_imprecise_dtypes=True), inc="gpc")
                iota_gpc = kb.now("gpc")
                precast_pump(4)
                _maybe_stop("s3")
                xT_evs = [qk_evs[0], qk_evs[1]]
                xc_dve = [0, 0]
                for tt in range(NT):
                    gld_x = kb.op("sync", lambda e, tt=tt: e.dma_start(out=xbf[:, tt % 2], in_=d_x.ap()[tt * 128:(tt + 1) * 128, :]),
                                  waits=[("dve", xc_dve[tt % 2])], inc="dXB", inck=16)
                    xc_dve[tt % 2] = kb.op("vector", lambda e, tt=tt: e.tensor_copy(out=xb[:, tt % 2], in_=xbf[:, tt % 2]),
                                           waits=[("dXB", gld_x), ("pe", kb.now("pe") if tt >= 2 else 0)], inc="dve")
                    for c in range(8):
                        kb.op("tensor", lambda e, tt=tt, c=c: e.transpose(P6[:, tt % 2, c], xb[:, tt % 2, c * 128:(c + 1) * 128], ident[:]),
                              waits=[("dve", xc_dve[tt % 2]), ("dID", ident_dma), ("dve", xT_evs[tt % 2])],
                              inc="pe" if c == 7 else None)
                    xT_evs[tt % 2] = kb.op("vector", lambda e, tt=tt: e.tensor_copy(
                        out=AP(tensor=xT, offset=tt * 128, ap=[[8 * S, 128], [S, 8], [1, 128]]),
                        in_=P6[:, tt % 2]),
                        waits=[("pe", kb.now("pe"))], inc="dve")
                xT_dve = kb.now("dve")
                _maybe_stop("s2")


                for tt in range(NT):
                    for hf in range(2):
                        for d in range(8):
                            kb.op("tensor", lambda e, tt=tt, hf=hf, d=d: e.matmul(
                                P4, xT[:, d, tt * 128:(tt + 1) * 128],
                                qkvwTv[:, d, hf * 512:(hf + 1) * 512],
                                start=(d == 0), stop=(d == 7)),
                                waits=[("dve", xT_dve if (tt == 0 and hf == 0) else kb.now("dve"))],
                                inc="pe" if d == 7 else None)
                        pv = kb.now("pe")
                        kb.op("vector", lambda e, tt=tt, hf=hf: e.tensor_copy(
                            out=AP(tensor=V, offset=tt * H * 65 + hf * 8 * 65, ap=[[NT * H * 65, 128], [65, 8], [1, 64]]),
                            in_=AP(tensor=P4t, offset=0, ap=[[1024, 128], [64, 8], [1, 64]])),
                            waits=[("pe", pv)], inc="dve")
                V_dve = kb.now("dve")
                precast_pump(8)
                _maybe_stop("s4")


                kb.op("vector", lambda e: e.tensor_copy(out=wghi[:], in_=wgf[:]), waits=[("dS", sld_small)], inc="dve")
                kb.op("vector", lambda e: e.scalar_tensor_tensor(out=wglo[:], in0=wghi[:], scalar=-1.0, in1=wgf[:], op0=ALU.mult, op1=ALU.add), inc="dve")

                with contextlib.ExitStack() as st2:
                    freq4 = sb(st2, "freq4", [128, 1], F32)
                    angs = sb(st2, "angs", [128, S], F32)
                    angred = sb(st2, "angred", [128, S], F32)
                    angi = sb(st2, "angi", [128, S], I32)
                    sgnf = sb(st2, "sgnf", [128, 1], F32)
                    sgncol = sb(st2, "sgncol", [128, 1], F32)
                    sgt = sb(st2, "sgt", [128, S], F32)

                    kb.op("sync", lambda e: e.dma_start(out=freq4[:], in_=AP(tensor=d_freqs, offset=0, ap=[[0, 4], [1, 32], [1, 1]])), inc="dS", inck=16)
                    sld_freq = kb.now("dS")
                    gpc_rope = iota_gpc
                    for qt in range(4):
                        kb.op("tensor", lambda e, qt=qt: e.matmul(P4, ones1f[:], posrow[:, qt * 512:(qt + 1) * 512], start=True, stop=True),
                              waits=[("gpc", gpc_rope), ("dve", kb.now("dve")), ("dS", sld_freq)], inc="pe")
                        kb.op("vector", lambda e, qt=qt: e.tensor_scalar(out=angs[:, qt * 512:(qt + 1) * 512], in0=P4, scalar1=freq4[:], scalar2=None, op0=ALU.mult),
                              waits=[("pe", kb.now("pe"))], inc="dve")
                    kb.op("vector", lambda e: e.tensor_scalar(out=angred[:], in0=angs[:], scalar1=1.0 / TWO_PI, scalar2=None, op0=ALU.mult), inc="dve")
                    kb.op("vector", lambda e: e.tensor_copy(out=angi[:], in_=angred[:]), inc="dve")
                    kb.op("vector", lambda e: e.tensor_copy(out=angred[:], in_=angi[:]), inc="dve")
                    kb.op("vector", lambda e: e.scalar_tensor_tensor(out=angred[:], in0=angred[:], scalar=-TWO_PI, in1=angs[:], op0=ALU.mult, op1=ALU.add), inc="dve")
                    kb.op("vector", lambda e: e.tensor_scalar(out=sgt[:], in0=angred[:], scalar1=float(np.pi), scalar2=-TWO_PI, op0=ALU.is_gt, op1=ALU.mult), inc="dve")
                    kb.op("vector", lambda e: e.tensor_add(out=angred[:], in0=angred[:], in1=sgt[:]), inc="dve")
                    kb.op("vector", lambda e: e.tensor_scalar(out=sgt[:], in0=angred[:], scalar1=float(-np.pi), scalar2=TWO_PI, op0=ALU.is_lt, op1=ALU.mult), inc="dve")
                    kb.op("vector", lambda e: e.tensor_add(out=angred[:], in0=angred[:], in1=sgt[:]), inc="dve")
                    dve_ang = kb.now("dve")
                    act_sin = kb.op("scalar", lambda e: e.activation(out=S4[:], in_=angred[:], func=AF.Sin), waits=[("dve", dve_ang)], inc="act")
                    aab = kb.op("scalar", lambda e: e.activation(out=angs[:], in_=angred[:], func=AF.Abs),
                                waits=[("act", act_sin), ("dve", dve_ang)], inc="act")
                    kb.op("scalar", lambda e: e.activation(out=C4[:], in_=angs[:], func=AF.Sin, bias=HALF_PI, scale=-1.0), waits=[("act", aab)], inc="act")
                    kb.op("vector", lambda e: e.tensor_scalar(out=sgni[:], in0=sgni[:], scalar1=32, scalar2=None, op0=ALU.bitwise_and), waits=[("gpc", gpc_rope)], inc="dve")
                    kb.op("vector", lambda e: e.tensor_copy(out=sgnf[:], in_=sgni[:]), inc="dve")
                    kb.op("vector", lambda e: e.tensor_scalar(out=sgncol[:], in0=sgnf[:], scalar1=1.0 / 16.0, scalar2=-1.0, op0=ALU.mult, op1=ALU.add), inc="dve")
                    kb.op("vector", lambda e: e.tensor_scalar(out=S4[:], in0=S4[:], scalar1=sgncol[:], scalar2=None, op0=ALU.mult),
                          waits=[("act", kb.now("act"))], inc="dve")

                    kb.op("tensor", lambda e: e.matmul(P3[:, 0:H], ones1f[:], bparow[:, 0], start=True, stop=True),
                          waits=[("dS", sld_small), ("dve", kb.now("dve"))], inc="pe")
                    kb.op("vector", lambda e: e.tensor_scalar(out=paac[:], in0=P3[:, 0:H], scalar1=0.01, scalar2=None, op0=ALU.max),
                          waits=[("pe", kb.now("pe"))], inc="dve")
                    kb.op("tensor", lambda e: e.matmul(P3[:, 0:H], ones1f[:], bparow[:, 1], start=True, stop=True),
                          waits=[("dve", kb.now("dve"))], inc="pe")
                    kb.op("vector", lambda e: e.tensor_scalar(out=npac[:], in0=P3[:, 0:H], scalar1=0.01, scalar2=-1.0, op0=ALU.max, op1=ALU.mult),
                          waits=[("pe", kb.now("pe"))], inc="dve")

                _maybe_stop("s5")
                kb.fence()
                with contextlib.ExitStack() as st3:
                    bandtmp = sb(st3, "bandtmp", [128, 2, 4096], BF16)
                    bandst = sb(st3, "bandst", [128, 2, 4096], BF16)

                    kb.op("scalar", lambda e: e.activation(out=bandabs[:], in_=bandabs[:], func=AF.Abs),
                          waits=[("gpc", iota_gpc)], inc="act")
                    babs_act = kb.now("act")
                    babs_dve = kb.now("dve")
                    bst_st = [0, 0]
                    bsc_mark = [0, 0]
                    for h in range(H):
                        a_ln = kb.op("scalar", lambda e, h=h: e.activation(out=bandtmp[:, h % 2], in_=bandabs[:], func=AF.Ln, bias=1.0, scale=paac[:, h:h + 1]),
                                     waits=[("act", babs_act), ("dve", max(babs_dve, bsc_mark[h % 2]))], inc="act")
                        d_sc = kb.op("vector", lambda e, h=h: e.tensor_scalar(out=bandst[:, h % 2], in0=bandtmp[:, h % 2], scalar1=npac[:, h:h + 1], scalar2=None, op0=ALU.mult),
                                     waits=[("act", a_ln), ("dBS", bst_st[h % 2])], inc="dve")
                        bsc_mark[h % 2] = d_sc
                        bst_st[h % 2] = kb.op("sync", lambda e, h=h: e.dma_start(out=d_band.ap()[h], in_=bandst[:, h % 2]),
                                              waits=[("dve", d_sc)], inc="dBS", inck=16)
                    band_sld = kb.now("dBS")
                setup_dve = kb.now("dve")
                setup_act = kb.now("act")
                kb.wait("tensor", ("dve", setup_dve), ("act", setup_act), ("dBS", kb.now("dBS")), ("dTS", kb.now("dTS")))
                setup_bar = kb.op("tensor", lambda e: e.nop(), inc="pe")
            _maybe_stop("setup")
            kb.fence()

            with contextlib.ExitStack() as at:
                wsl = sb(at, "wsl", [128, 4, 8, 256], BF16)
                qbf = sb(at, "qbf", [128, S], BF16)
                ctmp = sb(at, "ctmp", [128, S], F32)
                qr = sb(at, "qr", [128, S], BF16)
                kr = sb(at, "kr", [128, S], BF16)
                Pp = sb(at, "Pp", [128, 2, 1024], BF16)
                araw = sb(at, "araw", [64, 2, S], BF16)
                aout = sb(at, "aout", [64, S], BF16)
                bandbf = sb(at, "bandbf", [128, 2, 4096], BF16)
                zasm = sb(at, "zasm", [4, 2, 512], F32)
                rzasm = sb(at, "rzasm", [4, 512], F32)
                rzflat = sb(at, "rzflat", [1, 2, 4, 512], F32)
                ztmp = sb(at, "ztmp", [1, 2, 512], F32)

                psS = (P1, P2)
                psV_banks = (P3, P4t)

                exp_mark = [0, 0]
                kb_mark = [0, 0]
                zrec_mark = 0
                psv_z_p = [0, 0]
                psv_dve_p = [0, 0]
                araw_free_p = [0, 0]
                pending_tail = []
                nonlocal_state = {"zrec_mark": 0, "aout_dma": 0}
                ztail_pe_p = [0, 0]
                rz_ld_p = [0, 0]

                qkv_done = [0, 0, 0, 0]
                swp_war = [0, 0]
                qbf_war = [0, 0, 0, 0]
                for p in range(H // 2):
                    precast_pump(15)
                    war = qkv_done[p % 4]
                    w_ld = kb.op("sync", lambda e, p=p: e.dma_start(
                        out=wsl[:, p % 4, :, 0:128],
                        in_=AP(tensor=d_qkT, offset=p * 128, ap=[[2 * DM, 128], [128 * 2 * DM, 8], [1, 128]])),
                        waits=[("pe", war), ("dTS", qkT_store_sld)], inc="dWSL", inck=16)
                    w_ld = kb.op("sync", lambda e, p=p: e.dma_start(
                        out=wsl[:, p % 4, :, 128:256],
                        in_=AP(tensor=d_qkT, offset=DM + p * 128, ap=[[2 * DM, 128], [128 * 2 * DM, 8], [1, 128]])),
                        inc="dWSL", inck=16)
                    for hh in range(2):
                        h = 2 * p + hh
                        kb.op("sync", lambda e, h=h: e.dma_start(out=bandbf[:, h % 2], in_=d_band.ap()[h]),
                              waits=[("dBS", band_sld), ("pe", kb_mark[h % 2])], inc="dBBF", inck=16)
                    band_ld = kb.now("dBBF")

                    for w, dst, dsc in ((0, qr, SCALE), (1, kr, 1.0)):
                        evac_marks = {}
                        for qt in range(4):
                            g = (w * 4 + qt) % 2
                            psq = P1[:, 0:512] if g == 0 else P2[:, 0:512]
                            pssw = P1[:, 512:1024] if g == 0 else P2[:, 512:1024]
                            for d in range(8):
                                kb.op("tensor", lambda e, p=p, w=w, qt=qt, d=d, psq=psq: e.matmul(
                                    psq, wsl[:, p % 4, d, w * 128:(w + 1) * 128],
                                    xT[:, d, qt * 512:(qt + 1) * 512],
                                    start=(d == 0), stop=(d == 7)),
                                    waits=[("dWSL", w_ld), ("pe", setup_bar),
                                           ("act", max(exp_mark[0], exp_mark[1], evac_marks.get(g, 0)))],
                                    inc="pe" if d == 7 else None)
                            ev = kb.op("scalar", lambda e, qt=qt, dsc=dsc, psq=psq: e.activation(
                                out=qbf[:, qt * 512:(qt + 1) * 512], in_=psq, func=AF.Copy, scale=dsc),
                                waits=[("pe", kb.now("pe")), ("dve", qbf_war[qt])], inc="act")
                            evac_marks[g] = ev
                            sm = kb.op("tensor", lambda e, qt=qt, pssw=pssw: e.matmul(
                                pssw, psw[:], qbf[:, qt * 512:(qt + 1) * 512], start=True, stop=True),
                                waits=[("act", ev), ("dSW", psw_ld), ("dve", swp_war[g])], inc="pe")
                            qbf_war[qt] = kb.op("vector", lambda e, qt=qt: e.tensor_mul(
                                out=ctmp[:, qt * 512:(qt + 1) * 512], in0=qbf[:, qt * 512:(qt + 1) * 512],
                                in1=C4[:, qt * 512:(qt + 1) * 512]),
                                waits=[("act", ev)], inc="dve")
                            kb.op("vector", lambda e, qt=qt, dst=dst, pssw=pssw: e.tensor_mul(
                                out=dst[:, qt * 512:(qt + 1) * 512], in0=pssw,
                                in1=S4[:, qt * 512:(qt + 1) * 512]),
                                waits=[("pe", sm)], inc="dve")
                            swp_war[g] = kb.op("vector", lambda e, qt=qt, dst=dst: e.tensor_add(
                                out=dst[:, qt * 512:(qt + 1) * 512], in0=dst[:, qt * 512:(qt + 1) * 512],
                                in1=ctmp[:, qt * 512:(qt + 1) * 512]), inc="dve")
                    rope_dve = kb.now("dve")
                    qkv_done[p % 4] = kb.now("pe")

                    for hh in range(2):
                        h = 2 * p + hh
                        hb = hh * 64
                        bpar = h % 2
                        vpar = h % 2
                        PV = psV_banks[vpar]
                        for qh in range(2):
                            q0 = qh * 1024
                            evs = [0] * 16

                            def emit_attnV(kt, h=h, PV=PV, vpar=vpar):
                                for j in range(2):
                                    kb.op("tensor", lambda e, kt=kt, h=h, j=j, PV=PV: e.matmul(
                                        PV[0:65, j * 512:(j + 1) * 512], V[:, kt, h * 65:(h + 1) * 65],
                                        Pp[:, kt % 2, j * 512:(j + 1) * 512],
                                        start=(kt == 0), stop=(kt == 15)),
                                        waits=[("act", evs[kt])] +
                                              ([("dZA", psv_z_p[vpar]), ("dve", psv_dve_p[vpar])] if kt == 0 else []),
                                        inc="pe" if (j == 1 and kt == 15) else None)

                            for kt in range(16):
                                par = kt % 2
                                for j in range(2):
                                    kb.op("tensor", lambda e, hb=hb, kt=kt, q0=q0, j=j, par=par: e.matmul(
                                        psS[par][:, j * 512:(j + 1) * 512],
                                        kr[hb:hb + 64, kt * 128:(kt + 1) * 128],
                                        qr[hb:hb + 64, q0 + j * 512:q0 + (j + 1) * 512],
                                        start=True, stop=False),
                                        waits=[("dve", rope_dve), ("dBBF", band_ld),
                                               ("act", exp_mark[par])])
                                for j in range(2):
                                    u0 = q0 + j * 512 + 1920 - kt * 128
                                    r = kb.op("tensor", lambda e, par=par, bpar=bpar, u0=u0, j=j: e.matmul(
                                        psS[par][:, j * 512:(j + 1) * 512],
                                        ident[:], bandbf[:, bpar, u0:u0 + 512],
                                        start=False, stop=True),
                                        inc="pe" if j == 1 else None)
                                    if r is not None:
                                        kb_mark[bpar] = r
                                sdone = kb.now("pe")
                                if kt >= 1:
                                    emit_attnV(kt - 1)
                                if qh == 0 and kt == 8 and pending_tail:
                                    pending_tail.pop(0)()
                                exp_mark[par] = kb.op("scalar", lambda e, par=par: e.activation(
                                    out=Pp[:, par, :], in_=psS[par][:], func=AF.Exp),
                                    waits=[("pe", sdone)], inc="act")
                                evs[kt] = exp_mark[par]
                            emit_attnV(15)
                            pv_done = kb.now("pe")
                            for j in range(2):
                                kb.op("vector", lambda e, q0=q0, j=j, PV=PV, vpar=vpar: e.tensor_copy(
                                    out=araw[:, vpar, q0 + j * 512:q0 + (j + 1) * 512], in_=PV[0:64, j * 512:(j + 1) * 512]),
                                    waits=[("pe", pv_done), ("dve", araw_free_p[vpar])], inc="dve")
                                za = kb.op("vector", lambda e, j=j, PV=PV: e.tensor_copy(
                                    out=ztmp[:, j, :], in_=PV[64:65, j * 512:(j + 1) * 512]),
                                    waits=[("pe", pv_done), ("dZA", kb.now("dZA"))], inc="dve")
                                psv_z_p[vpar] = kb.op("sync", lambda e, qh=qh, j=j, vpar=vpar: e.dma_start(
                                    out=zasm[2 * qh + j:2 * qh + j + 1, vpar, :], in_=ztmp[:, j, :]),
                                    waits=[("dve", za)], inc="dZA", inck=16)
                        araw_act = kb.now("dve")
                        nonlocal_state["zrec_mark"] = kb.op("vector",
                            lambda e, vpar=vpar: e.reciprocal(out=rzasm[:], in_=zasm[:, vpar, :]),
                            waits=[("dZA", psv_z_p[vpar]), ("dRZ", kb.now("dRZ")), ("pe", ztail_pe_p[vpar])], inc="dve")
                        rz_ld_p[vpar] = kb.op("sync", lambda e, vpar=vpar: e.dma_start(
                            out=rzflat[0:1, vpar], in_=rzasm[:]),
                            waits=[("dve", nonlocal_state["zrec_mark"])], inc="dRZ", inck=16)

                        def make_tail(h=h, vpar=vpar, PV=PV, araw_act=araw_act, rz_ld=None):
                            rz_ld = rz_ld_p[vpar]

                            def tail():
                                pd_prev = araw_act
                                for qj in range(4):
                                    kb.op("tensor", lambda e, qj=qj, PV=PV, vpar=vpar: e.matmul(
                                        PV[0:64, 0:512], ones1f[:, 0:64], rzflat[0:1, vpar, qj, :], start=True, stop=True),
                                        waits=[("dRZ", rz_ld), ("dve", pd_prev)], inc="pe")
                                    pd = kb.op("vector", lambda e, qj=qj, PV=PV, vpar=vpar: e.tensor_tensor(
                                        out=aout[:, qj * 512:(qj + 1) * 512],
                                        in0=araw[:, vpar, qj * 512:(qj + 1) * 512], in1=PV[0:64, 0:512], op=ALU.mult),
                                        waits=[("pe", kb.now("pe")), ("dAO", nonlocal_state["aout_dma"])], inc="dve")
                                    psv_dve_p[vpar] = pd
                                    pd_prev = pd
                                    nonlocal_state["aout_dma"] = kb.op("sync", lambda e, h=h, qj=qj: e.dma_start(
                                        out=d_aT.ap()[h * 64:(h + 1) * 64, qj * 512:(qj + 1) * 512],
                                        in_=aout[:, qj * 512:(qj + 1) * 512]),
                                        waits=[("dve", pd)], inc="dAO", inck=16)
                                araw_free_p[vpar] = psv_dve_p[vpar]
                                ztail_pe_p[vpar] = kb.now("pe")
                            return tail

                        pending_tail.append(make_tail())
                while pending_tail:
                    pending_tail.pop(0)()
                precast_pump(10**9)
                att_sld = kb.now("dAO")
                kb.wait("tensor", ("dAO", att_sld), ("dZA", kb.now("dZA")), ("dRZ", kb.now("dRZ")), ("dSW", kb.now("dSW")), ("dve", kb.now("dve")), ("act", kb.now("act")))
                att_bar = kb.op("tensor", lambda e: e.nop(), inc="pe")
            att_era.close()
            _maybe_stop("attention")
            kb.fence()

            with contextlib.ExitStack() as pr:
                m_all = sb(pr, "m_all", [128, NT, E], F32)
                idxall = sb(pr, "idxall", [128, E, 8], I32)
                mT8 = sb(pr, "mT8", [8, S], BF16)
                x1s = es.enter_context(contextlib.ExitStack())
                x1 = sb(x1s, "x1", [128, NT, DM], F32)
                with contextlib.ExitStack() as pro:
                    outwb = sb(pro, "outwb", [128, 8, DM], BF16)
                    outwT = sb(pro, "outwT", [128, 8, DM], BF16)
                    aoT = sb(pro, "aoT", [128, 2, 8, 128], BF16)
                    xres = sb(pro, "xres", [128, 2, DM], F32)

                    gld_outw = kb.op("gpsimd", lambda e: e.dma_start(out=outwb[:], in_=d_outw.ap().rearrange("(c p) d -> p c d", p=128)),
                                     waits=[("pe", att_bar)], inc="dOW", inck=16)
                    ow_evs = [0, 0]
                    for cc in range(8):
                        for dc in range(8):
                            kb.op("tensor", lambda e, cc=cc, dc=dc: e.transpose(P6[:, cc % 2, dc], outwb[:, cc, dc * 128:(dc + 1) * 128], ident[:]),
                                  waits=[("dOW", gld_outw), ("dve", ow_evs[cc % 2])],
                                  inc="pe" if dc == 7 else None)
                        ow_evs[cc % 2] = kb.op("vector", lambda e, cc=cc: e.tensor_copy(
                            out=AP(tensor=outwT, offset=cc * 128, ap=[[8 * DM, 128], [DM, 8], [1, 128]]),
                            in_=P6[:, cc % 2]),
                            waits=[("pe", kb.now("pe"))], inc="dve")
                    owt_dve = kb.now("dve")

                    for tt in range(NT):
                        war = kb.now("pe") if tt >= 2 else 0
                        ld1 = kb.op("sync", lambda e, tt=tt: e.dma_start(
                            out=aoT[:, tt % 2],
                            in_=AP(tensor=d_aT, offset=tt * 128, ap=[[S, 128], [128 * S, 8], [1, 128]])),
                            waits=[("dAO", att_sld), ("pe", war)], inc="dAOT", inck=16)
                        war2 = kb.now("dve") if tt >= 2 else 0
                        ld2 = kb.op("sync", lambda e, tt=tt: e.dma_start(out=xres[:, tt % 2], in_=d_x.ap()[tt * 128:(tt + 1) * 128, :]),
                                    waits=[("dve", war2)], inc="dXR", inck=16)
                        for dd in range(2):
                            for c in range(8):
                                kb.op("tensor", lambda e, tt=tt, dd=dd, c=c: e.matmul(
                                    P1[:, dd * 512:(dd + 1) * 512], aoT[:, tt % 2, c, :],
                                    outwT[:, c, dd * 512:(dd + 1) * 512],
                                    start=(c == 0), stop=(c == 7)),
                                    waits=[("dAOT", ld1), ("dve", owt_dve), ("dve", kb.now("dve"))],
                                    inc="pe" if c == 7 else None)
                        mm_tt = kb.now("pe")
                        for dd in range(2):
                            kb.op("vector", lambda e, tt=tt, dd=dd: e.tensor_add(
                                out=x1[:, tt, dd * 512:(dd + 1) * 512], in0=xres[:, tt % 2, dd * 512:(dd + 1) * 512],
                                in1=P1[:, dd * 512:(dd + 1) * 512]),
                                waits=[("pe", mm_tt), ("dXR", ld2)], inc="dve")
                        kb.op("vector", lambda e, tt=tt: e.reduce_sum(out=meancol[:], in_=x1[:, tt, :], axis=AX.X), inc="dve")
                        kb.op("vector", lambda e: e.tensor_scalar(out=meancol[:], in0=meancol[:], scalar1=1.0 / DM, scalar2=None, op0=ALU.mult), inc="dve")
                        kb.op("vector", lambda e, tt=tt: e.tensor_scalar(out=x1[:, tt, :], in0=x1[:, tt, :], scalar1=meancol[:], scalar2=None, op0=ALU.subtract), inc="dve")
                        kb.op("scalar", lambda e, tt=tt: e.activation(out=sqt[:], in_=x1[:, tt, :], func=AF.Square, accum_out=varall[:, tt:tt + 1]),
                              waits=[("dve", kb.now("dve")), ("act", kb.now("act"))], inc="act")
                kb.op("vector", lambda e: e.tensor_scalar(out=varall[:], in0=varall[:], scalar1=1.0 / DM, scalar2=None, op0=ALU.mult),
                      waits=[("act", kb.now("act"))], inc="dve")
                a1 = kb.op("scalar", lambda e: e.activation(out=rstd[:], in_=varall[:], func=AF.Ln, bias=1e-5),
                           waits=[("dve", kb.now("dve"))], inc="act")
                kb.op("scalar", lambda e: e.activation(out=rstd[:], in_=rstd[:], func=AF.Exp, scale=-0.5),
                      waits=[("act", a1)], inc="act")
                ln1_act = kb.now("act")
                for tt in range(NT):
                    kb.op("vector", lambda e, tt=tt: e.tensor_scalar(out=x1[:, tt, :], in0=x1[:, tt, :], scalar1=rstd[:, tt:tt + 1], scalar2=None, op0=ALU.mult),
                          waits=[("act", ln1_act)], inc="dve")
                x1_dve = kb.now("dve")

                _maybe_stop("proj")
                kb.fence()
                with contextlib.ExitStack() as gt:
                    x1T = sb(gt, "x1T", [128, 8, S], BF16)
                    x1loT = sb(gt, "x1loT", [128, 8, S], BF16)
                    x1b = sb(gt, "x1b", [128, 4, DM16], BF16)
                    x1lo = sb(gt, "x1lo", [128, 2, DM], BF16)
                    gsb = sb(gt, "gsb", [128, E], F32)
                    mx = sb(gt, "mx", [128, 1], F32)
                    nmx = sb(gt, "nmx", [128, 1], F32)
                    eq = sb(gt, "eq", [128, E], F32)
                    cnt2 = sb(gt, "cnt2", [128, 1], F32)
                    gwo = sb(gt, "gwo", [128, E], F32)
                    second = sb(gt, "second", [128, 1], F32)
                    c2 = sb(gt, "c2", [128, 1], F32)
                    c2i = sb(gt, "c2i", [128, 1], F32)
                    kthv = sb(gt, "kthv", [128, 1], F32)
                    ktha = sb(gt, "ktha", [128, 1], F32)
                    kthb = sb(gt, "kthb", [128, 1], F32)
                    selt = sb(gt, "selt", [128, E], F32)
                    emt = sb(gt, "emt", [128, E], F32)
                    z8 = sb(gt, "z8", [128, 1], F32)
                    rz8 = sb(gt, "rz8", [128, 1], F32)
                    mbf = sb(gt, "mbf", [128, E], BF16)
                    utri = sb(gt, "utri", [128, 128], F32)
                    eoff = sb(gt, "eoff", [128, E], F32)
                    tokf = sb(gt, "tokf", [128, 1], F32)
                    onescol = sb(gt, "onescol", [128, 1], F32)
                    selt2 = sb(gt, "selt2", [128, E], F32)
                    possb = sb(gt, "possb", [128, E], F32)
                    carrow = sb(gt, "carrow", [1, E], F32)
                    totrow = sb(gt, "totrow", [1, E], F32)
                    slf = sb(gt, "slf", [128, E], F32)
                    slint = sb(gt, "slint", [128, 2, 2], I32)
                    tokint = sb(gt, "tokint", [128, 2, 1], I32)
                    ecolB = sb(gt, "ecolB", [128, E], F32)
                    a1sb = sb(gt, "a1sb", [128, E], F32)
                    ohsb = sb(gt, "ohsb", [128, E], F32)
                    tsel = sb(gt, "tsel", [128, E], F32)
                    e1c = sb(gt, "e1c", [128, 1], F32)
                    slout = sb(gt, "slout", [128, 2], F32)
                    mtmp = sb(gt, "mtmp", [128, E], F32)
                    tokft = sb(gt, "tokft", [128, 1], F32)
                    idfill = sb(gt, "idfill", [128, E * CAP // 128], I32)
                    selt3 = sb(gt, "selt3", [128, E], F32)
                    zrowx = sb(gt, "zrowx", [1, DM16], BF16)

                    kb.op("gpsimd", lambda e: e.iota(utri[:], pattern=[[1, 128]], base=0, channel_multiplier=-1, allow_small_or_imprecise_dtypes=True), inc="gpc")
                    kb.op("gpsimd", lambda e: e.iota(eoff[:], pattern=[[CAP, E]], base=0, channel_multiplier=0, allow_small_or_imprecise_dtypes=True), inc="gpc")
                    kb.op("gpsimd", lambda e: e.iota(ecolB[:], pattern=[[1, E]], base=int(BIGF), channel_multiplier=0, allow_small_or_imprecise_dtypes=True), inc="gpc")
                    kb.op("gpsimd", lambda e: e.iota(tokf[:], pattern=[[1, 1]], base=0, channel_multiplier=1, allow_small_or_imprecise_dtypes=True), inc="gpc")
                    gidx_gpc = kb.now("gpc")
                    kb.op("vector", lambda e: e.tensor_scalar(out=utri[:], in0=utri[:], scalar1=0.5, scalar2=None, op0=ALU.is_ge),
                          waits=[("gpc", gidx_gpc)], inc="dve")
                    kb.op("vector", lambda e: e.memset(onescol[:], 1.0), inc="dve")
                    idf = kb.op("vector", lambda e: e.memset(idfill[:], S), inc="dve")
                    gcon_dve = kb.now("dve")
                    kb.op("sync", lambda e: e.dma_start(
                        out=AP(tensor=d_idx, offset=0, ap=[[E * CAP // 128, 128], [1, E * CAP // 128]]),
                        in_=idfill[:]),
                        waits=[("dve", idf)], inc="dIDF", inck=16)
                    zr2 = kb.op("vector", lambda e: e.memset(zrowx[:], 0.0), inc="dve")
                    kb.op("sync", lambda e: e.dma_start(out=d_x1b.ap()[S:S + 1, :], in_=zrowx[:]),
                          waits=[("dve", zr2)], inc="dX1B", inck=16)

                    xb_free = [0, 0, 0, 0]
                    x1b_st = [0, 0, 0, 0]
                    x1T_ev = 0
                    x1loT_ev = 0
                    for tt in range(NT):
                        ca = kb.op("scalar", lambda e, tt=tt: e.activation(out=x1b[:, tt % 4, 0:DM], in_=x1[:, tt, :], func=AF.Copy),
                                   waits=[("dve", x1_dve), ("pe", xb_free[tt % 4]), ("dX1B", x1b_st[tt % 4])], inc="act")
                        cd = kb.op("vector", lambda e, tt=tt: e.scalar_tensor_tensor(
                            out=x1lo[:, tt % 2], in0=x1b[:, tt % 4, 0:DM], scalar=-1.0, in1=x1[:, tt, :], op0=ALU.mult, op1=ALU.add),
                            waits=[("act", ca)], inc="dve")
                        for c in range(8):
                            kb.op("tensor", lambda e, tt=tt, c=c: e.transpose(P6[:, 0, c], x1b[:, tt % 4, c * 128:(c + 1) * 128], ident[:]),
                                  waits=[("act", ca), ("pe", setup_bar), ("dve", x1T_ev)],
                                  inc="pe" if c == 7 else None)
                        x1T_ev = kb.op("vector", lambda e, tt=tt: e.tensor_copy(
                            out=AP(tensor=x1T, offset=tt * 128, ap=[[8 * S, 128], [S, 8], [1, 128]]),
                            in_=P6[:, 0]),
                            waits=[("pe", kb.now("pe"))], inc="dve")
                        for c in range(8):
                            kb.op("tensor", lambda e, tt=tt, c=c: e.transpose(P6[:, 1, c], x1lo[:, tt % 2, c * 128:(c + 1) * 128], ident[:]),
                                  waits=[("dve", cd), ("dve", x1loT_ev)],
                                  inc="pe" if c == 7 else None)
                        x1loT_ev = kb.op("vector", lambda e, tt=tt: e.tensor_copy(
                            out=AP(tensor=x1loT, offset=tt * 128, ap=[[8 * S, 128], [S, 8], [1, 128]]),
                            in_=P6[:, 1]),
                            waits=[("pe", kb.now("pe"))], inc="dve")
                        xb_free[tt % 4] = kb.now("pe")
                        tp_dve = kb.now("dve")
                        for d in range(8):
                            kb.op("tensor", lambda e, tt=tt, d=d: e.matmul(
                                P3[:, 0:E], x1T[:, d, tt * 128:(tt + 1) * 128], wghi[:, d, :],
                                start=(d == 0), stop=False),
                                waits=[("dve", tp_dve)])
                            kb.op("tensor", lambda e, tt=tt, d=d: e.matmul(
                                P3[:, 0:E], x1T[:, d, tt * 128:(tt + 1) * 128], wglo[:, d, :],
                                start=False, stop=False))
                            kb.op("tensor", lambda e, tt=tt, d=d: e.matmul(
                                P3[:, 0:E], x1loT[:, d, tt * 128:(tt + 1) * 128], wghi[:, d, :],
                                start=False, stop=(d == 7)),
                                inc="pe" if d == 7 else None)
                        g_pe = kb.now("pe")
                        kb.op("vector", lambda e: e.tensor_copy(out=gsb[:], in_=P3[:, 0:E]), waits=[("pe", g_pe)], inc="dve")
                        kb.op("vector", lambda e: e.reduce_max(out=mx[:], in_=gsb[:], axis=AX.X), inc="dve")
                        kb.op("vector", lambda e: e.tensor_scalar(out=nmx[:], in0=mx[:], scalar1=-1.0, scalar2=None, op0=ALU.mult), inc="dve")
                        kb.op("vector", lambda e: e.tensor_scalar(out=eq[:], in0=gsb[:], scalar1=mx[:], scalar2=None, op0=ALU.is_equal), inc="dve")
                        kb.op("vector", lambda e: e.reduce_sum(out=cnt2[:], in_=eq[:], axis=AX.X), inc="dve")
                        kb.op("vector", lambda e: e.scalar_tensor_tensor(out=gwo[:], in0=eq[:], scalar=-1e9, in1=gsb[:], op0=ALU.mult, op1=ALU.add), inc="dve")
                        kb.op("vector", lambda e: e.reduce_max(out=second[:], in_=gwo[:], axis=AX.X), inc="dve")
                        kb.op("vector", lambda e: e.tensor_scalar(out=c2[:], in0=cnt2[:], scalar1=1.5, scalar2=None, op0=ALU.is_ge), inc="dve")
                        kb.op("vector", lambda e: e.tensor_scalar(out=c2i[:], in0=c2[:], scalar1=-1.0, scalar2=1.0, op0=ALU.mult, op1=ALU.add), inc="dve")
                        kb.op("vector", lambda e: e.tensor_mul(out=ktha[:], in0=c2[:], in1=mx[:]), inc="dve")
                        kb.op("vector", lambda e: e.tensor_mul(out=kthb[:], in0=c2i[:], in1=second[:]), inc="dve")
                        kb.op("vector", lambda e: e.tensor_add(out=kthv[:], in0=ktha[:], in1=kthb[:]), inc="dve")
                        kb.op("vector", lambda e: e.tensor_scalar(out=selt[:], in0=gsb[:], scalar1=kthv[:], scalar2=None, op0=ALU.is_ge), inc="dve")
                        ae = kb.op("scalar", lambda e: e.activation(out=emt[:], in_=gsb[:], func=AF.Exp, bias=nmx[:]),
                                   waits=[("dve", kb.now("dve"))], inc="act")
                        kb.op("vector", lambda e: e.tensor_mul(out=emt[:], in0=emt[:], in1=selt[:]), waits=[("act", ae)], inc="dve")
                        kb.op("vector", lambda e: e.reduce_sum(out=z8[:], in_=emt[:], axis=AX.X), inc="dve")
                        kb.op("vector", lambda e: e.reciprocal(out=rz8[:], in_=z8[:]), inc="dve")
                        m_dve = kb.op("vector", lambda e, tt=tt: e.tensor_scalar(out=m_all[:, tt, :], in0=emt[:], scalar1=rz8[:], scalar2=1.0 / E, op0=ALU.mult, op1=ALU.mult), inc="dve")
                        kb.op("vector", lambda e, tt=tt: e.tensor_copy(out=x1b[:, tt % 4, DM:DM + 8], in_=m_all[:, tt, :]), inc="dve")
                        kb.op("vector", lambda e, tt=tt: e.tensor_copy(out=mtmp[:], in_=x1b[:, tt % 4, DM:DM + 8]), inc="dve")
                        mlo_dve = kb.op("vector", lambda e, tt=tt: e.tensor_sub(out=x1b[:, tt % 4, DM + 8:DM + 16], in0=m_all[:, tt, :], in1=mtmp[:]), inc="dve")
                        x1b_st[tt % 4] = kb.op("sync", lambda e, tt=tt: e.dma_start(out=d_x1b.ap()[tt * 128:(tt + 1) * 128, :], in_=x1b[:, tt % 4]),
                              waits=[("act", ca), ("dve", mlo_dve), ("pe", kb.now("pe"))], inc="dX1B", inck=16)
                        kb.op("vector", lambda e, tt=tt: e.tensor_scalar(out=selt2[:], in0=m_all[:, tt, :], scalar1=0.0, scalar2=None, op0=ALU.is_gt),
                              waits=[("dve", gcon_dve)], inc="dve")
                        sel_dve = kb.now("dve")
                        if tt == 0:
                            cr = kb.op("vector", lambda e: e.memset(carrow[:], 0.0), inc="dve")
                            tot_ev_prev = 0
                        kb.op("tensor", lambda e: e.matmul(P3[:, 0:E], utri[:], selt2[:], start=True, stop=False),
                              waits=[("dve", sel_dve), ("pe", kb.now("pe"))])
                        pos_pe = kb.op("tensor", lambda e: e.matmul(P3[:, 0:E], ones1f[:], carrow[:], start=False, stop=True),
                                       waits=[("dve", cr)], inc="pe")
                        tot_pe = kb.op("tensor", lambda e: e.matmul(P4t[0:1, 512:512 + E], onescol[:], selt2[:],
                                                                    start=True, stop=True),
                                       waits=[("dve", max(cr, sel_dve, tot_ev_prev))], inc="pe")
                        tot_ev_prev = kb.op("vector", lambda e: e.tensor_copy(out=totrow[:], in_=P4t[0:1, 512:512 + E]),
                                            waits=[("pe", tot_pe)], inc="dve")
                        cr = kb.op("vector", lambda e: e.tensor_add(out=carrow[:], in0=carrow[:], in1=totrow[:]),
                                   waits=[("pe", pos_pe)], inc="dve")
                        pv2 = kb.op("vector", lambda e: e.tensor_copy(out=possb[:], in_=P3[:, 0:E]),
                                    waits=[("pe", pos_pe)], inc="dve")
                        kb.op("vector", lambda e: e.tensor_add(out=slf[:], in0=possb[:], in1=eoff[:]), inc="dve")
                        kb.op("vector", lambda e: e.tensor_scalar(out=slf[:], in0=slf[:], scalar1=BIGF, scalar2=None, op0=ALU.add), inc="dve")
                        kb.op("vector", lambda e: e.scalar_tensor_tensor(out=slf[:], in0=selt2[:], scalar=-BIGF, in1=slf[:], op0=ALU.mult, op1=ALU.add), inc="dve")
                        kb.op("vector", lambda e: e.tensor_scalar(out=selt3[:], in0=possb[:], scalar1=float(CAP) - 0.5, scalar2=BIGF, op0=ALU.is_ge, op1=ALU.mult), inc="dve")
                        kb.op("vector", lambda e: e.tensor_add(out=slf[:], in0=slf[:], in1=selt3[:]), inc="dve")
                        kb.op("vector", lambda e: e.scalar_tensor_tensor(out=a1sb[:], in0=selt2[:], scalar=-BIGF, in1=ecolB[:], op0=ALU.mult, op1=ALU.add), inc="dve")
                        kb.op("vector", lambda e: e.tensor_reduce(out=e1c[:], in_=a1sb[:], axis=AX.X, op=ALU.min), inc="dve")
                        kb.op("vector", lambda e: e.tensor_scalar(out=ohsb[:], in0=a1sb[:], scalar1=e1c[:], scalar2=None, op0=ALU.is_equal), inc="dve")
                        kb.op("vector", lambda e: e.tensor_mul(out=tsel[:], in0=slf[:], in1=ohsb[:]), inc="dve")
                        kb.op("vector", lambda e: e.reduce_sum(out=slout[:, 0:1], in_=tsel[:], axis=AX.X), inc="dve")
                        kb.op("vector", lambda e: e.scalar_tensor_tensor(out=a1sb[:], in0=ohsb[:], scalar=BIGF, in1=a1sb[:], op0=ALU.mult, op1=ALU.add), inc="dve")
                        kb.op("vector", lambda e: e.tensor_reduce(out=e1c[:], in_=a1sb[:], axis=AX.X, op=ALU.min), inc="dve")
                        kb.op("vector", lambda e: e.tensor_scalar(out=ohsb[:], in0=a1sb[:], scalar1=e1c[:], scalar2=None, op0=ALU.is_equal), inc="dve")
                        kb.op("vector", lambda e: e.tensor_mul(out=tsel[:], in0=slf[:], in1=ohsb[:]), inc="dve")
                        kb.op("vector", lambda e: e.reduce_sum(out=slout[:, 1:2], in_=tsel[:], axis=AX.X), inc="dve")
                        kb.op("vector", lambda e, tt=tt: e.tensor_scalar(out=slout[:, 0:1], in0=slout[:, 0:1], scalar1=float(E * CAP + tt * 2), scalar2=None, op0=ALU.min), inc="dve")
                        kb.op("vector", lambda e, tt=tt: e.tensor_scalar(out=slout[:, 1:2], in0=slout[:, 1:2], scalar1=float(E * CAP + tt * 2 + 1), scalar2=None, op0=ALU.min), inc="dve")
                        kb.op("vector", lambda e, tt=tt: e.tensor_copy(out=slint[:, tt % 2], in_=slout[:]),
                              waits=([("dIX%d" % (tt % 2), (tt // 2) * 2 * 16)] if tt >= 2 else []), inc="dve")
                        kb.op("vector", lambda e, tt=tt: e.tensor_scalar(out=tokft[:], in0=tokf[:], scalar1=float(tt * 128), scalar2=None, op0=ALU.add), inc="dve")
                        ti_dve = kb.op("vector", lambda e, tt=tt: e.tensor_copy(out=tokint[:, tt % 2], in_=tokft[:]), inc="dve")
                        for e_x in range(2):
                            kb.op("gpsimd", lambda e, e_x=e_x, tt=tt: e.indirect_dma_start(
                                out=d_idx.ap(),
                                out_offset=IndirectOffsetOnAxis(slint[:, tt % 2, e_x:e_x + 1], 0),
                                in_=tokint[:, tt % 2], in_offset=None),
                                waits=[("dve", ti_dve), ("dIDF", 16)], inc="dIX%d" % (tt % 2), inck=16, unordered=True)
                        kb.op("vector", lambda e, tt=tt: e.tensor_copy(out=mbf[:], in_=m_all[:, tt, :]), inc="dve")
                        mb_dve = kb.now("dve")
                        kb.op("tensor", lambda e, tt=tt: e.transpose(P6[0:8, 0, 0], mbf[:], ident[:]),
                              waits=[("dve", mb_dve)], inc="pe")
                        kb.op("vector", lambda e, tt=tt: e.tensor_copy(out=mT8[:, tt * 128:(tt + 1) * 128], in_=P6[0:8, 0, 0]),
                              waits=[("pe", kb.now("pe"))], inc="dve")
                        x1T_ev = kb.now("dve")
                    mask_dve = kb.now("dve")

                    b2ev = 0
                    for tt in range(NT):
                        for dd in range(2):
                            bm = kb.op("tensor", lambda e, tt=tt, dd=dd: e.matmul(
                                P4, mT8[:, tt * 128:(tt + 1) * 128], b2sb[:, dd * 512:(dd + 1) * 512],
                                start=True, stop=True),
                                waits=[("dve", max(mask_dve, b2ev)), ("dB2", b2_gld)], inc="pe")
                            b2ev = kb.op("vector", lambda e, tt=tt, dd=dd: e.tensor_add(
                                out=x1[:, tt, dd * 512:(dd + 1) * 512], in0=x1[:, tt, dd * 512:(dd + 1) * 512], in1=P4),
                                waits=[("pe", bm)], inc="dve")
                    b2_dve = kb.now("dve")

                    yzm = kb.op("vector", lambda e: e.memset(episb[:], 0.0), inc="dve")
                    w_init = kb.now("dve")
                    for tt in range(NT):
                        kb.op("sync", lambda e, tt=tt: e.dma_start(out=d_y.ap()[tt * 128:(tt + 1) * 128, :], in_=x1[:, tt, :]),
                              waits=[("dve", w_init)], inc="dYZ", inck=16, unordered=True)
                    kb.op("sync", lambda e: e.dma_start(out=d_y.ap()[S:S + 128, :], in_=episb[:]),
                          waits=[("dve", yzm)], inc="dYZ", inck=16, unordered=True)

                    for e_i2 in range(E):
                        for sbk in range(8):
                            kb.op("sync", lambda e, e_i2=e_i2, sbk=sbk: e.dma_start(
                                out=idxall[:, e_i2, sbk:sbk + 1],
                                in_=AP(tensor=d_idx, offset=e_i2 * CAP + sbk * 128, ap=[[1, 128], [1, 1]])),
                                waits=[("dIX0", (NT // 2) * 2 * 16), ("dIX1", (NT // 2) * 2 * 16)], inc="dIL", inck=16, unordered=True)

                _maybe_stop("gate")
                kb.fence()
                x1s.close()
                fx = es.enter_context(contextlib.ExitStack())
                hT = sb(fx, "hT", [128, 32, CAP], BF16)
                w1sb = sb(fx, "w1sb", [128, 4, 8, 256], BF16)
                w2sb = sb(fx, "w2sb", [128, 4, 8, 512], BF16)
                xg = sb(fx, "xg", [128, 8, DM16], BF16)
                xgT = sb(fx, "xgT", [128, 8, CAP], BF16)
                mgcol = sb(fx, "mgcol", [128, 2, 8], F32)
                idxsc = sb(fx, "idxsc", [128, 16], I32)
                idxeq = sb(fx, "idxeq", [128, 16], I32)
                ctrash = sb(fx, "ctrash", [128, 16], I32)
                ysc = sb(fx, "ysc", [128, 4, 512], F32)
                psY = [P1[:, 0:512], P1[:, 512:1024], P2[:, 0:512], P2[:, 512:1024],
                       P3[:, 0:512], P3[:, 512:1024], P4t[:, 0:512], P4t[:, 512:1024]]

                idx_done = (NT // 2) * 2 * 16
                x1b_done = kb.now("dX1B")
                yz_done = kb.now("dYZ")

                il_total = kb.now("dIL")

                w1_free = [0, 0, 0, 0]
                w2_free = [0, 0, 0, 0]
                psh_ev = [("dve", 0), ("dve", 0)]
                y_ev = 0
                yev_bank = [0, 0]
                yev_first4 = 0
                mg_war = [0, 0]
                relu_alt = 0
                w1_last_pe = 0
                tp_done_prev = 0
                gx_total = [0] * E
                ys_hist = []
                ys_prev_expert = {}

                def issue_xgathers(e_i, war_pe, war_dve):
                    for sbk in range(8):
                        kb.op("gpsimd", lambda e, e_i=e_i, sbk=sbk: e.indirect_dma_start(
                            out=xg[:, sbk], out_offset=None,
                            in_=d_x1b.ap(), in_offset=IndirectOffsetOnAxis(idxall[:, e_i, sbk:sbk + 1], 0)),
                            waits=[("dIL", il_total), ("dX1B", x1b_done), ("pe", war_pe), ("dve", war_dve)],
                            inc="dGX%d" % (e_i % 2), inck=16, unordered=True)
                    return kb.now("dGX%d" % (e_i % 2))

                kb.op("gpsimd", lambda e: e.iota(ctrash[:], pattern=[[1, 16]], base=0, channel_multiplier=0), inc="gpc")
                ct_gpc = kb.now("gpc")
                gx_total[0] = issue_xgathers(0, 0, 0)
                for e_i in range(E):
                    ep = e_i % 2
                    kb.op("vector", lambda e, e_i=e_i: e.tensor_copy(out=idxsc[:, 0:8], in_=idxall[:, e_i, :]),
                          waits=[("gpc", ct_gpc), ("dIL", il_total)] + [(sn, cv) for sn, cv in ys_prev_expert.items()], inc="dve")
                    kb.op("vector", lambda e, e_i=e_i: e.tensor_copy(out=idxsc[:, 8:16], in_=idxall[:, e_i, :]), inc="dve")
                    kb.op("vector", lambda e, e_i=e_i: e.tensor_scalar(out=idxeq[:, 0:8], in0=idxall[:, e_i, :], scalar1=S, scalar2=None, op0=ALU.is_equal),
                          inc="dve")
                    kb.op("vector", lambda e: e.tensor_copy(out=idxeq[:, 8:16], in_=idxeq[:, 0:8]), inc="dve")
                    kb.op("vector", lambda e: e.tensor_mul(out=idxeq[:], in0=idxeq[:], in1=ctrash[:]), inc="dve")
                    idxsc_dve = kb.op("vector", lambda e: e.tensor_add(out=idxsc[:], in0=idxsc[:], in1=idxeq[:]), inc="dve")
                    xgt_evs = [0, 0]
                    for sbk in range(8):
                        for c in range(8):
                            kb.op("tensor", lambda e, sbk=sbk, c=c, e_i=e_i: e.transpose(P6[:, sbk % 2, c], xg[:, sbk, c * 128:(c + 1) * 128], ident[:]),
                                  waits=[("dGX%d" % (e_i % 2), gx_total[e_i]), ("dve", xgt_evs[sbk % 2]), ("dve", yev_bank[sbk % 2])],
                                  inc="pe" if c == 7 else None)
                        xgt_evs[sbk % 2] = kb.op("vector", lambda e, sbk=sbk: e.tensor_copy(
                            out=AP(tensor=xgT, offset=sbk * 128, ap=[[8 * CAP, 128], [CAP, 8], [1, 128]]),
                            in_=P6[:, sbk % 2]),
                            waits=[("pe", kb.now("pe")), ("pe", w1_last_pe)], inc="dve")
                    for sbk in range(8):
                        kb.op("vector", lambda e, e_i=e_i, sbk=sbk, ep=ep: e.tensor_tensor(
                            out=mgcol[:, ep, sbk:sbk + 1], in0=xg[:, sbk, DM + e_i:DM + e_i + 1],
                            in1=xg[:, sbk, DM + 8 + e_i:DM + 9 + e_i], op=ALU.add),
                            waits=[("dGX%d" % (e_i % 2), gx_total[e_i])], inc="dve")
                    xgt_dve = kb.now("dve")
                    tp_done = kb.now("pe")
                    if e_i + 1 < E:
                        gx_total[e_i + 1] = issue_xgathers(e_i + 1, tp_done, xgt_dve)
                    for jg in range(16):
                        w1_ld = kb.op("sync", lambda e, e_i=e_i, jg=jg: e.dma_start(
                            out=w1sb[:, jg % 4],
                            in_=AP(tensor=d_w1b, offset=e_i * DM * HF + jg * 256,
                                   ap=[[HF, 128], [128 * HF, 8], [1, 256]])),
                            waits=[("pe", w1_free[jg % 4]), ("dWCo", (PC_TOTAL // 2) * 16)],
                            inc="dW1", inck=16)
                        for js in range(2):
                            jc = jg * 2 + js
                            pp = jc % 2
                            psH = (P1, P2)[pp]
                            for th in range(2):
                                for d in range(8):
                                    kb.op("tensor", lambda e, jg=jg, js=js, d=d, th=th, psH=psH: e.matmul(
                                        psH[:, th * 512:(th + 1) * 512],
                                        w1sb[:, jg % 4, d, js * 128:(js + 1) * 128],
                                        xgT[:, d, th * 512:(th + 1) * 512],
                                        start=(d == 0), stop=(d == 7)),
                                        waits=[("dW1", w1_ld), ("dve", max(xgt_dve, yev_first4)), psh_ev[pp]],
                                        inc="pe" if (d == 7 and th == 1) else None)
                            h_pe = kb.now("pe")
                            if relu_alt == 0:
                                m = kb.op("scalar", lambda e, e_i=e_i, jc=jc, psH=psH: e.activation(
                                    out=hT[:, jc, :], in_=psH[:], func=AF.Relu, bias=b1sb[:, e_i, jc:jc + 1]),
                                    waits=[("pe", h_pe)], inc="act")
                                psh_ev[pp] = ("act", m)
                            else:
                                m = kb.op("vector", lambda e, e_i=e_i, jc=jc, psH=psH: e.tensor_scalar(
                                    out=hT[:, jc, :], in0=psH[:],
                                    scalar1=b1sb[:, e_i, jc:jc + 1], scalar2=0.0,
                                    op0=ALU.add, op1=ALU.max),
                                    waits=[("pe", h_pe)], inc="dve")
                                psh_ev[pp] = ("dve", m)
                            relu_alt ^= 1
                        w1_free[jg % 4] = kb.now("pe")
                    w1_last_pe = kb.now("pe")
                    h_act = kb.now("act")
                    h_dve = kb.now("dve")
                    def emit_w2_load(k2, e_i=e_i):
                        ddh, jcg = divmod(k2, 4)
                        return kb.op("scalar", lambda e, e_i=e_i, jcg=jcg, ddh=ddh, k2=k2: e.dma_start(
                            out=w2sb[:, k2 % 4],
                            in_=AP(tensor=d_w2b, offset=e_i * HF * DM + jcg * 8 * 128 * DM + ddh * 512,
                                   ap=[[DM, 128], [128 * DM, 8], [1, 512]])),
                            waits=[("pe", w2_free[k2 % 4]), ("dWCo", PC_TOTAL * 16)],
                            inc="dW2", inck=16)
                    w2_marks = {}
                    for k2 in range(3):
                        w2_marks[k2] = emit_w2_load(k2)
                    for ddh in range(2):
                        for jcg in range(4):
                            k2 = ddh * 4 + jcg
                            if k2 + 3 < 8:
                                w2_marks[k2 + 3] = emit_w2_load(k2 + 3)
                            w2_ld = w2_marks[k2]
                            for ji in range(8):
                                jc = jcg * 8 + ji
                                for ts in range(8):
                                    kb.op("tensor", lambda e, jc=jc, ji=ji, k2=k2, ts=ts: e.matmul(
                                        psY[ts], hT[:, jc, ts * 128:(ts + 1) * 128],
                                        w2sb[:, k2 % 4, ji, :],
                                        start=(jc == 0), stop=(jc == 31)),
                                        waits=[("dW2", w2_ld), ("act", h_act), ("dve", h_dve)] +
                                              ([("dve", y_ev)] if jc == 0 else []),
                                        inc="pe" if ts == 7 else None)
                            w2_free[k2 % 4] = kb.now("pe")
                        y_pe = kb.now("pe")
                        for ts in range(8):
                            k = len(ys_hist)
                            war = ys_hist[k - 4][1] if k >= 4 else 0
                            war_sem = ys_hist[k - 4][0] if k >= 4 else "dY0"
                            y_ev = kb.op("vector", lambda e, ts=ts, ep=ep: e.tensor_scalar(
                                out=ysc[:, ts % 4, :], in0=psY[ts],
                                scalar1=mgcol[:, ep, ts:ts + 1], scalar2=None, op0=ALU.mult),
                                waits=[("pe", y_pe), (war_sem, war)], inc="dve")
                            sname = "dY%d" % ts
                            jcol = ddh * 8 + ts
                            cnt = kb.op("gpsimd", lambda e, ts=ts, ddh=ddh, ep=ep, jcol=jcol: e.indirect_dma_start(
                                out=d_y.ap(), out_offset=IndirectOffsetOnAxis(idxsc[:, jcol:jcol + 1], 0),
                                in_=ysc[:, ts % 4, :], in_offset=None,
                                element_offset=ddh * 512,
                                compute_op=ALU.add),
                                waits=[("dve", y_ev), ("dYZ", yz_done)] +
                                      [(sn, cv) for sn, cv in ys_prev_expert.items()],
                                inc=sname, inck=16, unordered=True)
                            ys_hist.append((sname, cnt))
                            if ddh == 1 and ts == 1:
                                yev_bank = [None, None]
                                yev_bank[0] = y_ev - 1
                                yev_bank[1] = y_ev
                            if ddh == 1 and ts == 3:
                                yev_first4 = y_ev
                    mg_war[ep] = y_ev
                    tp_done_prev = tp_done
                    ys_prev_expert = {("dY%d" % t): kb.now("dY%d" % t) for t in range(8)}
                ffn_dve = kb.now("dve")

                fx.close()
                kb.fence()
                outsb = sb(pr, "outsb", [128, 2, DM], F32)
                yld = sb(pr, "yld", [128, 2, DM], F32)
                meanall = sb(pr, "meanall", [128, NT], F32)
                msq = sb(pr, "msq", [128, NT], F32)
                ys_totals = [("dY%d" % t, kb.now("dY%d" % t)) for t in range(8)]
                war_d = [0, 0]
                war_a = [0, 0]
                for tt in range(NT):
                    ly = kb.op("sync", lambda e, tt=tt: e.dma_start(out=yld[:, tt % 2], in_=d_y.ap()[tt * 128:(tt + 1) * 128, :]),
                               waits=ys_totals + [("dve", war_d[tt % 2]), ("act", war_a[tt % 2])], inc="dYL", inck=16)
                    war_d[tt % 2] = kb.op("vector", lambda e, tt=tt: e.reduce_sum(out=meanall[:, tt:tt + 1], in_=yld[:, tt % 2], axis=AX.X),
                                          waits=[("dYL", ly)], inc="dve")
                    war_a[tt % 2] = kb.op("scalar", lambda e, tt=tt: e.activation(out=sqt[:], in_=yld[:, tt % 2], func=AF.Square, accum_out=varall[:, tt:tt + 1]),
                          waits=[("dYL", ly), ("act", kb.now("act"))] + ([("dYZ", (NT + 1) * 16)] if tt == 0 else []), inc="act")
                kb.op("vector", lambda e: e.tensor_scalar(out=meanall[:], in0=meanall[:], scalar1=1.0 / DM, scalar2=None, op0=ALU.mult), inc="dve")
                kb.op("vector", lambda e: e.tensor_mul(out=msq[:], in0=meanall[:], in1=meanall[:]), inc="dve")
                kb.op("vector", lambda e: e.tensor_scalar(out=varall[:], in0=varall[:], scalar1=1.0 / DM, scalar2=None, op0=ALU.mult),
                      waits=[("act", kb.now("act"))], inc="dve")
                kb.op("vector", lambda e: e.tensor_sub(out=varall[:], in0=varall[:], in1=msq[:]), inc="dve")
                a2 = kb.op("scalar", lambda e: e.activation(out=rstd[:], in_=varall[:], func=AF.Ln, bias=1e-5),
                           waits=[("dve", kb.now("dve"))], inc="act")
                kb.op("scalar", lambda e: e.activation(out=rstd[:], in_=rstd[:], func=AF.Exp, scale=-0.5),
                      waits=[("act", a2)], inc="act")
                ln2_act = kb.now("act")
                for tt in range(NT):
                    ly = kb.op("sync", lambda e, tt=tt: e.dma_start(out=yld[:, tt % 2], in_=d_y.ap()[tt * 128:(tt + 1) * 128, :]),
                               waits=[("act", ln2_act), ("dve", war_d[tt % 2])], inc="dYL", inck=16)
                    od = kb.op("vector", lambda e, tt=tt: e.tensor_scalar(out=outsb[:, tt % 2], in0=yld[:, tt % 2],
                                                                          scalar1=meanall[:, tt:tt + 1], scalar2=rstd[:, tt:tt + 1],
                                                                          op0=ALU.subtract, op1=ALU.mult),
                               waits=[("dYL", ly), ("out", max(0, (tt - 1) * 16))], inc="dve")
                    war_d[tt % 2] = od
                    kb.op("sync", lambda e, tt=tt: e.dma_start(out=d_out.ap()[tt * 128:(tt + 1) * 128, :], in_=outsb[:, tt % 2]),
                          waits=[("dve", od)], inc="out", inck=16)
                kb.wait("sync", ("out", NT * 16))


        except _Stop:
            pass

        blk = es.enter_context(nc.Block())
        kb.emit(blk)

    return nc


_NC_CACHE = None


def kernel(L0_attn_x, L0_attn_qkv_w, L0_attn_out_w, L0_attn_bias_p, L0_attn_bias_a,
           L0_attn_rope_freqs, L0_ffn_W_gate, L0_ffn_W1, L0_ffn_b1, L0_ffn_W2, L0_ffn_b2):
    global _NC_CACHE
    if _NC_CACHE is None:
        _NC_CACHE = build_program()
    nc = _NC_CACHE
    weights = {
        "qkvw": np.ascontiguousarray(np.asarray(L0_attn_qkv_w, dtype=np.float32)),
        "outw": np.ascontiguousarray(np.asarray(L0_attn_out_w, dtype=np.float32)),
        "bp": np.ascontiguousarray(np.asarray(L0_attn_bias_p, dtype=np.float32).reshape(H)),
        "ba": np.ascontiguousarray(np.asarray(L0_attn_bias_a, dtype=np.float32).reshape(H)),
        "freqs": np.ascontiguousarray(np.asarray(L0_attn_rope_freqs, dtype=np.float32)),
        "wg": np.ascontiguousarray(np.asarray(L0_ffn_W_gate, dtype=np.float32)),
        "w1": np.ascontiguousarray(np.asarray(L0_ffn_W1, dtype=np.float32)),
        "b1": np.ascontiguousarray(np.asarray(L0_ffn_b1, dtype=np.float32)),
        "w2": np.ascontiguousarray(np.asarray(L0_ffn_W2, dtype=np.float32)),
        "b2": np.ascontiguousarray(np.asarray(L0_ffn_b2, dtype=np.float32)),
    }
    x = np.asarray(L0_attn_x, dtype=np.float32)
    in_maps = []
    for i in range(8):
        m = {"x": np.ascontiguousarray(x[i])}
        m.update(weights)
        in_maps.append(m)
    res = run_bass_kernel_spmd(nc, in_maps, core_ids=list(range(8)))
    return np.stack([np.asarray(r["out"], dtype=np.float32) for r in res.results], axis=0)


# version 13
# speedup vs baseline: 1.0448x; 1.0123x over previous
import contextlib
import numpy as np
import concourse.bass as bass
import concourse.mybir as mybir
from concourse.ap import AP
from concourse.bass import IndirectOffsetOnAxis
from concourse.bass_utils import run_bass_kernel_spmd

F32 = mybir.dt.float32
BF16 = mybir.dt.bfloat16
I32 = mybir.dt.int32
AF = mybir.ActivationFunctionType
ALU = mybir.AluOpType
AX = mybir.AxisListType

S = 2048
DM = 1024
H = 16
E = 8
HF = 4096
NT = S // 128
SCALE = 0.125
CAP = 1024
BIGF = 1.0e6
TWO_PI = float(2.0 * np.pi)
HALF_PI = float(0.5 * np.pi)

ENGINES = ("tensor", "vector", "scalar", "gpsimd", "sync")


class KB:

    def __init__(self, nc, es):
        self.nc = nc
        self.streams = {e: [] for e in ENGINES}
        self.sems = {}
        self.cnt = {}
        self.last_wait = {e: {} for e in ENGINES}
        self._es = es

    def sem(self, name):
        self.sems[name] = self._es.enter_context(self.nc.semaphore(name))
        self.cnt[name] = 0

    def now(self, sem):
        return self.cnt[sem]

    AUTO = {"vector": "dve", "scalar": "act", "gpsimd": "gpc"}

    def op(self, eng, build, waits=(), inc=None, inck=1, unordered=False):
        if getattr(self, "stopped", False):
            return self.cnt.get(inc, 0)
        waits = list(waits)
        auto = self.AUTO.get(eng)
        if build is not None and auto is not None:
            waits.append((auto, self.cnt[auto]))
            if inc is None:
                inc = auto
                inck = 1
        if inc is not None and inck == 16 and not unordered:
            waits.append((inc, self.cnt[inc]))
        wl = []
        for s_, c in waits:
            if c > 0 and c > self.last_wait[eng].get(s_, 0):
                wl.append((s_, c))
                self.last_wait[eng][s_] = c
        if inc is not None:
            self.cnt[inc] += inck
        self.streams[eng].append((wl, build, inc, inck))
        return self.cnt[inc] if inc is not None else None

    def wait(self, eng, *waits):
        self.op(eng, None, waits=waits)

    def fence(self, engs=ENGINES):
        snap = [(s_, c) for s_, c in self.cnt.items() if c > 0]
        for e_ in engs:
            self.op(e_, None, waits=snap)

    def emit(self, blk):
        for eng in ENGINES:
            stream = self.streams[eng]
            if not stream:
                continue

            def make(stream=stream):
                def f(e):
                    for wl, build, inc, inck in stream:
                        for s_, c in wl:
                            e.wait_ge(self.sems[s_], c)
                        ins = build(e) if build is not None else None
                        if inc is not None:
                            if ins is None:
                                ins = e.nop()
                            ins.then_inc(self.sems[inc], inck)
                return f

            getattr(blk, eng)(make())


class _Stop(Exception):
    pass


def build_program(upto=None):
    nc = bass.Bass("TRN2", target_bir_lowering=False)

    d_x = nc.dram_tensor("x", [S, DM], F32, kind="ExternalInput")
    d_qkvw = nc.dram_tensor("qkvw", [3 * DM, DM], F32, kind="ExternalInput")
    d_outw = nc.dram_tensor("outw", [DM, DM], F32, kind="ExternalInput")
    d_bp = nc.dram_tensor("bp", [H], F32, kind="ExternalInput")
    d_ba = nc.dram_tensor("ba", [H], F32, kind="ExternalInput")
    d_freqs = nc.dram_tensor("freqs", [32], F32, kind="ExternalInput")
    d_wg = nc.dram_tensor("wg", [DM, E], F32, kind="ExternalInput")
    d_w1 = nc.dram_tensor("w1", [E, DM, HF], F32, kind="ExternalInput")
    d_b1 = nc.dram_tensor("b1", [E, HF], F32, kind="ExternalInput")
    d_w2 = nc.dram_tensor("w2", [E, HF, DM], F32, kind="ExternalInput")
    d_b2 = nc.dram_tensor("b2", [E, DM], F32, kind="ExternalInput")
    d_out = nc.dram_tensor("out", [S, DM], F32, kind="ExternalOutput")
    d_qkT = nc.dram_tensor("qkT_s", [DM, 2 * DM], BF16, kind="Internal")
    d_aT = nc.dram_tensor("aT_s", [DM, S], BF16, kind="Internal")
    d_band = nc.dram_tensor("band_s", [H, 128, 4096], BF16, kind="Internal")
    d_w1b = nc.dram_tensor("w1b_s", [E * DM * HF], BF16, kind="Internal")
    d_w2b = nc.dram_tensor("w2b_s", [E * HF * DM], BF16, kind="Internal")
    DM16 = DM + 16
    d_x1b = nc.dram_tensor("x1b_s", [S + 1, DM16], BF16, kind="Internal")
    d_idx = nc.dram_tensor("idx_s", [E * CAP + 128, 1], I32, kind="Internal")
    d_y = nc.dram_tensor("y_s", [S + 128, DM], F32, kind="Internal")

    es = contextlib.ExitStack()
    with es:
        kb = KB(nc, es)
        for s_ in ("pe", "act", "dve", "gpc", "out",
                   "dS", "dB2", "dXB", "dQW", "dTS", "dOW", "dBS",
                   "dWSL", "dBBF", "dSW", "dZA", "dRZ", "dAO", "dAOT", "dXR",
                   "dW1", "dW2", "dWC", "dWCo", "dID",
                   "dX1B", "dIX0", "dIX1", "dIDF", "dGX0", "dGX1", "dYZ", "dIL",
                   "dYL", "dYL1", "dYL2", "dYL3", "out1",
                   "dY0", "dY1", "dY2", "dY3", "dY4", "dY5", "dY6", "dY7"):
            kb.sem(s_)

        def sb(stack, name, shape, dtype):
            return stack.enter_context(nc.sbuf_tensor(name, shape, dtype))

        P1 = es.enter_context(nc.psum_tensor("P1", [128, 1024], F32))
        P2 = es.enter_context(nc.psum_tensor("P2", [128, 1024], F32))
        P3 = es.enter_context(nc.psum_tensor("P3", [128, 1024], F32))
        P4t = es.enter_context(nc.psum_tensor("P4t", [128, 1024], F32))
        P4 = P4t[:, 0:512]
        P6 = P1[:].bitcast(BF16).rearrange("p (a b c) -> p a b c", a=2, b=8)

        ident = sb(es, "ident", [128, 128], BF16)
        psw = sb(es, "psw", [128, 128], BF16)
        onebf = sb(es, "onebf", [1, 1], BF16)
        ones1f = sb(es, "ones1f", [1, 128], F32)
        ones16 = sb(es, "ones16", [128, NT], F32)
        CONSTS = [0.0, 1.0, -1.0, 1e-5, -0.5, 0.125, HALF_PI]
        csb = sb(es, "csb", [128, len(CONSTS)], F32)
        for ci, cv in enumerate(CONSTS):
            nc.const_aps.register(F32, cv, csb[:, ci:ci + 1])
        b1sb = sb(es, "b1sb", [128, E, 32], F32)
        b2sb = sb(es, "b2sb", [E, DM], BF16)
        wghi = sb(es, "wghi", [128, 8, E], BF16)
        wglo = sb(es, "wglo", [128, 8, E], BF16)
        varall = sb(es, "varall", [128, NT], F32)
        rstd = sb(es, "rstd", [128, NT], F32)
        meancol = sb(es, "meancol", [128, 1], F32)

        episb = sb(es, "episb", [128, DM], F32)
        sqt = episb

        def _maybe_stop(stage):
            if upto != stage:
                return
            kb.fence()
            gm = kb.op("gpsimd", lambda e: e.memset(episb[:], 0.0))
            for tt_ in range(NT):
                kb.op("sync", lambda e, tt_=tt_: e.dma_start(out=d_out.ap()[tt_ * 128:(tt_ + 1) * 128, :], in_=episb[:]),
                      waits=[("gpc", kb.now("gpc"))], inc="out", inck=16)
            kb.wait("sync", ("out", NT * 16))
            kb.stopped = True

        try:
            att_era = es.enter_context(contextlib.ExitStack())
            xT = sb(att_era, "xT", [128, 8, S], BF16)
            V = sb(att_era, "V", [128, NT, H * 65], BF16)
            C4 = sb(att_era, "C4", [128, S], F32)
            S4 = sb(att_era, "S4", [128, S], F32)
            with contextlib.ExitStack() as st1:
                posrow = sb(st1, "posrow", [1, S], F32)
                sgni = sb(st1, "sgni", [128, 1], I32)
                bandabs = sb(st1, "bandabs", [128, 4096], F32)
                qkvwb = sb(st1, "qkvwb", [128, 6, DM], BF16)
                qkvwTv = sb(st1, "qkvwTv", [128, 8, DM], BF16)
                tstage = sb(st1, "tstage", [128, 2, 8, 128], BF16)
                xb = sb(st1, "xb", [128, 2, DM], BF16)
                xbf = sb(st1, "xbf", [128, 2, DM], F32)
                wgf = sb(st1, "wgf", [128, 8, E], F32)
                bparow = sb(st1, "bparow", [1, 2, H], F32)
                paac = sb(st1, "paac", [128, H], F32)
                npac = sb(st1, "npac", [128, H], F32)

                kb.op("sync", lambda e: e.dma_start(out=bparow[:, 0], in_=AP(tensor=d_ba, offset=0, ap=[[1, 1], [1, H]])), inc="dS", inck=16)
                kb.op("sync", lambda e: e.dma_start(out=bparow[:, 1], in_=AP(tensor=d_bp, offset=0, ap=[[1, 1], [1, H]])), inc="dS", inck=16)
                for e_i in range(E):
                    for jc in range(32):
                        kb.op("sync", lambda e, e_i=e_i, jc=jc: e.dma_start(
                            out=b1sb[:, e_i, jc:jc + 1],
                            in_=AP(tensor=d_b1, offset=e_i * HF + jc * 128, ap=[[1, 128], [1, 1]])), inc="dS", inck=16, unordered=True)
                kb.op("sync", lambda e: e.dma_start(out=wgf[:], in_=AP(tensor=d_wg, offset=0, ap=[[E, 128], [128 * E, 8], [1, E]])), inc="dS", inck=16)
                sld_small = kb.now("dS")
                b2_gld = kb.op("gpsimd", lambda e: e.dma_start(out=b2sb[:], in_=d_b2.ap()), inc="dB2", inck=16)

                for ci, cv in enumerate(CONSTS):
                    kb.op("vector", lambda e, ci=ci, cv=cv: e.memset(csb[:, ci:ci + 1], cv))
                imz = kb.op("vector", lambda e: e.memset(ident[:], 0.0))
                ob = kb.op("vector", lambda e: e.memset(onebf[:], 1.0))
                ident_gpc = 0
                for ii in range(128):
                    kb.op("sync", lambda e, ii=ii: e.dma_start(
                        out=AP(tensor=ident, offset=ii * 128 + ii, ap=[[128, 1], [1, 1]]),
                        in_=onebf[:]),
                        waits=[("dve", ob)], inc="dID", inck=16, unordered=True)
                ident_dma = kb.now("dID")
                pm = kb.op("vector", lambda e: e.memset(psw[:], 0.0), inc="dve")
                for (ob, ib) in ((0, 32), (32, 0), (64, 96), (96, 64)):
                    kb.op("sync", lambda e, ob=ob, ib=ib: e.dma_start(
                        out=psw[ib:ib + 32, ob:ob + 32], in_=ident[ib:ib + 32, ib:ib + 32]),
                        waits=[("dID", ident_dma), ("dve", pm)], inc="dSW", inck=16, unordered=True)
                psw_ld = kb.now("dSW")
                kb.op("vector", lambda e: e.memset(ones1f[:], 1.0), inc="dve")
                v16 = kb.op("vector", lambda e: e.memset(ones16[:], 1.0), inc="dve")
                for vh in range(H):
                    kb.op("vector", lambda e, vh=vh: e.tensor_copy(
                        out=AP(tensor=V, offset=vh * 65 + 64, ap=[[NT * H * 65, 128], [H * 65, NT], [1, 1]]),
                        in_=ones16[:]),
                        waits=[("dve", v16)], inc="dve" if vh == H - 1 else None)
                gpc_const = kb.now("gpc")

                _maybe_stop("s1")
                _pc_state = {"i": 0}
                W1_ELEMS = E * DM * HF
                PC_TOTAL = 2 * W1_ELEMS // (128 * 4096)

                def precast_pump(n):
                    for _ in range(n):
                        i = _pc_state["i"]
                        if i >= PC_TOTAL:
                            return
                        _pc_state["i"] = i + 1
                        if i < PC_TOTAL // 2:
                            src_t, dst_t, off = d_w1, d_w1b, i * 128 * 4096
                        else:
                            src_t, dst_t, off = d_w2, d_w2b, (i - PC_TOTAL // 2) * 128 * 4096
                        kb.op("gpsimd", lambda e, src_t=src_t, dst_t=dst_t, off=off: e.dma_start(
                            out=AP(tensor=dst_t, offset=off, ap=[[4096, 128], [1, 4096]]),
                            in_=AP(tensor=src_t, offset=off, ap=[[4096, 128], [1, 4096]])),
                            inc="dWCo", inck=16)

                qkT_sld = [0, 0]
                qk_evs = [0, 0]
                for rnd in range(4):
                    gld_qkvw = kb.op("gpsimd", lambda e, rnd=rnd: e.dma_start(
                        out=qkvwb[:], in_=d_qkvw.ap()[rnd * 6 * 128:(rnd + 1) * 6 * 128, :].rearrange("(c p) d -> p c d", p=128)),
                        waits=[("pe", kb.now("pe") if rnd else 0)], inc="dQW", inck=16)
                    for cc12 in range(6):
                        cc = rnd * 6 + cc12
                        par = cc % 2
                        for dc in range(8):
                            kb.op("tensor", lambda e, cc12=cc12, dc=dc, par=par: e.transpose(P6[:, par, dc], qkvwb[:, cc12, dc * 128:(dc + 1) * 128], ident[:]),
                                  waits=[("dQW", gld_qkvw), ("dID", ident_dma), ("dve", qk_evs[par])],
                                  inc="pe" if dc == 7 else None)
                        if cc < 16:
                            qk_evs[par] = kb.op("vector", lambda e, par=par: e.tensor_copy(out=tstage[:, par], in_=P6[:, par]),
                                  waits=[("pe", kb.now("pe")), ("dTS", qkT_sld[par])], inc="dve")
                            qkT_sld[par] = kb.op("sync", lambda e, cc=cc, par=par: e.dma_start(
                                out=AP(tensor=d_qkT, offset=cc * 128, ap=[[2 * DM, 128], [128 * 2 * DM, 8], [1, 128]]),
                                in_=tstage[:, par]),
                                waits=[("dve", kb.now("dve"))], inc="dTS", inck=16)
                        else:
                            qk_evs[par] = kb.op("vector", lambda e, cc=cc, par=par: e.tensor_copy(
                                out=AP(tensor=qkvwTv, offset=(cc - 16) * 128, ap=[[8 * DM, 128], [DM, 8], [1, 128]]),
                                in_=P6[:, par]),
                                waits=[("pe", kb.now("pe"))], inc="dve")
                qkT_store_sld = kb.now("dTS")
                kb.op("gpsimd", lambda e: e.iota(posrow[:], pattern=[[1, S]], base=0, channel_multiplier=0, allow_small_or_imprecise_dtypes=True), inc="gpc")
                kb.op("gpsimd", lambda e: e.iota(sgni[:], pattern=[[1, 1]], base=0, channel_multiplier=1), inc="gpc")
                kb.op("gpsimd", lambda e: e.iota(bandabs[:], pattern=[[-1, 4096]], base=1920, channel_multiplier=1, allow_small_or_imprecise_dtypes=True), inc="gpc")
                iota_gpc = kb.now("gpc")
                precast_pump(4)
                _maybe_stop("s3")
                xT_evs = [qk_evs[0], qk_evs[1]]
                xc_dve = [0, 0]
                for tt in range(NT):
                    gld_x = kb.op("sync", lambda e, tt=tt: e.dma_start(out=xbf[:, tt % 2], in_=d_x.ap()[tt * 128:(tt + 1) * 128, :]),
                                  waits=[("dve", xc_dve[tt % 2])], inc="dXB", inck=16)
                    xc_dve[tt % 2] = kb.op("vector", lambda e, tt=tt: e.tensor_copy(out=xb[:, tt % 2], in_=xbf[:, tt % 2]),
                                           waits=[("dXB", gld_x), ("pe", kb.now("pe") if tt >= 2 else 0)], inc="dve")
                    for c in range(8):
                        kb.op("tensor", lambda e, tt=tt, c=c: e.transpose(P6[:, tt % 2, c], xb[:, tt % 2, c * 128:(c + 1) * 128], ident[:]),
                              waits=[("dve", xc_dve[tt % 2]), ("dID", ident_dma), ("dve", xT_evs[tt % 2])],
                              inc="pe" if c == 7 else None)
                    xT_evs[tt % 2] = kb.op("vector", lambda e, tt=tt: e.tensor_copy(
                        out=AP(tensor=xT, offset=tt * 128, ap=[[8 * S, 128], [S, 8], [1, 128]]),
                        in_=P6[:, tt % 2]),
                        waits=[("pe", kb.now("pe"))], inc="dve")
                xT_dve = kb.now("dve")
                _maybe_stop("s2")


                for tt in range(NT):
                    for hf in range(2):
                        for d in range(8):
                            kb.op("tensor", lambda e, tt=tt, hf=hf, d=d: e.matmul(
                                P4, xT[:, d, tt * 128:(tt + 1) * 128],
                                qkvwTv[:, d, hf * 512:(hf + 1) * 512],
                                start=(d == 0), stop=(d == 7)),
                                waits=[("dve", xT_dve if (tt == 0 and hf == 0) else kb.now("dve"))],
                                inc="pe" if d == 7 else None)
                        pv = kb.now("pe")
                        kb.op("vector", lambda e, tt=tt, hf=hf: e.tensor_copy(
                            out=AP(tensor=V, offset=tt * H * 65 + hf * 8 * 65, ap=[[NT * H * 65, 128], [65, 8], [1, 64]]),
                            in_=AP(tensor=P4t, offset=0, ap=[[1024, 128], [64, 8], [1, 64]])),
                            waits=[("pe", pv)], inc="dve")
                V_dve = kb.now("dve")
                precast_pump(8)
                _maybe_stop("s4")


                kb.op("vector", lambda e: e.tensor_copy(out=wghi[:], in_=wgf[:]), waits=[("dS", sld_small)], inc="dve")
                kb.op("vector", lambda e: e.scalar_tensor_tensor(out=wglo[:], in0=wghi[:], scalar=-1.0, in1=wgf[:], op0=ALU.mult, op1=ALU.add), inc="dve")

                with contextlib.ExitStack() as st2:
                    freq4 = sb(st2, "freq4", [128, 1], F32)
                    angs = sb(st2, "angs", [128, S], F32)
                    angred = sb(st2, "angred", [128, S], F32)
                    angi = sb(st2, "angi", [128, S], I32)
                    sgnf = sb(st2, "sgnf", [128, 1], F32)
                    sgncol = sb(st2, "sgncol", [128, 1], F32)
                    sgt = sb(st2, "sgt", [128, S], F32)

                    kb.op("sync", lambda e: e.dma_start(out=freq4[:], in_=AP(tensor=d_freqs, offset=0, ap=[[0, 4], [1, 32], [1, 1]])), inc="dS", inck=16)
                    sld_freq = kb.now("dS")
                    gpc_rope = iota_gpc
                    for qt in range(4):
                        kb.op("tensor", lambda e, qt=qt: e.matmul(P4, ones1f[:], posrow[:, qt * 512:(qt + 1) * 512], start=True, stop=True),
                              waits=[("gpc", gpc_rope), ("dve", kb.now("dve")), ("dS", sld_freq)], inc="pe")
                        kb.op("vector", lambda e, qt=qt: e.tensor_scalar(out=angs[:, qt * 512:(qt + 1) * 512], in0=P4, scalar1=freq4[:], scalar2=None, op0=ALU.mult),
                              waits=[("pe", kb.now("pe"))], inc="dve")
                    kb.op("vector", lambda e: e.tensor_scalar(out=angred[:], in0=angs[:], scalar1=1.0 / TWO_PI, scalar2=None, op0=ALU.mult), inc="dve")
                    kb.op("vector", lambda e: e.tensor_copy(out=angi[:], in_=angred[:]), inc="dve")
                    kb.op("vector", lambda e: e.tensor_copy(out=angred[:], in_=angi[:]), inc="dve")
                    kb.op("vector", lambda e: e.scalar_tensor_tensor(out=angred[:], in0=angred[:], scalar=-TWO_PI, in1=angs[:], op0=ALU.mult, op1=ALU.add), inc="dve")
                    kb.op("vector", lambda e: e.tensor_scalar(out=sgt[:], in0=angred[:], scalar1=float(np.pi), scalar2=-TWO_PI, op0=ALU.is_gt, op1=ALU.mult), inc="dve")
                    kb.op("vector", lambda e: e.tensor_add(out=angred[:], in0=angred[:], in1=sgt[:]), inc="dve")
                    kb.op("vector", lambda e: e.tensor_scalar(out=sgt[:], in0=angred[:], scalar1=float(-np.pi), scalar2=TWO_PI, op0=ALU.is_lt, op1=ALU.mult), inc="dve")
                    kb.op("vector", lambda e: e.tensor_add(out=angred[:], in0=angred[:], in1=sgt[:]), inc="dve")
                    dve_ang = kb.now("dve")
                    act_sin = kb.op("scalar", lambda e: e.activation(out=S4[:], in_=angred[:], func=AF.Sin), waits=[("dve", dve_ang)], inc="act")
                    aab = kb.op("scalar", lambda e: e.activation(out=angs[:], in_=angred[:], func=AF.Abs),
                                waits=[("act", act_sin), ("dve", dve_ang)], inc="act")
                    kb.op("scalar", lambda e: e.activation(out=C4[:], in_=angs[:], func=AF.Sin, bias=HALF_PI, scale=-1.0), waits=[("act", aab)], inc="act")
                    kb.op("vector", lambda e: e.tensor_scalar(out=sgni[:], in0=sgni[:], scalar1=32, scalar2=None, op0=ALU.bitwise_and), waits=[("gpc", gpc_rope)], inc="dve")
                    kb.op("vector", lambda e: e.tensor_copy(out=sgnf[:], in_=sgni[:]), inc="dve")
                    kb.op("vector", lambda e: e.tensor_scalar(out=sgncol[:], in0=sgnf[:], scalar1=1.0 / 16.0, scalar2=-1.0, op0=ALU.mult, op1=ALU.add), inc="dve")
                    kb.op("vector", lambda e: e.tensor_scalar(out=S4[:], in0=S4[:], scalar1=sgncol[:], scalar2=None, op0=ALU.mult),
                          waits=[("act", kb.now("act"))], inc="dve")

                    kb.op("tensor", lambda e: e.matmul(P3[:, 0:H], ones1f[:], bparow[:, 0], start=True, stop=True),
                          waits=[("dS", sld_small), ("dve", kb.now("dve"))], inc="pe")
                    kb.op("vector", lambda e: e.tensor_scalar(out=paac[:], in0=P3[:, 0:H], scalar1=0.01, scalar2=None, op0=ALU.max),
                          waits=[("pe", kb.now("pe"))], inc="dve")
                    kb.op("tensor", lambda e: e.matmul(P3[:, 0:H], ones1f[:], bparow[:, 1], start=True, stop=True),
                          waits=[("dve", kb.now("dve"))], inc="pe")
                    kb.op("vector", lambda e: e.tensor_scalar(out=npac[:], in0=P3[:, 0:H], scalar1=0.01, scalar2=-1.0, op0=ALU.max, op1=ALU.mult),
                          waits=[("pe", kb.now("pe"))], inc="dve")

                _maybe_stop("s5")
                kb.fence()
                with contextlib.ExitStack() as st3:
                    bandtmp = sb(st3, "bandtmp", [128, 2, 4096], BF16)
                    bandst = sb(st3, "bandst", [128, 2, 4096], BF16)

                    kb.op("scalar", lambda e: e.activation(out=bandabs[:], in_=bandabs[:], func=AF.Abs),
                          waits=[("gpc", iota_gpc)], inc="act")
                    babs_act = kb.now("act")
                    babs_dve = kb.now("dve")
                    bst_st = [0, 0]
                    bsc_mark = [0, 0]
                    for h in range(H):
                        a_ln = kb.op("scalar", lambda e, h=h: e.activation(out=bandtmp[:, h % 2], in_=bandabs[:], func=AF.Ln, bias=1.0, scale=paac[:, h:h + 1]),
                                     waits=[("act", babs_act), ("dve", max(babs_dve, bsc_mark[h % 2]))], inc="act")
                        d_sc = kb.op("vector", lambda e, h=h: e.tensor_scalar(out=bandst[:, h % 2], in0=bandtmp[:, h % 2], scalar1=npac[:, h:h + 1], scalar2=None, op0=ALU.mult),
                                     waits=[("act", a_ln), ("dBS", bst_st[h % 2])], inc="dve")
                        bsc_mark[h % 2] = d_sc
                        bst_st[h % 2] = kb.op("sync", lambda e, h=h: e.dma_start(out=d_band.ap()[h], in_=bandst[:, h % 2]),
                                              waits=[("dve", d_sc)], inc="dBS", inck=16)
                    band_sld = kb.now("dBS")
                setup_dve = kb.now("dve")
                setup_act = kb.now("act")
                kb.wait("tensor", ("dve", setup_dve), ("act", setup_act), ("dBS", kb.now("dBS")), ("dTS", kb.now("dTS")))
                setup_bar = kb.op("tensor", lambda e: e.nop(), inc="pe")
            _maybe_stop("setup")
            kb.fence()

            with contextlib.ExitStack() as at:
                wsl = sb(at, "wsl", [128, 4, 8, 256], BF16)
                qbf = sb(at, "qbf", [128, S], BF16)
                ctmp = sb(at, "ctmp", [128, S], F32)
                qr = sb(at, "qr", [128, S], BF16)
                kr = sb(at, "kr", [128, S], BF16)
                Pp = sb(at, "Pp", [128, 2, 1024], BF16)
                araw = sb(at, "araw", [64, 2, S], BF16)
                aout = sb(at, "aout", [64, S], BF16)
                bandbf = sb(at, "bandbf", [128, 2, 4096], BF16)
                zasm = sb(at, "zasm", [4, 2, 512], F32)
                rzasm = sb(at, "rzasm", [4, 512], F32)
                rzflat = sb(at, "rzflat", [1, 2, 4, 512], F32)
                ztmp = sb(at, "ztmp", [1, 2, 512], F32)

                psS = (P1, P2)
                psV_banks = (P3, P4t)

                exp_mark = [0, 0]
                kb_mark = [0, 0]
                zrec_mark = 0
                psv_z_p = [0, 0]
                psv_dve_p = [0, 0]
                araw_free_p = [0, 0]
                pending_tail = []
                nonlocal_state = {"zrec_mark": 0, "aout_dma": 0}
                ztail_pe_p = [0, 0]
                rz_ld_p = [0, 0]

                qkv_done = [0, 0, 0, 0]
                swp_war = [0, 0]
                qbf_war = [0, 0, 0, 0]
                for p in range(H // 2):
                    precast_pump(15)
                    war = qkv_done[p % 4]
                    w_ld = kb.op("sync", lambda e, p=p: e.dma_start(
                        out=wsl[:, p % 4, :, 0:128],
                        in_=AP(tensor=d_qkT, offset=p * 128, ap=[[2 * DM, 128], [128 * 2 * DM, 8], [1, 128]])),
                        waits=[("pe", war), ("dTS", qkT_store_sld)], inc="dWSL", inck=16)
                    w_ld = kb.op("sync", lambda e, p=p: e.dma_start(
                        out=wsl[:, p % 4, :, 128:256],
                        in_=AP(tensor=d_qkT, offset=DM + p * 128, ap=[[2 * DM, 128], [128 * 2 * DM, 8], [1, 128]])),
                        inc="dWSL", inck=16)
                    for hh in range(2):
                        h = 2 * p + hh
                        kb.op("sync", lambda e, h=h: e.dma_start(out=bandbf[:, h % 2], in_=d_band.ap()[h]),
                              waits=[("dBS", band_sld), ("pe", kb_mark[h % 2])], inc="dBBF", inck=16)
                    band_ld = kb.now("dBBF")

                    for w, dst, dsc in ((0, qr, SCALE), (1, kr, 1.0)):
                        evac_marks = {}
                        for qt in range(4):
                            g = (w * 4 + qt) % 2
                            psq = P1[:, 0:512] if g == 0 else P2[:, 0:512]
                            pssw = P1[:, 512:1024] if g == 0 else P2[:, 512:1024]
                            for d in range(8):
                                kb.op("tensor", lambda e, p=p, w=w, qt=qt, d=d, psq=psq: e.matmul(
                                    psq, wsl[:, p % 4, d, w * 128:(w + 1) * 128],
                                    xT[:, d, qt * 512:(qt + 1) * 512],
                                    start=(d == 0), stop=(d == 7)),
                                    waits=[("dWSL", w_ld), ("pe", setup_bar),
                                           ("act", max(exp_mark[0], exp_mark[1], evac_marks.get(g, 0)))],
                                    inc="pe" if d == 7 else None)
                            ev = kb.op("scalar", lambda e, qt=qt, dsc=dsc, psq=psq: e.activation(
                                out=qbf[:, qt * 512:(qt + 1) * 512], in_=psq, func=AF.Copy, scale=dsc),
                                waits=[("pe", kb.now("pe")), ("dve", qbf_war[qt])], inc="act")
                            evac_marks[g] = ev
                            sm = kb.op("tensor", lambda e, qt=qt, pssw=pssw: e.matmul(
                                pssw, psw[:], qbf[:, qt * 512:(qt + 1) * 512], start=True, stop=True),
                                waits=[("act", ev), ("dSW", psw_ld), ("dve", swp_war[g])], inc="pe")
                            qbf_war[qt] = kb.op("vector", lambda e, qt=qt: e.tensor_mul(
                                out=ctmp[:, qt * 512:(qt + 1) * 512], in0=qbf[:, qt * 512:(qt + 1) * 512],
                                in1=C4[:, qt * 512:(qt + 1) * 512]),
                                waits=[("act", ev)], inc="dve")
                            kb.op("vector", lambda e, qt=qt, dst=dst, pssw=pssw: e.tensor_mul(
                                out=dst[:, qt * 512:(qt + 1) * 512], in0=pssw,
                                in1=S4[:, qt * 512:(qt + 1) * 512]),
                                waits=[("pe", sm)], inc="dve")
                            swp_war[g] = kb.op("vector", lambda e, qt=qt, dst=dst: e.tensor_add(
                                out=dst[:, qt * 512:(qt + 1) * 512], in0=dst[:, qt * 512:(qt + 1) * 512],
                                in1=ctmp[:, qt * 512:(qt + 1) * 512]), inc="dve")
                    rope_dve = kb.now("dve")
                    qkv_done[p % 4] = kb.now("pe")

                    for hh in range(2):
                        h = 2 * p + hh
                        hb = hh * 64
                        bpar = h % 2
                        vpar = h % 2
                        PV = psV_banks[vpar]
                        for qh in range(2):
                            q0 = qh * 1024
                            evs = [0] * 16

                            def emit_attnV(kt, h=h, PV=PV, vpar=vpar):
                                for j in range(2):
                                    kb.op("tensor", lambda e, kt=kt, h=h, j=j, PV=PV: e.matmul(
                                        PV[0:65, j * 512:(j + 1) * 512], V[:, kt, h * 65:(h + 1) * 65],
                                        Pp[:, kt % 2, j * 512:(j + 1) * 512],
                                        start=(kt == 0), stop=(kt == 15)),
                                        waits=[("act", evs[kt])] +
                                              ([("dZA", psv_z_p[vpar]), ("dve", psv_dve_p[vpar])] if kt == 0 else []),
                                        inc="pe" if (j == 1 and kt == 15) else None)

                            for kt in range(16):
                                par = kt % 2
                                for j in range(2):
                                    kb.op("tensor", lambda e, hb=hb, kt=kt, q0=q0, j=j, par=par: e.matmul(
                                        psS[par][:, j * 512:(j + 1) * 512],
                                        kr[hb:hb + 64, kt * 128:(kt + 1) * 128],
                                        qr[hb:hb + 64, q0 + j * 512:q0 + (j + 1) * 512],
                                        start=True, stop=False),
                                        waits=[("dve", rope_dve), ("dBBF", band_ld),
                                               ("act", exp_mark[par])])
                                for j in range(2):
                                    u0 = q0 + j * 512 + 1920 - kt * 128
                                    r = kb.op("tensor", lambda e, par=par, bpar=bpar, u0=u0, j=j: e.matmul(
                                        psS[par][:, j * 512:(j + 1) * 512],
                                        ident[:], bandbf[:, bpar, u0:u0 + 512],
                                        start=False, stop=True),
                                        inc="pe" if j == 1 else None)
                                    if r is not None:
                                        kb_mark[bpar] = r
                                sdone = kb.now("pe")
                                if kt >= 1:
                                    emit_attnV(kt - 1)
                                if qh == 0 and kt == 8 and pending_tail:
                                    pending_tail.pop(0)()
                                exp_mark[par] = kb.op("scalar", lambda e, par=par: e.activation(
                                    out=Pp[:, par, :], in_=psS[par][:], func=AF.Exp),
                                    waits=[("pe", sdone)], inc="act")
                                evs[kt] = exp_mark[par]
                            emit_attnV(15)
                            pv_done = kb.now("pe")
                            for j in range(2):
                                kb.op("vector", lambda e, q0=q0, j=j, PV=PV, vpar=vpar: e.tensor_copy(
                                    out=araw[:, vpar, q0 + j * 512:q0 + (j + 1) * 512], in_=PV[0:64, j * 512:(j + 1) * 512]),
                                    waits=[("pe", pv_done), ("dve", araw_free_p[vpar])], inc="dve")
                                za = kb.op("vector", lambda e, j=j, PV=PV: e.tensor_copy(
                                    out=ztmp[:, j, :], in_=PV[64:65, j * 512:(j + 1) * 512]),
                                    waits=[("pe", pv_done), ("dZA", kb.now("dZA"))], inc="dve")
                                psv_z_p[vpar] = kb.op("sync", lambda e, qh=qh, j=j, vpar=vpar: e.dma_start(
                                    out=zasm[2 * qh + j:2 * qh + j + 1, vpar, :], in_=ztmp[:, j, :]),
                                    waits=[("dve", za)], inc="dZA", inck=16)
                        araw_act = kb.now("dve")
                        nonlocal_state["zrec_mark"] = kb.op("vector",
                            lambda e, vpar=vpar: e.reciprocal(out=rzasm[:], in_=zasm[:, vpar, :]),
                            waits=[("dZA", psv_z_p[vpar]), ("dRZ", kb.now("dRZ")), ("pe", ztail_pe_p[vpar])], inc="dve")
                        rz_ld_p[vpar] = kb.op("sync", lambda e, vpar=vpar: e.dma_start(
                            out=rzflat[0:1, vpar], in_=rzasm[:]),
                            waits=[("dve", nonlocal_state["zrec_mark"])], inc="dRZ", inck=16)

                        def make_tail(h=h, vpar=vpar, PV=PV, araw_act=araw_act, rz_ld=None):
                            rz_ld = rz_ld_p[vpar]

                            def tail():
                                pd_prev = araw_act
                                for qj in range(4):
                                    kb.op("tensor", lambda e, qj=qj, PV=PV, vpar=vpar: e.matmul(
                                        PV[0:64, 0:512], ones1f[:, 0:64], rzflat[0:1, vpar, qj, :], start=True, stop=True),
                                        waits=[("dRZ", rz_ld), ("dve", pd_prev)], inc="pe")
                                    pd = kb.op("vector", lambda e, qj=qj, PV=PV, vpar=vpar: e.tensor_tensor(
                                        out=aout[:, qj * 512:(qj + 1) * 512],
                                        in0=araw[:, vpar, qj * 512:(qj + 1) * 512], in1=PV[0:64, 0:512], op=ALU.mult),
                                        waits=[("pe", kb.now("pe")), ("dAO", nonlocal_state["aout_dma"])], inc="dve")
                                    psv_dve_p[vpar] = pd
                                    pd_prev = pd
                                    nonlocal_state["aout_dma"] = kb.op("sync", lambda e, h=h, qj=qj: e.dma_start(
                                        out=d_aT.ap()[h * 64:(h + 1) * 64, qj * 512:(qj + 1) * 512],
                                        in_=aout[:, qj * 512:(qj + 1) * 512]),
                                        waits=[("dve", pd)], inc="dAO", inck=16)
                                araw_free_p[vpar] = psv_dve_p[vpar]
                                ztail_pe_p[vpar] = kb.now("pe")
                            return tail

                        pending_tail.append(make_tail())
                while pending_tail:
                    pending_tail.pop(0)()
                precast_pump(10**9)
                att_sld = kb.now("dAO")
                kb.wait("tensor", ("dAO", att_sld), ("dZA", kb.now("dZA")), ("dRZ", kb.now("dRZ")), ("dSW", kb.now("dSW")), ("dve", kb.now("dve")), ("act", kb.now("act")))
                att_bar = kb.op("tensor", lambda e: e.nop(), inc="pe")
            att_era.close()
            _maybe_stop("attention")
            kb.fence()

            with contextlib.ExitStack() as pr:
                m_all = sb(pr, "m_all", [128, NT, E], F32)
                idxall = sb(pr, "idxall", [128, E, 8], I32)
                mT8 = sb(pr, "mT8", [8, S], BF16)
                x1s = es.enter_context(contextlib.ExitStack())
                x1 = sb(x1s, "x1", [128, NT, DM], F32)
                with contextlib.ExitStack() as pro:
                    outwb = sb(pro, "outwb", [128, 8, DM], BF16)
                    outwT = sb(pro, "outwT", [128, 8, DM], BF16)
                    aoT = sb(pro, "aoT", [128, 2, 8, 128], BF16)
                    xres = sb(pro, "xres", [128, 2, DM], F32)

                    gld_outw = kb.op("gpsimd", lambda e: e.dma_start(out=outwb[:], in_=d_outw.ap().rearrange("(c p) d -> p c d", p=128)),
                                     waits=[("pe", att_bar)], inc="dOW", inck=16)
                    ow_evs = [0, 0]
                    for cc in range(8):
                        for dc in range(8):
                            kb.op("tensor", lambda e, cc=cc, dc=dc: e.transpose(P6[:, cc % 2, dc], outwb[:, cc, dc * 128:(dc + 1) * 128], ident[:]),
                                  waits=[("dOW", gld_outw), ("dve", ow_evs[cc % 2])],
                                  inc="pe" if dc == 7 else None)
                        ow_evs[cc % 2] = kb.op("vector", lambda e, cc=cc: e.tensor_copy(
                            out=AP(tensor=outwT, offset=cc * 128, ap=[[8 * DM, 128], [DM, 8], [1, 128]]),
                            in_=P6[:, cc % 2]),
                            waits=[("pe", kb.now("pe"))], inc="dve")
                    owt_dve = kb.now("dve")

                    for tt in range(NT):
                        war = kb.now("pe") if tt >= 2 else 0
                        ld1 = kb.op("sync", lambda e, tt=tt: e.dma_start(
                            out=aoT[:, tt % 2],
                            in_=AP(tensor=d_aT, offset=tt * 128, ap=[[S, 128], [128 * S, 8], [1, 128]])),
                            waits=[("dAO", att_sld), ("pe", war)], inc="dAOT", inck=16)
                        war2 = kb.now("dve") if tt >= 2 else 0
                        ld2 = kb.op("sync", lambda e, tt=tt: e.dma_start(out=xres[:, tt % 2], in_=d_x.ap()[tt * 128:(tt + 1) * 128, :]),
                                    waits=[("dve", war2)], inc="dXR", inck=16)
                        for dd in range(2):
                            for c in range(8):
                                kb.op("tensor", lambda e, tt=tt, dd=dd, c=c: e.matmul(
                                    P1[:, dd * 512:(dd + 1) * 512], aoT[:, tt % 2, c, :],
                                    outwT[:, c, dd * 512:(dd + 1) * 512],
                                    start=(c == 0), stop=(c == 7)),
                                    waits=[("dAOT", ld1), ("dve", owt_dve), ("dve", kb.now("dve"))],
                                    inc="pe" if c == 7 else None)
                        mm_tt = kb.now("pe")
                        for dd in range(2):
                            kb.op("vector", lambda e, tt=tt, dd=dd: e.tensor_add(
                                out=x1[:, tt, dd * 512:(dd + 1) * 512], in0=xres[:, tt % 2, dd * 512:(dd + 1) * 512],
                                in1=P1[:, dd * 512:(dd + 1) * 512]),
                                waits=[("pe", mm_tt), ("dXR", ld2)], inc="dve")
                        kb.op("vector", lambda e, tt=tt: e.reduce_sum(out=meancol[:], in_=x1[:, tt, :], axis=AX.X), inc="dve")
                        kb.op("vector", lambda e: e.tensor_scalar(out=meancol[:], in0=meancol[:], scalar1=1.0 / DM, scalar2=None, op0=ALU.mult), inc="dve")
                        kb.op("vector", lambda e, tt=tt: e.tensor_scalar(out=x1[:, tt, :], in0=x1[:, tt, :], scalar1=meancol[:], scalar2=None, op0=ALU.subtract), inc="dve")
                        kb.op("scalar", lambda e, tt=tt: e.activation(out=sqt[:], in_=x1[:, tt, :], func=AF.Square, accum_out=varall[:, tt:tt + 1]),
                              waits=[("dve", kb.now("dve")), ("act", kb.now("act"))], inc="act")
                kb.op("vector", lambda e: e.tensor_scalar(out=varall[:], in0=varall[:], scalar1=1.0 / DM, scalar2=None, op0=ALU.mult),
                      waits=[("act", kb.now("act"))], inc="dve")
                a1 = kb.op("scalar", lambda e: e.activation(out=rstd[:], in_=varall[:], func=AF.Ln, bias=1e-5),
                           waits=[("dve", kb.now("dve"))], inc="act")
                kb.op("scalar", lambda e: e.activation(out=rstd[:], in_=rstd[:], func=AF.Exp, scale=-0.5),
                      waits=[("act", a1)], inc="act")
                ln1_act = kb.now("act")
                for tt in range(NT):
                    kb.op("vector", lambda e, tt=tt: e.tensor_scalar(out=x1[:, tt, :], in0=x1[:, tt, :], scalar1=rstd[:, tt:tt + 1], scalar2=None, op0=ALU.mult),
                          waits=[("act", ln1_act)], inc="dve")
                x1_dve = kb.now("dve")

                _maybe_stop("proj")
                kb.fence()
                with contextlib.ExitStack() as gt:
                    x1T = sb(gt, "x1T", [128, 8, S], BF16)
                    x1loT = sb(gt, "x1loT", [128, 8, S], BF16)
                    x1b = sb(gt, "x1b", [128, 4, DM16], BF16)
                    x1lo = sb(gt, "x1lo", [128, 2, DM], BF16)
                    gsb = sb(gt, "gsb", [128, E], F32)
                    mx = sb(gt, "mx", [128, 1], F32)
                    nmx = sb(gt, "nmx", [128, 1], F32)
                    eq = sb(gt, "eq", [128, E], F32)
                    cnt2 = sb(gt, "cnt2", [128, 1], F32)
                    gwo = sb(gt, "gwo", [128, E], F32)
                    second = sb(gt, "second", [128, 1], F32)
                    c2 = sb(gt, "c2", [128, 1], F32)
                    c2i = sb(gt, "c2i", [128, 1], F32)
                    kthv = sb(gt, "kthv", [128, 1], F32)
                    ktha = sb(gt, "ktha", [128, 1], F32)
                    kthb = sb(gt, "kthb", [128, 1], F32)
                    selt = sb(gt, "selt", [128, E], F32)
                    emt = sb(gt, "emt", [128, E], F32)
                    z8 = sb(gt, "z8", [128, 1], F32)
                    rz8 = sb(gt, "rz8", [128, 1], F32)
                    mbf = sb(gt, "mbf", [128, E], BF16)
                    utri = sb(gt, "utri", [128, 128], F32)
                    eoff = sb(gt, "eoff", [128, E], F32)
                    tokf = sb(gt, "tokf", [128, 1], F32)
                    onescol = sb(gt, "onescol", [128, 1], F32)
                    selt2 = sb(gt, "selt2", [128, E], F32)
                    possb = sb(gt, "possb", [128, E], F32)
                    carrow = sb(gt, "carrow", [1, E], F32)
                    totrow = sb(gt, "totrow", [1, E], F32)
                    slf = sb(gt, "slf", [128, E], F32)
                    slint = sb(gt, "slint", [128, 2, 2], I32)
                    tokint = sb(gt, "tokint", [128, 2, 1], I32)
                    ecolB = sb(gt, "ecolB", [128, E], F32)
                    a1sb = sb(gt, "a1sb", [128, E], F32)
                    ohsb = sb(gt, "ohsb", [128, E], F32)
                    tsel = sb(gt, "tsel", [128, E], F32)
                    e1c = sb(gt, "e1c", [128, 1], F32)
                    slout = sb(gt, "slout", [128, 2], F32)
                    mtmp = sb(gt, "mtmp", [128, E], F32)
                    tokft = sb(gt, "tokft", [128, 1], F32)
                    idfill = sb(gt, "idfill", [128, E * CAP // 128], I32)
                    selt3 = sb(gt, "selt3", [128, E], F32)
                    zrowx = sb(gt, "zrowx", [1, DM16], BF16)

                    kb.op("gpsimd", lambda e: e.iota(utri[:], pattern=[[1, 128]], base=0, channel_multiplier=-1, allow_small_or_imprecise_dtypes=True), inc="gpc")
                    kb.op("gpsimd", lambda e: e.iota(eoff[:], pattern=[[CAP, E]], base=0, channel_multiplier=0, allow_small_or_imprecise_dtypes=True), inc="gpc")
                    kb.op("gpsimd", lambda e: e.iota(ecolB[:], pattern=[[1, E]], base=int(BIGF), channel_multiplier=0, allow_small_or_imprecise_dtypes=True), inc="gpc")
                    kb.op("gpsimd", lambda e: e.iota(tokf[:], pattern=[[1, 1]], base=0, channel_multiplier=1, allow_small_or_imprecise_dtypes=True), inc="gpc")
                    gidx_gpc = kb.now("gpc")
                    kb.op("vector", lambda e: e.tensor_scalar(out=utri[:], in0=utri[:], scalar1=0.5, scalar2=None, op0=ALU.is_ge),
                          waits=[("gpc", gidx_gpc)], inc="dve")
                    kb.op("vector", lambda e: e.memset(onescol[:], 1.0), inc="dve")
                    idf = kb.op("vector", lambda e: e.memset(idfill[:], S), inc="dve")
                    gcon_dve = kb.now("dve")
                    kb.op("sync", lambda e: e.dma_start(
                        out=AP(tensor=d_idx, offset=0, ap=[[E * CAP // 128, 128], [1, E * CAP // 128]]),
                        in_=idfill[:]),
                        waits=[("dve", idf)], inc="dIDF", inck=16)
                    zr2 = kb.op("vector", lambda e: e.memset(zrowx[:], 0.0), inc="dve")
                    kb.op("sync", lambda e: e.dma_start(out=d_x1b.ap()[S:S + 1, :], in_=zrowx[:]),
                          waits=[("dve", zr2)], inc="dX1B", inck=16)

                    xb_free = [0, 0, 0, 0]
                    x1b_st = [0, 0, 0, 0]
                    x1T_ev = 0
                    x1loT_ev = 0
                    for tt in range(NT):
                        ca = kb.op("scalar", lambda e, tt=tt: e.activation(out=x1b[:, tt % 4, 0:DM], in_=x1[:, tt, :], func=AF.Copy),
                                   waits=[("dve", x1_dve), ("pe", xb_free[tt % 4]), ("dX1B", x1b_st[tt % 4])], inc="act")
                        cd = kb.op("vector", lambda e, tt=tt: e.scalar_tensor_tensor(
                            out=x1lo[:, tt % 2], in0=x1b[:, tt % 4, 0:DM], scalar=-1.0, in1=x1[:, tt, :], op0=ALU.mult, op1=ALU.add),
                            waits=[("act", ca)], inc="dve")
                        for c in range(8):
                            kb.op("tensor", lambda e, tt=tt, c=c: e.transpose(P6[:, 0, c], x1b[:, tt % 4, c * 128:(c + 1) * 128], ident[:]),
                                  waits=[("act", ca), ("pe", setup_bar), ("dve", x1T_ev)],
                                  inc="pe" if c == 7 else None)
                        x1T_ev = kb.op("vector", lambda e, tt=tt: e.tensor_copy(
                            out=AP(tensor=x1T, offset=tt * 128, ap=[[8 * S, 128], [S, 8], [1, 128]]),
                            in_=P6[:, 0]),
                            waits=[("pe", kb.now("pe"))], inc="dve")
                        for c in range(8):
                            kb.op("tensor", lambda e, tt=tt, c=c: e.transpose(P6[:, 1, c], x1lo[:, tt % 2, c * 128:(c + 1) * 128], ident[:]),
                                  waits=[("dve", cd), ("dve", x1loT_ev)],
                                  inc="pe" if c == 7 else None)
                        x1loT_ev = kb.op("vector", lambda e, tt=tt: e.tensor_copy(
                            out=AP(tensor=x1loT, offset=tt * 128, ap=[[8 * S, 128], [S, 8], [1, 128]]),
                            in_=P6[:, 1]),
                            waits=[("pe", kb.now("pe"))], inc="dve")
                        xb_free[tt % 4] = kb.now("pe")
                        tp_dve = kb.now("dve")
                        for d in range(8):
                            kb.op("tensor", lambda e, tt=tt, d=d: e.matmul(
                                P3[:, 0:E], x1T[:, d, tt * 128:(tt + 1) * 128], wghi[:, d, :],
                                start=(d == 0), stop=False),
                                waits=[("dve", tp_dve)])
                            kb.op("tensor", lambda e, tt=tt, d=d: e.matmul(
                                P3[:, 0:E], x1T[:, d, tt * 128:(tt + 1) * 128], wglo[:, d, :],
                                start=False, stop=False))
                            kb.op("tensor", lambda e, tt=tt, d=d: e.matmul(
                                P3[:, 0:E], x1loT[:, d, tt * 128:(tt + 1) * 128], wghi[:, d, :],
                                start=False, stop=(d == 7)),
                                inc="pe" if d == 7 else None)
                        g_pe = kb.now("pe")
                        kb.op("vector", lambda e: e.tensor_copy(out=gsb[:], in_=P3[:, 0:E]), waits=[("pe", g_pe)], inc="dve")
                        kb.op("vector", lambda e: e.reduce_max(out=mx[:], in_=gsb[:], axis=AX.X), inc="dve")
                        kb.op("vector", lambda e: e.tensor_scalar(out=nmx[:], in0=mx[:], scalar1=-1.0, scalar2=None, op0=ALU.mult), inc="dve")
                        kb.op("vector", lambda e: e.tensor_scalar(out=eq[:], in0=gsb[:], scalar1=mx[:], scalar2=None, op0=ALU.is_equal), inc="dve")
                        kb.op("vector", lambda e: e.reduce_sum(out=cnt2[:], in_=eq[:], axis=AX.X), inc="dve")
                        kb.op("vector", lambda e: e.scalar_tensor_tensor(out=gwo[:], in0=eq[:], scalar=-1e9, in1=gsb[:], op0=ALU.mult, op1=ALU.add), inc="dve")
                        kb.op("vector", lambda e: e.reduce_max(out=second[:], in_=gwo[:], axis=AX.X), inc="dve")
                        kb.op("vector", lambda e: e.tensor_scalar(out=c2[:], in0=cnt2[:], scalar1=1.5, scalar2=None, op0=ALU.is_ge), inc="dve")
                        kb.op("vector", lambda e: e.tensor_scalar(out=c2i[:], in0=c2[:], scalar1=-1.0, scalar2=1.0, op0=ALU.mult, op1=ALU.add), inc="dve")
                        kb.op("vector", lambda e: e.tensor_mul(out=ktha[:], in0=c2[:], in1=mx[:]), inc="dve")
                        kb.op("vector", lambda e: e.tensor_mul(out=kthb[:], in0=c2i[:], in1=second[:]), inc="dve")
                        kb.op("vector", lambda e: e.tensor_add(out=kthv[:], in0=ktha[:], in1=kthb[:]), inc="dve")
                        kb.op("vector", lambda e: e.tensor_scalar(out=selt[:], in0=gsb[:], scalar1=kthv[:], scalar2=None, op0=ALU.is_ge), inc="dve")
                        ae = kb.op("scalar", lambda e: e.activation(out=emt[:], in_=gsb[:], func=AF.Exp, bias=nmx[:]),
                                   waits=[("dve", kb.now("dve"))], inc="act")
                        kb.op("vector", lambda e: e.tensor_mul(out=emt[:], in0=emt[:], in1=selt[:]), waits=[("act", ae)], inc="dve")
                        kb.op("vector", lambda e: e.reduce_sum(out=z8[:], in_=emt[:], axis=AX.X), inc="dve")
                        kb.op("vector", lambda e: e.reciprocal(out=rz8[:], in_=z8[:]), inc="dve")
                        m_dve = kb.op("vector", lambda e, tt=tt: e.tensor_scalar(out=m_all[:, tt, :], in0=emt[:], scalar1=rz8[:], scalar2=1.0 / E, op0=ALU.mult, op1=ALU.mult), inc="dve")
                        kb.op("vector", lambda e, tt=tt: e.tensor_copy(out=x1b[:, tt % 4, DM:DM + 8], in_=m_all[:, tt, :]), inc="dve")
                        kb.op("vector", lambda e, tt=tt: e.tensor_copy(out=mtmp[:], in_=x1b[:, tt % 4, DM:DM + 8]), inc="dve")
                        mlo_dve = kb.op("vector", lambda e, tt=tt: e.tensor_sub(out=x1b[:, tt % 4, DM + 8:DM + 16], in0=m_all[:, tt, :], in1=mtmp[:]), inc="dve")
                        x1b_st[tt % 4] = kb.op("sync", lambda e, tt=tt: e.dma_start(out=d_x1b.ap()[tt * 128:(tt + 1) * 128, :], in_=x1b[:, tt % 4]),
                              waits=[("act", ca), ("dve", mlo_dve), ("pe", kb.now("pe"))], inc="dX1B", inck=16)
                        kb.op("vector", lambda e, tt=tt: e.tensor_scalar(out=selt2[:], in0=m_all[:, tt, :], scalar1=0.0, scalar2=None, op0=ALU.is_gt),
                              waits=[("dve", gcon_dve)], inc="dve")
                        sel_dve = kb.now("dve")
                        if tt == 0:
                            cr = kb.op("vector", lambda e: e.memset(carrow[:], 0.0), inc="dve")
                            tot_ev_prev = 0
                        kb.op("tensor", lambda e: e.matmul(P3[:, 0:E], utri[:], selt2[:], start=True, stop=False),
                              waits=[("dve", sel_dve), ("pe", kb.now("pe"))])
                        pos_pe = kb.op("tensor", lambda e: e.matmul(P3[:, 0:E], ones1f[:], carrow[:], start=False, stop=True),
                                       waits=[("dve", cr)], inc="pe")
                        tot_pe = kb.op("tensor", lambda e: e.matmul(P4t[0:1, 512:512 + E], onescol[:], selt2[:],
                                                                    start=True, stop=True),
                                       waits=[("dve", max(cr, sel_dve, tot_ev_prev))], inc="pe")
                        tot_ev_prev = kb.op("vector", lambda e: e.tensor_copy(out=totrow[:], in_=P4t[0:1, 512:512 + E]),
                                            waits=[("pe", tot_pe)], inc="dve")
                        cr = kb.op("vector", lambda e: e.tensor_add(out=carrow[:], in0=carrow[:], in1=totrow[:]),
                                   waits=[("pe", pos_pe)], inc="dve")
                        pv2 = kb.op("vector", lambda e: e.tensor_copy(out=possb[:], in_=P3[:, 0:E]),
                                    waits=[("pe", pos_pe)], inc="dve")
                        kb.op("vector", lambda e: e.tensor_add(out=slf[:], in0=possb[:], in1=eoff[:]), inc="dve")
                        kb.op("vector", lambda e: e.tensor_scalar(out=slf[:], in0=slf[:], scalar1=BIGF, scalar2=None, op0=ALU.add), inc="dve")
                        kb.op("vector", lambda e: e.scalar_tensor_tensor(out=slf[:], in0=selt2[:], scalar=-BIGF, in1=slf[:], op0=ALU.mult, op1=ALU.add), inc="dve")
                        kb.op("vector", lambda e: e.tensor_scalar(out=selt3[:], in0=possb[:], scalar1=float(CAP) - 0.5, scalar2=BIGF, op0=ALU.is_ge, op1=ALU.mult), inc="dve")
                        kb.op("vector", lambda e: e.tensor_add(out=slf[:], in0=slf[:], in1=selt3[:]), inc="dve")
                        kb.op("vector", lambda e: e.scalar_tensor_tensor(out=a1sb[:], in0=selt2[:], scalar=-BIGF, in1=ecolB[:], op0=ALU.mult, op1=ALU.add), inc="dve")
                        kb.op("vector", lambda e: e.tensor_reduce(out=e1c[:], in_=a1sb[:], axis=AX.X, op=ALU.min), inc="dve")
                        kb.op("vector", lambda e: e.tensor_scalar(out=ohsb[:], in0=a1sb[:], scalar1=e1c[:], scalar2=None, op0=ALU.is_equal), inc="dve")
                        kb.op("vector", lambda e: e.tensor_mul(out=tsel[:], in0=slf[:], in1=ohsb[:]), inc="dve")
                        kb.op("vector", lambda e: e.reduce_sum(out=slout[:, 0:1], in_=tsel[:], axis=AX.X), inc="dve")
                        kb.op("vector", lambda e: e.scalar_tensor_tensor(out=a1sb[:], in0=ohsb[:], scalar=BIGF, in1=a1sb[:], op0=ALU.mult, op1=ALU.add), inc="dve")
                        kb.op("vector", lambda e: e.tensor_reduce(out=e1c[:], in_=a1sb[:], axis=AX.X, op=ALU.min), inc="dve")
                        kb.op("vector", lambda e: e.tensor_scalar(out=ohsb[:], in0=a1sb[:], scalar1=e1c[:], scalar2=None, op0=ALU.is_equal), inc="dve")
                        kb.op("vector", lambda e: e.tensor_mul(out=tsel[:], in0=slf[:], in1=ohsb[:]), inc="dve")
                        kb.op("vector", lambda e: e.reduce_sum(out=slout[:, 1:2], in_=tsel[:], axis=AX.X), inc="dve")
                        kb.op("vector", lambda e, tt=tt: e.tensor_scalar(out=slout[:, 0:1], in0=slout[:, 0:1], scalar1=float(E * CAP + tt * 2), scalar2=None, op0=ALU.min), inc="dve")
                        kb.op("vector", lambda e, tt=tt: e.tensor_scalar(out=slout[:, 1:2], in0=slout[:, 1:2], scalar1=float(E * CAP + tt * 2 + 1), scalar2=None, op0=ALU.min), inc="dve")
                        kb.op("vector", lambda e, tt=tt: e.tensor_copy(out=slint[:, tt % 2], in_=slout[:]),
                              waits=([("dIX%d" % (tt % 2), (tt // 2) * 2 * 16)] if tt >= 2 else []), inc="dve")
                        kb.op("vector", lambda e, tt=tt: e.tensor_scalar(out=tokft[:], in0=tokf[:], scalar1=float(tt * 128), scalar2=None, op0=ALU.add), inc="dve")
                        ti_dve = kb.op("vector", lambda e, tt=tt: e.tensor_copy(out=tokint[:, tt % 2], in_=tokft[:]), inc="dve")
                        for e_x in range(2):
                            kb.op("gpsimd", lambda e, e_x=e_x, tt=tt: e.indirect_dma_start(
                                out=d_idx.ap(),
                                out_offset=IndirectOffsetOnAxis(slint[:, tt % 2, e_x:e_x + 1], 0),
                                in_=tokint[:, tt % 2], in_offset=None),
                                waits=[("dve", ti_dve), ("dIDF", 16)], inc="dIX%d" % (tt % 2), inck=16, unordered=True)
                        kb.op("vector", lambda e, tt=tt: e.tensor_copy(out=mbf[:], in_=m_all[:, tt, :]), inc="dve")
                        mb_dve = kb.now("dve")
                        kb.op("tensor", lambda e, tt=tt: e.transpose(P6[0:8, 0, 0], mbf[:], ident[:]),
                              waits=[("dve", mb_dve)], inc="pe")
                        kb.op("vector", lambda e, tt=tt: e.tensor_copy(out=mT8[:, tt * 128:(tt + 1) * 128], in_=P6[0:8, 0, 0]),
                              waits=[("pe", kb.now("pe"))], inc="dve")
                        x1T_ev = kb.now("dve")
                    mask_dve = kb.now("dve")

                    b2ev = 0
                    for tt in range(NT):
                        for dd in range(2):
                            bm = kb.op("tensor", lambda e, tt=tt, dd=dd: e.matmul(
                                P4, mT8[:, tt * 128:(tt + 1) * 128], b2sb[:, dd * 512:(dd + 1) * 512],
                                start=True, stop=True),
                                waits=[("dve", max(mask_dve, b2ev)), ("dB2", b2_gld)], inc="pe")
                            b2ev = kb.op("vector", lambda e, tt=tt, dd=dd: e.tensor_add(
                                out=x1[:, tt, dd * 512:(dd + 1) * 512], in0=x1[:, tt, dd * 512:(dd + 1) * 512], in1=P4),
                                waits=[("pe", bm)], inc="dve")
                    b2_dve = kb.now("dve")

                    yzm = kb.op("vector", lambda e: e.memset(episb[:], 0.0), inc="dve")
                    w_init = kb.now("dve")
                    for tt in range(NT):
                        kb.op("sync", lambda e, tt=tt: e.dma_start(out=d_y.ap()[tt * 128:(tt + 1) * 128, :], in_=x1[:, tt, :]),
                              waits=[("dve", w_init)], inc="dYZ", inck=16, unordered=True)
                    kb.op("sync", lambda e: e.dma_start(out=d_y.ap()[S:S + 128, :], in_=episb[:]),
                          waits=[("dve", yzm)], inc="dYZ", inck=16, unordered=True)

                    for e_i2 in range(E):
                        for sbk in range(8):
                            kb.op("sync", lambda e, e_i2=e_i2, sbk=sbk: e.dma_start(
                                out=idxall[:, e_i2, sbk:sbk + 1],
                                in_=AP(tensor=d_idx, offset=e_i2 * CAP + sbk * 128, ap=[[1, 128], [1, 1]])),
                                waits=[("dIX0", (NT // 2) * 2 * 16), ("dIX1", (NT // 2) * 2 * 16)], inc="dIL", inck=16, unordered=True)

                _maybe_stop("gate")
                kb.fence()
                x1s.close()
                fx = es.enter_context(contextlib.ExitStack())
                hT = sb(fx, "hT", [128, 32, CAP], BF16)
                w1sb = sb(fx, "w1sb", [128, 4, 8, 256], BF16)
                w2sb = sb(fx, "w2sb", [128, 4, 8, 512], BF16)
                xg = sb(fx, "xg", [128, 8, DM16], BF16)
                xgT = sb(fx, "xgT", [128, 8, CAP], BF16)
                mgcol = sb(fx, "mgcol", [128, 2, 8], F32)
                idxsc = sb(fx, "idxsc", [128, 16], I32)
                idxeq = sb(fx, "idxeq", [128, 16], I32)
                ctrash = sb(fx, "ctrash", [128, 16], I32)
                ysc = sb(fx, "ysc", [128, 4, 512], F32)
                psY = [P1[:, 0:512], P1[:, 512:1024], P2[:, 0:512], P2[:, 512:1024],
                       P3[:, 0:512], P3[:, 512:1024], P4t[:, 0:512], P4t[:, 512:1024]]

                idx_done = (NT // 2) * 2 * 16
                x1b_done = kb.now("dX1B")
                yz_done = kb.now("dYZ")

                il_total = kb.now("dIL")

                w1_free = [0, 0, 0, 0]
                w2_free = [0, 0, 0, 0]
                psh_ev = [("dve", 0), ("dve", 0)]
                y_ev = 0
                yev_bank = [0, 0]
                yev_first4 = 0
                mg_war = [0, 0]
                relu_alt = 0
                w1_last_pe = 0
                tp_done_prev = 0
                gx_total = [0] * E
                ys_hist = []
                ys_prev_expert = {}

                def issue_xgathers(e_i, war_pe, war_dve):
                    for sbk in range(8):
                        kb.op("gpsimd", lambda e, e_i=e_i, sbk=sbk: e.indirect_dma_start(
                            out=xg[:, sbk], out_offset=None,
                            in_=d_x1b.ap(), in_offset=IndirectOffsetOnAxis(idxall[:, e_i, sbk:sbk + 1], 0)),
                            waits=[("dIL", il_total), ("dX1B", x1b_done), ("pe", war_pe), ("dve", war_dve)],
                            inc="dGX%d" % (e_i % 2), inck=16, unordered=True)
                    return kb.now("dGX%d" % (e_i % 2))

                kb.op("gpsimd", lambda e: e.iota(ctrash[:], pattern=[[1, 16]], base=0, channel_multiplier=0), inc="gpc")
                ct_gpc = kb.now("gpc")
                gx_total[0] = issue_xgathers(0, 0, 0)
                for e_i in range(E):
                    ep = e_i % 2
                    kb.op("vector", lambda e, e_i=e_i: e.tensor_copy(out=idxsc[:, 0:8], in_=idxall[:, e_i, :]),
                          waits=[("gpc", ct_gpc), ("dIL", il_total)] + [(sn, cv) for sn, cv in ys_prev_expert.items()], inc="dve")
                    kb.op("vector", lambda e, e_i=e_i: e.tensor_copy(out=idxsc[:, 8:16], in_=idxall[:, e_i, :]), inc="dve")
                    kb.op("vector", lambda e, e_i=e_i: e.tensor_scalar(out=idxeq[:, 0:8], in0=idxall[:, e_i, :], scalar1=S, scalar2=None, op0=ALU.is_equal),
                          inc="dve")
                    kb.op("vector", lambda e: e.tensor_copy(out=idxeq[:, 8:16], in_=idxeq[:, 0:8]), inc="dve")
                    kb.op("vector", lambda e: e.tensor_mul(out=idxeq[:], in0=idxeq[:], in1=ctrash[:]), inc="dve")
                    idxsc_dve = kb.op("vector", lambda e: e.tensor_add(out=idxsc[:], in0=idxsc[:], in1=idxeq[:]), inc="dve")
                    xgt_evs = [0, 0]
                    for sbk in range(8):
                        for c in range(8):
                            kb.op("tensor", lambda e, sbk=sbk, c=c, e_i=e_i: e.transpose(P6[:, sbk % 2, c], xg[:, sbk, c * 128:(c + 1) * 128], ident[:]),
                                  waits=[("dGX%d" % (e_i % 2), gx_total[e_i]), ("dve", xgt_evs[sbk % 2]), ("dve", yev_bank[sbk % 2])],
                                  inc="pe" if c == 7 else None)
                        xgt_evs[sbk % 2] = kb.op("vector", lambda e, sbk=sbk: e.tensor_copy(
                            out=AP(tensor=xgT, offset=sbk * 128, ap=[[8 * CAP, 128], [CAP, 8], [1, 128]]),
                            in_=P6[:, sbk % 2]),
                            waits=[("pe", kb.now("pe")), ("pe", w1_last_pe)], inc="dve")
                    for sbk in range(8):
                        kb.op("vector", lambda e, e_i=e_i, sbk=sbk, ep=ep: e.tensor_tensor(
                            out=mgcol[:, ep, sbk:sbk + 1], in0=xg[:, sbk, DM + e_i:DM + e_i + 1],
                            in1=xg[:, sbk, DM + 8 + e_i:DM + 9 + e_i], op=ALU.add),
                            waits=[("dGX%d" % (e_i % 2), gx_total[e_i])], inc="dve")
                    xgt_dve = kb.now("dve")
                    tp_done = kb.now("pe")
                    if e_i + 1 < E:
                        gx_total[e_i + 1] = issue_xgathers(e_i + 1, tp_done, xgt_dve)
                    for jg in range(16):
                        w1_ld = kb.op("sync", lambda e, e_i=e_i, jg=jg: e.dma_start(
                            out=w1sb[:, jg % 4],
                            in_=AP(tensor=d_w1b, offset=e_i * DM * HF + jg * 256,
                                   ap=[[HF, 128], [128 * HF, 8], [1, 256]])),
                            waits=[("pe", w1_free[jg % 4]), ("dWCo", (PC_TOTAL // 2) * 16)],
                            inc="dW1", inck=16)
                        for js in range(2):
                            jc = jg * 2 + js
                            pp = jc % 2
                            psH = (P1, P2)[pp]
                            for th in range(2):
                                for d in range(8):
                                    kb.op("tensor", lambda e, jg=jg, js=js, d=d, th=th, psH=psH: e.matmul(
                                        psH[:, th * 512:(th + 1) * 512],
                                        w1sb[:, jg % 4, d, js * 128:(js + 1) * 128],
                                        xgT[:, d, th * 512:(th + 1) * 512],
                                        start=(d == 0), stop=(d == 7)),
                                        waits=[("dW1", w1_ld), ("dve", max(xgt_dve, yev_first4)), psh_ev[pp]],
                                        inc="pe" if (d == 7 and th == 1) else None)
                            h_pe = kb.now("pe")
                            if relu_alt == 0:
                                m = kb.op("scalar", lambda e, e_i=e_i, jc=jc, psH=psH: e.activation(
                                    out=hT[:, jc, :], in_=psH[:], func=AF.Relu, bias=b1sb[:, e_i, jc:jc + 1]),
                                    waits=[("pe", h_pe)], inc="act")
                                psh_ev[pp] = ("act", m)
                            else:
                                m = kb.op("vector", lambda e, e_i=e_i, jc=jc, psH=psH: e.tensor_scalar(
                                    out=hT[:, jc, :], in0=psH[:],
                                    scalar1=b1sb[:, e_i, jc:jc + 1], scalar2=0.0,
                                    op0=ALU.add, op1=ALU.max),
                                    waits=[("pe", h_pe)], inc="dve")
                                psh_ev[pp] = ("dve", m)
                            relu_alt ^= 1
                        w1_free[jg % 4] = kb.now("pe")
                    w1_last_pe = kb.now("pe")
                    h_act = kb.now("act")
                    h_dve = kb.now("dve")
                    def emit_w2_load(k2, e_i=e_i):
                        ddh, jcg = divmod(k2, 4)
                        return kb.op("scalar", lambda e, e_i=e_i, jcg=jcg, ddh=ddh, k2=k2: e.dma_start(
                            out=w2sb[:, k2 % 4],
                            in_=AP(tensor=d_w2b, offset=e_i * HF * DM + jcg * 8 * 128 * DM + ddh * 512,
                                   ap=[[DM, 128], [128 * DM, 8], [1, 512]])),
                            waits=[("pe", w2_free[k2 % 4]), ("dWCo", PC_TOTAL * 16)],
                            inc="dW2", inck=16)
                    w2_marks = {}
                    for k2 in range(3):
                        w2_marks[k2] = emit_w2_load(k2)
                    for ddh in range(2):
                        for jcg in range(4):
                            k2 = ddh * 4 + jcg
                            if k2 + 3 < 8:
                                w2_marks[k2 + 3] = emit_w2_load(k2 + 3)
                            w2_ld = w2_marks[k2]
                            for ji in range(8):
                                jc = jcg * 8 + ji
                                for ts in range(8):
                                    kb.op("tensor", lambda e, jc=jc, ji=ji, k2=k2, ts=ts: e.matmul(
                                        psY[ts], hT[:, jc, ts * 128:(ts + 1) * 128],
                                        w2sb[:, k2 % 4, ji, :],
                                        start=(jc == 0), stop=(jc == 31)),
                                        waits=[("dW2", w2_ld), ("act", h_act), ("dve", h_dve)] +
                                              ([("dve", y_ev)] if jc == 0 else []),
                                        inc="pe" if ts == 7 else None)
                            w2_free[k2 % 4] = kb.now("pe")
                        y_pe = kb.now("pe")
                        for ts in range(8):
                            k = len(ys_hist)
                            war = ys_hist[k - 4][1] if k >= 4 else 0
                            war_sem = ys_hist[k - 4][0] if k >= 4 else "dY0"
                            y_ev = kb.op("vector", lambda e, ts=ts, ep=ep: e.tensor_scalar(
                                out=ysc[:, ts % 4, :], in0=psY[ts],
                                scalar1=mgcol[:, ep, ts:ts + 1], scalar2=None, op0=ALU.mult),
                                waits=[("pe", y_pe), (war_sem, war)], inc="dve")
                            sname = "dY%d" % ts
                            jcol = ddh * 8 + ts
                            cnt = kb.op("gpsimd", lambda e, ts=ts, ddh=ddh, ep=ep, jcol=jcol: e.indirect_dma_start(
                                out=d_y.ap(), out_offset=IndirectOffsetOnAxis(idxsc[:, jcol:jcol + 1], 0),
                                in_=ysc[:, ts % 4, :], in_offset=None,
                                element_offset=ddh * 512,
                                compute_op=ALU.add),
                                waits=[("dve", y_ev), ("dYZ", yz_done)] +
                                      [(sn, cv) for sn, cv in ys_prev_expert.items()],
                                inc=sname, inck=16, unordered=True)
                            ys_hist.append((sname, cnt))
                            if ddh == 1 and ts == 1:
                                yev_bank = [None, None]
                                yev_bank[0] = y_ev - 1
                                yev_bank[1] = y_ev
                            if ddh == 1 and ts == 3:
                                yev_first4 = y_ev
                    mg_war[ep] = y_ev
                    tp_done_prev = tp_done
                    ys_prev_expert = {("dY%d" % t): kb.now("dY%d" % t) for t in range(8)}
                ffn_dve = kb.now("dve")

                fx.close()
                kb.fence()
                outsb = sb(pr, "outsb", [128, 2, DM], F32)
                yld = sb(pr, "yld", [128, 4, DM], F32)
                meanall = sb(pr, "meanall", [128, NT], F32)
                msq = sb(pr, "msq", [128, NT], F32)
                ys_totals = [("dY%d" % t, kb.now("dY%d" % t)) for t in range(8)]
                YL = ["dYL", "dYL1", "dYL2", "dYL3"]
                war_d = [0, 0, 0, 0]
                war_a = [0, 0, 0, 0]
                for tt in range(NT):
                    ly = kb.op("sync", lambda e, tt=tt: e.dma_start(out=yld[:, tt % 4], in_=d_y.ap()[tt * 128:(tt + 1) * 128, :]),
                               waits=ys_totals + [("dve", war_d[tt % 4]), ("act", war_a[tt % 4])], inc=YL[tt % 4], inck=16)
                    war_d[tt % 4] = kb.op("vector", lambda e, tt=tt: e.reduce_sum(out=meanall[:, tt:tt + 1], in_=yld[:, tt % 4], axis=AX.X),
                                          waits=[(YL[tt % 4], ly)], inc="dve")
                    war_a[tt % 4] = kb.op("scalar", lambda e, tt=tt: e.activation(out=sqt[:], in_=yld[:, tt % 4], func=AF.Square, accum_out=varall[:, tt:tt + 1]),
                          waits=[(YL[tt % 4], ly), ("act", kb.now("act"))] + ([("dYZ", (NT + 1) * 16)] if tt == 0 else []), inc="act")
                kb.op("vector", lambda e: e.tensor_scalar(out=meanall[:], in0=meanall[:], scalar1=1.0 / DM, scalar2=None, op0=ALU.mult), inc="dve")
                kb.op("vector", lambda e: e.tensor_mul(out=msq[:], in0=meanall[:], in1=meanall[:]), inc="dve")
                kb.op("vector", lambda e: e.tensor_scalar(out=varall[:], in0=varall[:], scalar1=1.0 / DM, scalar2=None, op0=ALU.mult),
                      waits=[("act", kb.now("act"))], inc="dve")
                kb.op("vector", lambda e: e.tensor_sub(out=varall[:], in0=varall[:], in1=msq[:]), inc="dve")
                a2 = kb.op("scalar", lambda e: e.activation(out=rstd[:], in_=varall[:], func=AF.Ln, bias=1e-5),
                           waits=[("dve", kb.now("dve"))], inc="act")
                kb.op("scalar", lambda e: e.activation(out=rstd[:], in_=rstd[:], func=AF.Exp, scale=-0.5),
                      waits=[("act", a2)], inc="act")
                ln2_act = kb.now("act")
                OUTS = ["out", "out1"]
                out_marks = [0, 0]
                for tt in range(NT):
                    ly = kb.op("sync", lambda e, tt=tt: e.dma_start(out=yld[:, tt % 4], in_=d_y.ap()[tt * 128:(tt + 1) * 128, :]),
                               waits=[("act", ln2_act), ("dve", war_d[tt % 4])], inc=YL[tt % 4], inck=16)
                    od = kb.op("vector", lambda e, tt=tt: e.tensor_scalar(out=outsb[:, tt % 2], in0=yld[:, tt % 4],
                                                                          scalar1=meanall[:, tt:tt + 1], scalar2=rstd[:, tt:tt + 1],
                                                                          op0=ALU.subtract, op1=ALU.mult),
                               waits=[(YL[tt % 4], ly), (OUTS[tt % 2], out_marks[tt % 2])], inc="dve")
                    war_d[tt % 4] = od
                    out_marks[tt % 2] = kb.op("sync", lambda e, tt=tt: e.dma_start(out=d_out.ap()[tt * 128:(tt + 1) * 128, :], in_=outsb[:, tt % 2]),
                          waits=[("dve", od)], inc=OUTS[tt % 2], inck=16)
                kb.wait("sync", ("out", out_marks[0]), ("out1", out_marks[1]))


        except _Stop:
            pass

        blk = es.enter_context(nc.Block())
        kb.emit(blk)

    return nc


_NC_CACHE = None


def kernel(L0_attn_x, L0_attn_qkv_w, L0_attn_out_w, L0_attn_bias_p, L0_attn_bias_a,
           L0_attn_rope_freqs, L0_ffn_W_gate, L0_ffn_W1, L0_ffn_b1, L0_ffn_W2, L0_ffn_b2):
    global _NC_CACHE
    if _NC_CACHE is None:
        _NC_CACHE = build_program()
    nc = _NC_CACHE
    weights = {
        "qkvw": np.ascontiguousarray(np.asarray(L0_attn_qkv_w, dtype=np.float32)),
        "outw": np.ascontiguousarray(np.asarray(L0_attn_out_w, dtype=np.float32)),
        "bp": np.ascontiguousarray(np.asarray(L0_attn_bias_p, dtype=np.float32).reshape(H)),
        "ba": np.ascontiguousarray(np.asarray(L0_attn_bias_a, dtype=np.float32).reshape(H)),
        "freqs": np.ascontiguousarray(np.asarray(L0_attn_rope_freqs, dtype=np.float32)),
        "wg": np.ascontiguousarray(np.asarray(L0_ffn_W_gate, dtype=np.float32)),
        "w1": np.ascontiguousarray(np.asarray(L0_ffn_W1, dtype=np.float32)),
        "b1": np.ascontiguousarray(np.asarray(L0_ffn_b1, dtype=np.float32)),
        "w2": np.ascontiguousarray(np.asarray(L0_ffn_W2, dtype=np.float32)),
        "b2": np.ascontiguousarray(np.asarray(L0_ffn_b2, dtype=np.float32)),
    }
    x = np.asarray(L0_attn_x, dtype=np.float32)
    in_maps = []
    for i in range(8):
        m = {"x": np.ascontiguousarray(x[i])}
        m.update(weights)
        in_maps.append(m)
    res = run_bass_kernel_spmd(nc, in_maps, core_ids=list(range(8)))
    return np.stack([np.asarray(r["out"], dtype=np.float32) for r in res.results], axis=0)


# version 14
# speedup vs baseline: 1.0580x; 1.0126x over previous
import contextlib
import numpy as np
import concourse.bass as bass
import concourse.mybir as mybir
from concourse.ap import AP
from concourse.bass import IndirectOffsetOnAxis
from concourse.bass_utils import run_bass_kernel_spmd

F32 = mybir.dt.float32
BF16 = mybir.dt.bfloat16
I32 = mybir.dt.int32
AF = mybir.ActivationFunctionType
ALU = mybir.AluOpType
AX = mybir.AxisListType

S = 2048
DM = 1024
H = 16
E = 8
HF = 4096
NT = S // 128
SCALE = 0.125
CAP = 1024
BIGF = 1.0e6
TWO_PI = float(2.0 * np.pi)
HALF_PI = float(0.5 * np.pi)

ENGINES = ("tensor", "vector", "scalar", "gpsimd", "sync")


class KB:

    def __init__(self, nc, es):
        self.nc = nc
        self.streams = {e: [] for e in ENGINES}
        self.sems = {}
        self.cnt = {}
        self.last_wait = {e: {} for e in ENGINES}
        self._es = es

    def sem(self, name):
        self.sems[name] = self._es.enter_context(self.nc.semaphore(name))
        self.cnt[name] = 0

    def now(self, sem):
        return self.cnt[sem]

    AUTO = {"vector": "dve", "scalar": "act", "gpsimd": "gpc"}

    def op(self, eng, build, waits=(), inc=None, inck=1, unordered=False):
        if getattr(self, "stopped", False):
            return self.cnt.get(inc, 0)
        waits = list(waits)
        auto = self.AUTO.get(eng)
        if build is not None and auto is not None:
            waits.append((auto, self.cnt[auto]))
            if inc is None:
                inc = auto
                inck = 1
        if inc is not None and inck == 16 and not unordered:
            waits.append((inc, self.cnt[inc]))
        wl = []
        for s_, c in waits:
            if c > 0 and c > self.last_wait[eng].get(s_, 0):
                wl.append((s_, c))
                self.last_wait[eng][s_] = c
        if inc is not None:
            self.cnt[inc] += inck
        self.streams[eng].append((wl, build, inc, inck))
        return self.cnt[inc] if inc is not None else None

    def wait(self, eng, *waits):
        self.op(eng, None, waits=waits)

    def fence(self, engs=ENGINES):
        snap = [(s_, c) for s_, c in self.cnt.items() if c > 0]
        for e_ in engs:
            self.op(e_, None, waits=snap)

    def emit(self, blk):
        for eng in ENGINES:
            stream = self.streams[eng]
            if not stream:
                continue

            def make(stream=stream):
                def f(e):
                    for wl, build, inc, inck in stream:
                        for s_, c in wl:
                            e.wait_ge(self.sems[s_], c)
                        ins = build(e) if build is not None else None
                        if inc is not None:
                            if ins is None:
                                ins = e.nop()
                            ins.then_inc(self.sems[inc], inck)
                return f

            getattr(blk, eng)(make())


class _Stop(Exception):
    pass


def build_program(upto=None):
    nc = bass.Bass("TRN2", target_bir_lowering=False)

    d_x = nc.dram_tensor("x", [S, DM], F32, kind="ExternalInput")
    d_qkvw = nc.dram_tensor("qkvw", [3 * DM, DM], F32, kind="ExternalInput")
    d_outw = nc.dram_tensor("outw", [DM, DM], F32, kind="ExternalInput")
    d_bp = nc.dram_tensor("bp", [H], F32, kind="ExternalInput")
    d_ba = nc.dram_tensor("ba", [H], F32, kind="ExternalInput")
    d_freqs = nc.dram_tensor("freqs", [32], F32, kind="ExternalInput")
    d_wg = nc.dram_tensor("wg", [DM, E], F32, kind="ExternalInput")
    d_w1 = nc.dram_tensor("w1", [E, DM, HF], F32, kind="ExternalInput")
    d_b1 = nc.dram_tensor("b1", [E, HF], F32, kind="ExternalInput")
    d_w2 = nc.dram_tensor("w2", [E, HF, DM], F32, kind="ExternalInput")
    d_b2 = nc.dram_tensor("b2", [E, DM], F32, kind="ExternalInput")
    d_out = nc.dram_tensor("out", [S, DM], F32, kind="ExternalOutput")
    d_qkT = nc.dram_tensor("qkT_s", [DM, 2 * DM], BF16, kind="Internal")
    d_aT = nc.dram_tensor("aT_s", [DM, S], BF16, kind="Internal")
    d_band = nc.dram_tensor("band_s", [H, 128, 4096], BF16, kind="Internal")
    d_w1b = nc.dram_tensor("w1b_s", [E * DM * HF], BF16, kind="Internal")
    d_w2b = nc.dram_tensor("w2b_s", [E * HF * DM], BF16, kind="Internal")
    DM16 = DM + 16
    d_x1b = nc.dram_tensor("x1b_s", [S + 1, DM16], BF16, kind="Internal")
    d_idx = nc.dram_tensor("idx_s", [E * CAP + 128, 1], I32, kind="Internal")
    d_y = nc.dram_tensor("y_s", [S + 128, DM], F32, kind="Internal")

    es = contextlib.ExitStack()
    with es:
        kb = KB(nc, es)
        for s_ in ("pe", "act", "dve", "gpc", "out",
                   "dS", "dB2", "dXB", "dQW", "dTS", "dOW", "dBS",
                   "dWSL", "dBBF", "dSW", "dZA", "dRZ", "dAO", "dAOT", "dXR",
                   "dW1", "dW2", "dWC", "dWCo", "dID",
                   "dX1B", "dIX0", "dIX1", "dIDF", "dGX0", "dGX1", "dYZ", "dIL",
                   "dYL", "dYL1", "dYL2", "dYL3", "out1",
                   "dY0", "dY1", "dY2", "dY3", "dY4", "dY5", "dY6", "dY7"):
            kb.sem(s_)

        def sb(stack, name, shape, dtype):
            return stack.enter_context(nc.sbuf_tensor(name, shape, dtype))

        P1 = es.enter_context(nc.psum_tensor("P1", [128, 1024], F32))
        P2 = es.enter_context(nc.psum_tensor("P2", [128, 1024], F32))
        P3 = es.enter_context(nc.psum_tensor("P3", [128, 1024], F32))
        P4t = es.enter_context(nc.psum_tensor("P4t", [128, 1024], F32))
        P4 = P4t[:, 0:512]
        P6 = P1[:].bitcast(BF16).rearrange("p (a b c) -> p a b c", a=2, b=8)

        ident = sb(es, "ident", [128, 128], BF16)
        psw = sb(es, "psw", [128, 128], BF16)
        onebf = sb(es, "onebf", [1, 1], BF16)
        ones1f = sb(es, "ones1f", [1, 128], F32)
        ones16 = sb(es, "ones16", [128, NT], F32)
        CONSTS = [0.0, 1.0, -1.0, 1e-5, -0.5, 0.125, HALF_PI]
        csb = sb(es, "csb", [128, len(CONSTS)], F32)
        for ci, cv in enumerate(CONSTS):
            nc.const_aps.register(F32, cv, csb[:, ci:ci + 1])
        b1sb = sb(es, "b1sb", [128, E, 32], F32)
        b2sb = sb(es, "b2sb", [E, DM], BF16)
        wghi = sb(es, "wghi", [128, 8, E], BF16)
        wglo = sb(es, "wglo", [128, 8, E], BF16)
        varall = sb(es, "varall", [128, NT], F32)
        rstd = sb(es, "rstd", [128, NT], F32)
        meancol = sb(es, "meancol", [128, 1], F32)

        episb = sb(es, "episb", [128, DM], F32)
        sqt = episb

        def _maybe_stop(stage):
            if upto != stage:
                return
            kb.fence()
            gm = kb.op("gpsimd", lambda e: e.memset(episb[:], 0.0))
            for tt_ in range(NT):
                kb.op("sync", lambda e, tt_=tt_: e.dma_start(out=d_out.ap()[tt_ * 128:(tt_ + 1) * 128, :], in_=episb[:]),
                      waits=[("gpc", kb.now("gpc"))], inc="out", inck=16)
            kb.wait("sync", ("out", NT * 16))
            kb.stopped = True

        try:
            att_era = es.enter_context(contextlib.ExitStack())
            xT = sb(att_era, "xT", [128, 8, S], BF16)
            V = sb(att_era, "V", [128, NT, H * 65], BF16)
            C4 = sb(att_era, "C4", [128, S], F32)
            S4 = sb(att_era, "S4", [128, S], F32)
            with contextlib.ExitStack() as st1:
                posrow = sb(st1, "posrow", [1, S], F32)
                sgni = sb(st1, "sgni", [128, 1], I32)
                bandabs = sb(st1, "bandabs", [128, 4096], F32)
                qkvwb = sb(st1, "qkvwb", [128, 6, DM], BF16)
                qkvwTv = sb(st1, "qkvwTv", [128, 8, DM], BF16)
                tstage = sb(st1, "tstage", [128, 2, 8, 128], BF16)
                xb = sb(st1, "xb", [128, 2, DM], BF16)
                xbf = sb(st1, "xbf", [128, 2, DM], F32)
                wgf = sb(st1, "wgf", [128, 8, E], F32)
                bparow = sb(st1, "bparow", [1, 2, H], F32)
                paac = sb(st1, "paac", [128, H], F32)
                npac = sb(st1, "npac", [128, H], F32)

                kb.op("sync", lambda e: e.dma_start(out=bparow[:, 0], in_=AP(tensor=d_ba, offset=0, ap=[[1, 1], [1, H]])), inc="dS", inck=16)
                kb.op("sync", lambda e: e.dma_start(out=bparow[:, 1], in_=AP(tensor=d_bp, offset=0, ap=[[1, 1], [1, H]])), inc="dS", inck=16)
                for e_i in range(E):
                    for jc in range(32):
                        kb.op("sync", lambda e, e_i=e_i, jc=jc: e.dma_start(
                            out=b1sb[:, e_i, jc:jc + 1],
                            in_=AP(tensor=d_b1, offset=e_i * HF + jc * 128, ap=[[1, 128], [1, 1]])), inc="dS", inck=16, unordered=True)
                kb.op("sync", lambda e: e.dma_start(out=wgf[:], in_=AP(tensor=d_wg, offset=0, ap=[[E, 128], [128 * E, 8], [1, E]])), inc="dS", inck=16)
                sld_small = kb.now("dS")
                b2_gld = kb.op("gpsimd", lambda e: e.dma_start(out=b2sb[:], in_=d_b2.ap()), inc="dB2", inck=16)

                for ci, cv in enumerate(CONSTS):
                    kb.op("vector", lambda e, ci=ci, cv=cv: e.memset(csb[:, ci:ci + 1], cv))
                imz = kb.op("vector", lambda e: e.memset(ident[:], 0.0))
                ob = kb.op("vector", lambda e: e.memset(onebf[:], 1.0))
                ident_gpc = 0
                for ii in range(128):
                    kb.op("sync", lambda e, ii=ii: e.dma_start(
                        out=AP(tensor=ident, offset=ii * 128 + ii, ap=[[128, 1], [1, 1]]),
                        in_=onebf[:]),
                        waits=[("dve", ob)], inc="dID", inck=16, unordered=True)
                ident_dma = kb.now("dID")
                pm = kb.op("vector", lambda e: e.memset(psw[:], 0.0), inc="dve")
                for (ob, ib) in ((0, 32), (32, 0), (64, 96), (96, 64)):
                    kb.op("sync", lambda e, ob=ob, ib=ib: e.dma_start(
                        out=psw[ib:ib + 32, ob:ob + 32], in_=ident[ib:ib + 32, ib:ib + 32]),
                        waits=[("dID", ident_dma), ("dve", pm)], inc="dSW", inck=16, unordered=True)
                psw_ld = kb.now("dSW")
                kb.op("vector", lambda e: e.memset(ones1f[:], 1.0), inc="dve")
                v16 = kb.op("vector", lambda e: e.memset(ones16[:], 1.0), inc="dve")
                for vh in range(H):
                    kb.op("vector", lambda e, vh=vh: e.tensor_copy(
                        out=AP(tensor=V, offset=vh * 65 + 64, ap=[[NT * H * 65, 128], [H * 65, NT], [1, 1]]),
                        in_=ones16[:]),
                        waits=[("dve", v16)], inc="dve" if vh == H - 1 else None)
                gpc_const = kb.now("gpc")

                _maybe_stop("s1")
                _pc_state = {"i": 0}
                W1_ELEMS = E * DM * HF
                PC_TOTAL = 2 * W1_ELEMS // (128 * 4096)

                def precast_pump(n):
                    for _ in range(n):
                        i = _pc_state["i"]
                        if i >= PC_TOTAL:
                            return
                        _pc_state["i"] = i + 1
                        if i < PC_TOTAL // 2:
                            src_t, dst_t, off = d_w1, d_w1b, i * 128 * 4096
                        else:
                            src_t, dst_t, off = d_w2, d_w2b, (i - PC_TOTAL // 2) * 128 * 4096
                        kb.op("gpsimd", lambda e, src_t=src_t, dst_t=dst_t, off=off: e.dma_start(
                            out=AP(tensor=dst_t, offset=off, ap=[[4096, 128], [1, 4096]]),
                            in_=AP(tensor=src_t, offset=off, ap=[[4096, 128], [1, 4096]])),
                            inc="dWCo", inck=16)

                qkT_sld = [0, 0]
                qk_evs = [0, 0]
                for rnd in range(4):
                    gld_qkvw = kb.op("gpsimd", lambda e, rnd=rnd: e.dma_start(
                        out=qkvwb[:], in_=d_qkvw.ap()[rnd * 6 * 128:(rnd + 1) * 6 * 128, :].rearrange("(c p) d -> p c d", p=128)),
                        waits=[("pe", kb.now("pe") if rnd else 0)], inc="dQW", inck=16)
                    for cc12 in range(6):
                        cc = rnd * 6 + cc12
                        par = cc % 2
                        for dc in range(8):
                            kb.op("tensor", lambda e, cc12=cc12, dc=dc, par=par: e.transpose(P6[:, par, dc], qkvwb[:, cc12, dc * 128:(dc + 1) * 128], ident[:]),
                                  waits=[("dQW", gld_qkvw), ("dID", ident_dma), ("dve", qk_evs[par])],
                                  inc="pe" if dc == 7 else None)
                        if cc < 16:
                            qk_evs[par] = kb.op("vector", lambda e, par=par: e.tensor_copy(out=tstage[:, par], in_=P6[:, par]),
                                  waits=[("pe", kb.now("pe")), ("dTS", qkT_sld[par])], inc="dve")
                            qkT_sld[par] = kb.op("sync", lambda e, cc=cc, par=par: e.dma_start(
                                out=AP(tensor=d_qkT, offset=cc * 128, ap=[[2 * DM, 128], [128 * 2 * DM, 8], [1, 128]]),
                                in_=tstage[:, par]),
                                waits=[("dve", kb.now("dve"))], inc="dTS", inck=16)
                        else:
                            qk_evs[par] = kb.op("vector", lambda e, cc=cc, par=par: e.tensor_copy(
                                out=AP(tensor=qkvwTv, offset=(cc - 16) * 128, ap=[[8 * DM, 128], [DM, 8], [1, 128]]),
                                in_=P6[:, par]),
                                waits=[("pe", kb.now("pe"))], inc="dve")
                qkT_store_sld = kb.now("dTS")
                kb.op("gpsimd", lambda e: e.iota(posrow[:], pattern=[[1, S]], base=0, channel_multiplier=0, allow_small_or_imprecise_dtypes=True), inc="gpc")
                kb.op("gpsimd", lambda e: e.iota(sgni[:], pattern=[[1, 1]], base=0, channel_multiplier=1), inc="gpc")
                kb.op("gpsimd", lambda e: e.iota(bandabs[:], pattern=[[-1, 4096]], base=1920, channel_multiplier=1, allow_small_or_imprecise_dtypes=True), inc="gpc")
                iota_gpc = kb.now("gpc")
                precast_pump(4)
                _maybe_stop("s3")
                xT_evs = [qk_evs[0], qk_evs[1]]
                xc_dve = [0, 0]
                for tt in range(NT):
                    gld_x = kb.op("sync", lambda e, tt=tt: e.dma_start(out=xbf[:, tt % 2], in_=d_x.ap()[tt * 128:(tt + 1) * 128, :]),
                                  waits=[("dve", xc_dve[tt % 2])], inc="dXB", inck=16)
                    xc_dve[tt % 2] = kb.op("vector", lambda e, tt=tt: e.tensor_copy(out=xb[:, tt % 2], in_=xbf[:, tt % 2]),
                                           waits=[("dXB", gld_x), ("pe", kb.now("pe") if tt >= 2 else 0)], inc="dve")
                    for c in range(8):
                        kb.op("tensor", lambda e, tt=tt, c=c: e.transpose(P6[:, tt % 2, c], xb[:, tt % 2, c * 128:(c + 1) * 128], ident[:]),
                              waits=[("dve", xc_dve[tt % 2]), ("dID", ident_dma), ("dve", xT_evs[tt % 2])],
                              inc="pe" if c == 7 else None)
                    xT_evs[tt % 2] = kb.op("vector", lambda e, tt=tt: e.tensor_copy(
                        out=AP(tensor=xT, offset=tt * 128, ap=[[8 * S, 128], [S, 8], [1, 128]]),
                        in_=P6[:, tt % 2]),
                        waits=[("pe", kb.now("pe"))], inc="dve")
                xT_dve = kb.now("dve")
                _maybe_stop("s2")


                for tt in range(NT):
                    for hf in range(2):
                        for d in range(8):
                            kb.op("tensor", lambda e, tt=tt, hf=hf, d=d: e.matmul(
                                P4, xT[:, d, tt * 128:(tt + 1) * 128],
                                qkvwTv[:, d, hf * 512:(hf + 1) * 512],
                                start=(d == 0), stop=(d == 7)),
                                waits=[("dve", xT_dve if (tt == 0 and hf == 0) else kb.now("dve"))],
                                inc="pe" if d == 7 else None)
                        pv = kb.now("pe")
                        kb.op("vector", lambda e, tt=tt, hf=hf: e.tensor_copy(
                            out=AP(tensor=V, offset=tt * H * 65 + hf * 8 * 65, ap=[[NT * H * 65, 128], [65, 8], [1, 64]]),
                            in_=AP(tensor=P4t, offset=0, ap=[[1024, 128], [64, 8], [1, 64]])),
                            waits=[("pe", pv)], inc="dve")
                V_dve = kb.now("dve")
                precast_pump(8)
                _maybe_stop("s4")


                kb.op("vector", lambda e: e.tensor_copy(out=wghi[:], in_=wgf[:]), waits=[("dS", sld_small)], inc="dve")
                kb.op("vector", lambda e: e.scalar_tensor_tensor(out=wglo[:], in0=wghi[:], scalar=-1.0, in1=wgf[:], op0=ALU.mult, op1=ALU.add), inc="dve")

                with contextlib.ExitStack() as st2:
                    freq4 = sb(st2, "freq4", [128, 1], F32)
                    angs = sb(st2, "angs", [128, S], F32)
                    angred = sb(st2, "angred", [128, S], F32)
                    angi = sb(st2, "angi", [128, S], I32)
                    sgnf = sb(st2, "sgnf", [128, 1], F32)
                    sgncol = sb(st2, "sgncol", [128, 1], F32)
                    sgt = sb(st2, "sgt", [128, S], F32)

                    kb.op("sync", lambda e: e.dma_start(out=freq4[:], in_=AP(tensor=d_freqs, offset=0, ap=[[0, 4], [1, 32], [1, 1]])), inc="dS", inck=16)
                    sld_freq = kb.now("dS")
                    gpc_rope = iota_gpc
                    for qt in range(4):
                        kb.op("tensor", lambda e, qt=qt: e.matmul(P4, ones1f[:], posrow[:, qt * 512:(qt + 1) * 512], start=True, stop=True),
                              waits=[("gpc", gpc_rope), ("dve", kb.now("dve")), ("dS", sld_freq)], inc="pe")
                        kb.op("vector", lambda e, qt=qt: e.tensor_scalar(out=angs[:, qt * 512:(qt + 1) * 512], in0=P4, scalar1=freq4[:], scalar2=None, op0=ALU.mult),
                              waits=[("pe", kb.now("pe"))], inc="dve")
                    kb.op("vector", lambda e: e.tensor_scalar(out=angred[:], in0=angs[:], scalar1=1.0 / TWO_PI, scalar2=None, op0=ALU.mult), inc="dve")
                    kb.op("vector", lambda e: e.tensor_copy(out=angi[:], in_=angred[:]), inc="dve")
                    kb.op("vector", lambda e: e.tensor_copy(out=angred[:], in_=angi[:]), inc="dve")
                    kb.op("vector", lambda e: e.scalar_tensor_tensor(out=angred[:], in0=angred[:], scalar=-TWO_PI, in1=angs[:], op0=ALU.mult, op1=ALU.add), inc="dve")
                    kb.op("vector", lambda e: e.tensor_scalar(out=sgt[:], in0=angred[:], scalar1=float(np.pi), scalar2=-TWO_PI, op0=ALU.is_gt, op1=ALU.mult), inc="dve")
                    kb.op("vector", lambda e: e.tensor_add(out=angred[:], in0=angred[:], in1=sgt[:]), inc="dve")
                    kb.op("vector", lambda e: e.tensor_scalar(out=sgt[:], in0=angred[:], scalar1=float(-np.pi), scalar2=TWO_PI, op0=ALU.is_lt, op1=ALU.mult), inc="dve")
                    kb.op("vector", lambda e: e.tensor_add(out=angred[:], in0=angred[:], in1=sgt[:]), inc="dve")
                    dve_ang = kb.now("dve")
                    act_sin = kb.op("scalar", lambda e: e.activation(out=S4[:], in_=angred[:], func=AF.Sin), waits=[("dve", dve_ang)], inc="act")
                    aab = kb.op("scalar", lambda e: e.activation(out=angs[:], in_=angred[:], func=AF.Abs),
                                waits=[("act", act_sin), ("dve", dve_ang)], inc="act")
                    kb.op("scalar", lambda e: e.activation(out=C4[:], in_=angs[:], func=AF.Sin, bias=HALF_PI, scale=-1.0), waits=[("act", aab)], inc="act")
                    kb.op("vector", lambda e: e.tensor_scalar(out=sgni[:], in0=sgni[:], scalar1=32, scalar2=None, op0=ALU.bitwise_and), waits=[("gpc", gpc_rope)], inc="dve")
                    kb.op("vector", lambda e: e.tensor_copy(out=sgnf[:], in_=sgni[:]), inc="dve")
                    kb.op("vector", lambda e: e.tensor_scalar(out=sgncol[:], in0=sgnf[:], scalar1=1.0 / 16.0, scalar2=-1.0, op0=ALU.mult, op1=ALU.add), inc="dve")
                    kb.op("vector", lambda e: e.tensor_scalar(out=S4[:], in0=S4[:], scalar1=sgncol[:], scalar2=None, op0=ALU.mult),
                          waits=[("act", kb.now("act"))], inc="dve")

                    kb.op("tensor", lambda e: e.matmul(P3[:, 0:H], ones1f[:], bparow[:, 0], start=True, stop=True),
                          waits=[("dS", sld_small), ("dve", kb.now("dve"))], inc="pe")
                    kb.op("vector", lambda e: e.tensor_scalar(out=paac[:], in0=P3[:, 0:H], scalar1=0.01, scalar2=None, op0=ALU.max),
                          waits=[("pe", kb.now("pe"))], inc="dve")
                    kb.op("tensor", lambda e: e.matmul(P3[:, 0:H], ones1f[:], bparow[:, 1], start=True, stop=True),
                          waits=[("dve", kb.now("dve"))], inc="pe")
                    kb.op("vector", lambda e: e.tensor_scalar(out=npac[:], in0=P3[:, 0:H], scalar1=0.01, scalar2=-1.0, op0=ALU.max, op1=ALU.mult),
                          waits=[("pe", kb.now("pe"))], inc="dve")

                _maybe_stop("s5")
                kb.fence()
                with contextlib.ExitStack() as st3:
                    bandtmp = sb(st3, "bandtmp", [128, 2, 4096], BF16)
                    bandst = sb(st3, "bandst", [128, 2, 4096], BF16)

                    kb.op("scalar", lambda e: e.activation(out=bandabs[:], in_=bandabs[:], func=AF.Abs),
                          waits=[("gpc", iota_gpc)], inc="act")
                    babs_act = kb.now("act")
                    babs_dve = kb.now("dve")
                    bst_st = [0, 0]
                    bsc_mark = [0, 0]
                    for h in range(H):
                        a_ln = kb.op("scalar", lambda e, h=h: e.activation(out=bandtmp[:, h % 2], in_=bandabs[:], func=AF.Ln, bias=1.0, scale=paac[:, h:h + 1]),
                                     waits=[("act", babs_act), ("dve", max(babs_dve, bsc_mark[h % 2]))], inc="act")
                        d_sc = kb.op("vector", lambda e, h=h: e.tensor_scalar(out=bandst[:, h % 2], in0=bandtmp[:, h % 2], scalar1=npac[:, h:h + 1], scalar2=None, op0=ALU.mult),
                                     waits=[("act", a_ln), ("dBS", bst_st[h % 2])], inc="dve")
                        bsc_mark[h % 2] = d_sc
                        bst_st[h % 2] = kb.op("sync", lambda e, h=h: e.dma_start(out=d_band.ap()[h], in_=bandst[:, h % 2]),
                                              waits=[("dve", d_sc)], inc="dBS", inck=16)
                    band_sld = kb.now("dBS")
                setup_dve = kb.now("dve")
                setup_act = kb.now("act")
                kb.wait("tensor", ("dve", setup_dve), ("act", setup_act), ("dBS", kb.now("dBS")), ("dTS", kb.now("dTS")))
                setup_bar = kb.op("tensor", lambda e: e.nop(), inc="pe")
            _maybe_stop("setup")
            kb.fence()

            with contextlib.ExitStack() as at:
                wsl = sb(at, "wsl", [128, 4, 8, 256], BF16)
                qbf = sb(at, "qbf", [128, S], BF16)
                ctmp = sb(at, "ctmp", [128, S], F32)
                qr = sb(at, "qr", [128, S], BF16)
                kr = sb(at, "kr", [128, S], BF16)
                Pp = sb(at, "Pp", [128, 2, 1024], BF16)
                araw = sb(at, "araw", [64, 2, S], BF16)
                aout = sb(at, "aout", [64, S], BF16)
                bandbf = sb(at, "bandbf", [128, 2, 4096], BF16)
                zasm = sb(at, "zasm", [4, 2, 512], F32)
                rzasm = sb(at, "rzasm", [4, 512], F32)
                rzflat = sb(at, "rzflat", [1, 2, 4, 512], F32)
                ztmp = sb(at, "ztmp", [1, 2, 512], F32)

                psS = (P1, P2)
                psV_banks = (P3, P4t)

                exp_mark = [0, 0]
                kb_mark = [0, 0]
                zrec_mark = 0
                psv_z_p = [0, 0]
                psv_dve_p = [0, 0]
                araw_free_p = [0, 0]
                pending_tail = []
                nonlocal_state = {"zrec_mark": 0, "aout_dma": 0}
                ztail_pe_p = [0, 0]
                rz_ld_p = [0, 0]

                qkv_done = [0, 0, 0, 0]
                swp_war = [0, 0]
                qbf_war = [0, 0, 0, 0]
                for p in range(H // 2):
                    precast_pump(15)
                    war = qkv_done[p % 4]
                    w_ld = kb.op("sync", lambda e, p=p: e.dma_start(
                        out=wsl[:, p % 4, :, 0:128],
                        in_=AP(tensor=d_qkT, offset=p * 128, ap=[[2 * DM, 128], [128 * 2 * DM, 8], [1, 128]])),
                        waits=[("pe", war), ("dTS", qkT_store_sld)], inc="dWSL", inck=16)
                    w_ld = kb.op("sync", lambda e, p=p: e.dma_start(
                        out=wsl[:, p % 4, :, 128:256],
                        in_=AP(tensor=d_qkT, offset=DM + p * 128, ap=[[2 * DM, 128], [128 * 2 * DM, 8], [1, 128]])),
                        inc="dWSL", inck=16)
                    for hh in range(2):
                        h = 2 * p + hh
                        kb.op("sync", lambda e, h=h: e.dma_start(out=bandbf[:, h % 2], in_=d_band.ap()[h]),
                              waits=[("dBS", band_sld), ("pe", kb_mark[h % 2])], inc="dBBF", inck=16)
                    band_ld = kb.now("dBBF")

                    for w, dst, dsc in ((0, qr, SCALE), (1, kr, 1.0)):
                        evac_marks = {}
                        for qt in range(4):
                            g = (w * 4 + qt) % 2
                            psq = P1[:, 0:512] if g == 0 else P2[:, 0:512]
                            pssw = P1[:, 512:1024] if g == 0 else P2[:, 512:1024]
                            for d in range(8):
                                kb.op("tensor", lambda e, p=p, w=w, qt=qt, d=d, psq=psq: e.matmul(
                                    psq, wsl[:, p % 4, d, w * 128:(w + 1) * 128],
                                    xT[:, d, qt * 512:(qt + 1) * 512],
                                    start=(d == 0), stop=(d == 7)),
                                    waits=[("dWSL", w_ld), ("pe", setup_bar),
                                           ("act", max(exp_mark[0], exp_mark[1], evac_marks.get(g, 0)))],
                                    inc="pe" if d == 7 else None)
                            ev = kb.op("scalar", lambda e, qt=qt, dsc=dsc, psq=psq: e.activation(
                                out=qbf[:, qt * 512:(qt + 1) * 512], in_=psq, func=AF.Copy, scale=dsc),
                                waits=[("pe", kb.now("pe")), ("dve", qbf_war[qt])], inc="act")
                            evac_marks[g] = ev
                            sm = kb.op("tensor", lambda e, qt=qt, pssw=pssw: e.matmul(
                                pssw, psw[:], qbf[:, qt * 512:(qt + 1) * 512], start=True, stop=True),
                                waits=[("act", ev), ("dSW", psw_ld), ("dve", swp_war[g])], inc="pe")
                            qbf_war[qt] = kb.op("vector", lambda e, qt=qt: e.tensor_mul(
                                out=ctmp[:, qt * 512:(qt + 1) * 512], in0=qbf[:, qt * 512:(qt + 1) * 512],
                                in1=C4[:, qt * 512:(qt + 1) * 512]),
                                waits=[("act", ev)], inc="dve")
                            kb.op("vector", lambda e, qt=qt, dst=dst, pssw=pssw: e.tensor_mul(
                                out=dst[:, qt * 512:(qt + 1) * 512], in0=pssw,
                                in1=S4[:, qt * 512:(qt + 1) * 512]),
                                waits=[("pe", sm)], inc="dve")
                            swp_war[g] = kb.op("vector", lambda e, qt=qt, dst=dst: e.tensor_add(
                                out=dst[:, qt * 512:(qt + 1) * 512], in0=dst[:, qt * 512:(qt + 1) * 512],
                                in1=ctmp[:, qt * 512:(qt + 1) * 512]), inc="dve")
                    rope_dve = kb.now("dve")
                    qkv_done[p % 4] = kb.now("pe")

                    for hh in range(2):
                        h = 2 * p + hh
                        hb = hh * 64
                        bpar = h % 2
                        vpar = h % 2
                        PV = psV_banks[vpar]
                        for qh in range(2):
                            q0 = qh * 1024
                            evs = [0] * 16

                            def emit_attnV(kt, h=h, PV=PV, vpar=vpar):
                                for j in range(2):
                                    kb.op("tensor", lambda e, kt=kt, h=h, j=j, PV=PV: e.matmul(
                                        PV[0:65, j * 512:(j + 1) * 512], V[:, kt, h * 65:(h + 1) * 65],
                                        Pp[:, kt % 2, j * 512:(j + 1) * 512],
                                        start=(kt == 0), stop=(kt == 15)),
                                        waits=[("act", evs[kt])] +
                                              ([("dZA", psv_z_p[vpar]), ("dve", psv_dve_p[vpar])] if kt == 0 else []),
                                        inc="pe" if (j == 1 and kt == 15) else None)

                            for kt in range(16):
                                par = kt % 2
                                for j in range(2):
                                    kb.op("tensor", lambda e, hb=hb, kt=kt, q0=q0, j=j, par=par: e.matmul(
                                        psS[par][:, j * 512:(j + 1) * 512],
                                        kr[hb:hb + 64, kt * 128:(kt + 1) * 128],
                                        qr[hb:hb + 64, q0 + j * 512:q0 + (j + 1) * 512],
                                        start=True, stop=False),
                                        waits=[("dve", rope_dve), ("dBBF", band_ld),
                                               ("act", exp_mark[par])])
                                for j in range(2):
                                    u0 = q0 + j * 512 + 1920 - kt * 128
                                    r = kb.op("tensor", lambda e, par=par, bpar=bpar, u0=u0, j=j: e.matmul(
                                        psS[par][:, j * 512:(j + 1) * 512],
                                        ident[:], bandbf[:, bpar, u0:u0 + 512],
                                        start=False, stop=True),
                                        inc="pe" if j == 1 else None)
                                    if r is not None:
                                        kb_mark[bpar] = r
                                sdone = kb.now("pe")
                                if kt >= 1:
                                    emit_attnV(kt - 1)
                                if qh == 0 and kt == 8 and pending_tail:
                                    pending_tail.pop(0)()
                                exp_mark[par] = kb.op("scalar", lambda e, par=par: e.activation(
                                    out=Pp[:, par, :], in_=psS[par][:], func=AF.Exp),
                                    waits=[("pe", sdone)], inc="act")
                                evs[kt] = exp_mark[par]
                            emit_attnV(15)
                            pv_done = kb.now("pe")
                            for j in range(2):
                                kb.op("vector", lambda e, q0=q0, j=j, PV=PV, vpar=vpar: e.tensor_copy(
                                    out=araw[:, vpar, q0 + j * 512:q0 + (j + 1) * 512], in_=PV[0:64, j * 512:(j + 1) * 512]),
                                    waits=[("pe", pv_done), ("dve", araw_free_p[vpar])], inc="dve")
                                za = kb.op("vector", lambda e, j=j, PV=PV: e.tensor_copy(
                                    out=ztmp[:, j, :], in_=PV[64:65, j * 512:(j + 1) * 512]),
                                    waits=[("pe", pv_done), ("dZA", kb.now("dZA"))], inc="dve")
                                psv_z_p[vpar] = kb.op("sync", lambda e, qh=qh, j=j, vpar=vpar: e.dma_start(
                                    out=zasm[2 * qh + j:2 * qh + j + 1, vpar, :], in_=ztmp[:, j, :]),
                                    waits=[("dve", za)], inc="dZA", inck=16)
                        araw_act = kb.now("dve")
                        nonlocal_state["zrec_mark"] = kb.op("vector",
                            lambda e, vpar=vpar: e.reciprocal(out=rzasm[:], in_=zasm[:, vpar, :]),
                            waits=[("dZA", psv_z_p[vpar]), ("dRZ", kb.now("dRZ")), ("pe", ztail_pe_p[vpar])], inc="dve")
                        rz_ld_p[vpar] = kb.op("sync", lambda e, vpar=vpar: e.dma_start(
                            out=rzflat[0:1, vpar], in_=rzasm[:]),
                            waits=[("dve", nonlocal_state["zrec_mark"])], inc="dRZ", inck=16)

                        def make_tail(h=h, vpar=vpar, PV=PV, araw_act=araw_act, rz_ld=None):
                            rz_ld = rz_ld_p[vpar]

                            def tail():
                                pd_prev = araw_act
                                for qj in range(4):
                                    kb.op("tensor", lambda e, qj=qj, PV=PV, vpar=vpar: e.matmul(
                                        PV[0:64, 0:512], ones1f[:, 0:64], rzflat[0:1, vpar, qj, :], start=True, stop=True),
                                        waits=[("dRZ", rz_ld), ("dve", pd_prev)], inc="pe")
                                    pd = kb.op("vector", lambda e, qj=qj, PV=PV, vpar=vpar: e.tensor_tensor(
                                        out=aout[:, qj * 512:(qj + 1) * 512],
                                        in0=araw[:, vpar, qj * 512:(qj + 1) * 512], in1=PV[0:64, 0:512], op=ALU.mult),
                                        waits=[("pe", kb.now("pe")), ("dAO", nonlocal_state["aout_dma"])], inc="dve")
                                    psv_dve_p[vpar] = pd
                                    pd_prev = pd
                                    nonlocal_state["aout_dma"] = kb.op("sync", lambda e, h=h, qj=qj: e.dma_start(
                                        out=d_aT.ap()[h * 64:(h + 1) * 64, qj * 512:(qj + 1) * 512],
                                        in_=aout[:, qj * 512:(qj + 1) * 512]),
                                        waits=[("dve", pd)], inc="dAO", inck=16)
                                araw_free_p[vpar] = psv_dve_p[vpar]
                                ztail_pe_p[vpar] = kb.now("pe")
                            return tail

                        pending_tail.append(make_tail())
                while pending_tail:
                    pending_tail.pop(0)()
                precast_pump(10**9)
                att_sld = kb.now("dAO")
                kb.wait("tensor", ("dAO", att_sld), ("dZA", kb.now("dZA")), ("dRZ", kb.now("dRZ")), ("dSW", kb.now("dSW")), ("dve", kb.now("dve")), ("act", kb.now("act")))
                att_bar = kb.op("tensor", lambda e: e.nop(), inc="pe")
            att_era.close()
            _maybe_stop("attention")
            kb.fence()

            with contextlib.ExitStack() as pr:
                m_all = sb(pr, "m_all", [128, NT, E], F32)
                idxall = sb(pr, "idxall", [128, E, 8], I32)
                mT8 = sb(pr, "mT8", [8, S], BF16)
                x1s = es.enter_context(contextlib.ExitStack())
                x1 = sb(x1s, "x1", [128, NT, DM], F32)
                with contextlib.ExitStack() as pro:
                    outwb = sb(pro, "outwb", [128, 8, DM], BF16)
                    outwT = sb(pro, "outwT", [128, 8, DM], BF16)
                    aoT = sb(pro, "aoT", [128, 2, 8, 128], BF16)
                    xres = sb(pro, "xres", [128, 2, DM], F32)

                    gld_outw = kb.op("gpsimd", lambda e: e.dma_start(out=outwb[:], in_=d_outw.ap().rearrange("(c p) d -> p c d", p=128)),
                                     waits=[("pe", att_bar)], inc="dOW", inck=16)
                    ow_evs = [0, 0]
                    for cc in range(8):
                        for dc in range(8):
                            kb.op("tensor", lambda e, cc=cc, dc=dc: e.transpose(P6[:, cc % 2, dc], outwb[:, cc, dc * 128:(dc + 1) * 128], ident[:]),
                                  waits=[("dOW", gld_outw), ("dve", ow_evs[cc % 2])],
                                  inc="pe" if dc == 7 else None)
                        ow_evs[cc % 2] = kb.op("vector", lambda e, cc=cc: e.tensor_copy(
                            out=AP(tensor=outwT, offset=cc * 128, ap=[[8 * DM, 128], [DM, 8], [1, 128]]),
                            in_=P6[:, cc % 2]),
                            waits=[("pe", kb.now("pe"))], inc="dve")
                    owt_dve = kb.now("dve")

                    for tt in range(NT):
                        war = kb.now("pe") if tt >= 2 else 0
                        ld1 = kb.op("sync", lambda e, tt=tt: e.dma_start(
                            out=aoT[:, tt % 2],
                            in_=AP(tensor=d_aT, offset=tt * 128, ap=[[S, 128], [128 * S, 8], [1, 128]])),
                            waits=[("dAO", att_sld), ("pe", war)], inc="dAOT", inck=16)
                        war2 = kb.now("dve") if tt >= 2 else 0
                        ld2 = kb.op("sync", lambda e, tt=tt: e.dma_start(out=xres[:, tt % 2], in_=d_x.ap()[tt * 128:(tt + 1) * 128, :]),
                                    waits=[("dve", war2)], inc="dXR", inck=16)
                        for dd in range(2):
                            for c in range(8):
                                kb.op("tensor", lambda e, tt=tt, dd=dd, c=c: e.matmul(
                                    P1[:, dd * 512:(dd + 1) * 512], aoT[:, tt % 2, c, :],
                                    outwT[:, c, dd * 512:(dd + 1) * 512],
                                    start=(c == 0), stop=(c == 7)),
                                    waits=[("dAOT", ld1), ("dve", owt_dve), ("dve", kb.now("dve"))],
                                    inc="pe" if c == 7 else None)
                        mm_tt = kb.now("pe")
                        for dd in range(2):
                            kb.op("vector", lambda e, tt=tt, dd=dd: e.tensor_add(
                                out=x1[:, tt, dd * 512:(dd + 1) * 512], in0=xres[:, tt % 2, dd * 512:(dd + 1) * 512],
                                in1=P1[:, dd * 512:(dd + 1) * 512]),
                                waits=[("pe", mm_tt), ("dXR", ld2)], inc="dve")
                        kb.op("vector", lambda e, tt=tt: e.reduce_sum(out=meancol[:], in_=x1[:, tt, :], axis=AX.X), inc="dve")
                        kb.op("vector", lambda e: e.tensor_scalar(out=meancol[:], in0=meancol[:], scalar1=1.0 / DM, scalar2=None, op0=ALU.mult), inc="dve")
                        kb.op("vector", lambda e, tt=tt: e.tensor_scalar(out=x1[:, tt, :], in0=x1[:, tt, :], scalar1=meancol[:], scalar2=None, op0=ALU.subtract), inc="dve")
                        kb.op("scalar", lambda e, tt=tt: e.activation(out=sqt[:], in_=x1[:, tt, :], func=AF.Square, accum_out=varall[:, tt:tt + 1]),
                              waits=[("dve", kb.now("dve")), ("act", kb.now("act"))], inc="act")
                kb.op("vector", lambda e: e.tensor_scalar(out=varall[:], in0=varall[:], scalar1=1.0 / DM, scalar2=None, op0=ALU.mult),
                      waits=[("act", kb.now("act"))], inc="dve")
                a1 = kb.op("scalar", lambda e: e.activation(out=rstd[:], in_=varall[:], func=AF.Ln, bias=1e-5),
                           waits=[("dve", kb.now("dve"))], inc="act")
                kb.op("scalar", lambda e: e.activation(out=rstd[:], in_=rstd[:], func=AF.Exp, scale=-0.5),
                      waits=[("act", a1)], inc="act")
                ln1_act = kb.now("act")
                for tt in range(NT):
                    kb.op("vector", lambda e, tt=tt: e.tensor_scalar(out=x1[:, tt, :], in0=x1[:, tt, :], scalar1=rstd[:, tt:tt + 1], scalar2=None, op0=ALU.mult),
                          waits=[("act", ln1_act)], inc="dve")
                x1_dve = kb.now("dve")

                _maybe_stop("proj")
                kb.fence()
                with contextlib.ExitStack() as gt:
                    x1T = sb(gt, "x1T", [128, 8, S], BF16)
                    x1loT = sb(gt, "x1loT", [128, 8, S], BF16)
                    x1b = sb(gt, "x1b", [128, 4, DM16], BF16)
                    x1lo = sb(gt, "x1lo", [128, 2, DM], BF16)
                    gsb = sb(gt, "gsb", [128, E], F32)
                    mx = sb(gt, "mx", [128, 1], F32)
                    nmx = sb(gt, "nmx", [128, 1], F32)
                    eq = sb(gt, "eq", [128, E], F32)
                    cnt2 = sb(gt, "cnt2", [128, 1], F32)
                    gwo = sb(gt, "gwo", [128, E], F32)
                    second = sb(gt, "second", [128, 1], F32)
                    c2 = sb(gt, "c2", [128, 1], F32)
                    c2i = sb(gt, "c2i", [128, 1], F32)
                    kthv = sb(gt, "kthv", [128, 1], F32)
                    ktha = sb(gt, "ktha", [128, 1], F32)
                    kthb = sb(gt, "kthb", [128, 1], F32)
                    selt = sb(gt, "selt", [128, E], F32)
                    emt = sb(gt, "emt", [128, E], F32)
                    z8 = sb(gt, "z8", [128, 1], F32)
                    rz8 = sb(gt, "rz8", [128, 1], F32)
                    mbf = sb(gt, "mbf", [128, E], BF16)
                    utri = sb(gt, "utri", [128, 128], F32)
                    eoff = sb(gt, "eoff", [128, E], F32)
                    tokf = sb(gt, "tokf", [128, 1], F32)
                    onescol = sb(gt, "onescol", [128, 1], F32)
                    selt2 = sb(gt, "selt2", [128, E], F32)
                    possb = sb(gt, "possb", [128, E], F32)
                    carrow = sb(gt, "carrow", [1, E], F32)
                    totrow = sb(gt, "totrow", [1, E], F32)
                    slf = sb(gt, "slf", [128, E], F32)
                    slint = sb(gt, "slint", [128, 2, 2], I32)
                    tokint = sb(gt, "tokint", [128, 2, 1], I32)
                    ecolB = sb(gt, "ecolB", [128, E], F32)
                    a1sb = sb(gt, "a1sb", [128, E], F32)
                    ohsb = sb(gt, "ohsb", [128, E], F32)
                    tsel = sb(gt, "tsel", [128, E], F32)
                    e1c = sb(gt, "e1c", [128, 1], F32)
                    slout = sb(gt, "slout", [128, 2], F32)
                    mtmp = sb(gt, "mtmp", [128, E], F32)
                    tokft = sb(gt, "tokft", [128, 1], F32)
                    idfill = sb(gt, "idfill", [128, E * CAP // 128], I32)
                    selt3 = sb(gt, "selt3", [128, E], F32)
                    zrowx = sb(gt, "zrowx", [1, DM16], BF16)

                    kb.op("gpsimd", lambda e: e.iota(utri[:], pattern=[[1, 128]], base=0, channel_multiplier=-1, allow_small_or_imprecise_dtypes=True), inc="gpc")
                    kb.op("gpsimd", lambda e: e.iota(eoff[:], pattern=[[CAP, E]], base=0, channel_multiplier=0, allow_small_or_imprecise_dtypes=True), inc="gpc")
                    kb.op("gpsimd", lambda e: e.iota(ecolB[:], pattern=[[1, E]], base=int(BIGF), channel_multiplier=0, allow_small_or_imprecise_dtypes=True), inc="gpc")
                    kb.op("gpsimd", lambda e: e.iota(tokf[:], pattern=[[1, 1]], base=0, channel_multiplier=1, allow_small_or_imprecise_dtypes=True), inc="gpc")
                    gidx_gpc = kb.now("gpc")
                    kb.op("vector", lambda e: e.tensor_scalar(out=utri[:], in0=utri[:], scalar1=0.5, scalar2=None, op0=ALU.is_ge),
                          waits=[("gpc", gidx_gpc)], inc="dve")
                    kb.op("vector", lambda e: e.memset(onescol[:], 1.0), inc="dve")
                    idf = kb.op("vector", lambda e: e.memset(idfill[:], S), inc="dve")
                    gcon_dve = kb.now("dve")
                    kb.op("sync", lambda e: e.dma_start(
                        out=AP(tensor=d_idx, offset=0, ap=[[E * CAP // 128, 128], [1, E * CAP // 128]]),
                        in_=idfill[:]),
                        waits=[("dve", idf)], inc="dIDF", inck=16)
                    zr2 = kb.op("vector", lambda e: e.memset(zrowx[:], 0.0), inc="dve")
                    kb.op("sync", lambda e: e.dma_start(out=d_x1b.ap()[S:S + 1, :], in_=zrowx[:]),
                          waits=[("dve", zr2)], inc="dX1B", inck=16)

                    xb_free = [0, 0, 0, 0]
                    x1b_st = [0, 0, 0, 0]
                    x1T_ev = 0
                    x1loT_ev = 0
                    for tt in range(NT):
                        ca = kb.op("scalar", lambda e, tt=tt: e.activation(out=x1b[:, tt % 4, 0:DM], in_=x1[:, tt, :], func=AF.Copy),
                                   waits=[("dve", x1_dve), ("pe", xb_free[tt % 4]), ("dX1B", x1b_st[tt % 4])], inc="act")
                        cd = kb.op("vector", lambda e, tt=tt: e.scalar_tensor_tensor(
                            out=x1lo[:, tt % 2], in0=x1b[:, tt % 4, 0:DM], scalar=-1.0, in1=x1[:, tt, :], op0=ALU.mult, op1=ALU.add),
                            waits=[("act", ca)], inc="dve")
                        for c in range(8):
                            kb.op("tensor", lambda e, tt=tt, c=c: e.transpose(P6[:, 0, c], x1b[:, tt % 4, c * 128:(c + 1) * 128], ident[:]),
                                  waits=[("act", ca), ("pe", setup_bar), ("dve", x1T_ev)],
                                  inc="pe" if c == 7 else None)
                        x1T_ev = kb.op("vector", lambda e, tt=tt: e.tensor_copy(
                            out=AP(tensor=x1T, offset=tt * 128, ap=[[8 * S, 128], [S, 8], [1, 128]]),
                            in_=P6[:, 0]),
                            waits=[("pe", kb.now("pe"))], inc="dve")
                        for c in range(8):
                            kb.op("tensor", lambda e, tt=tt, c=c: e.transpose(P6[:, 1, c], x1lo[:, tt % 2, c * 128:(c + 1) * 128], ident[:]),
                                  waits=[("dve", cd), ("dve", x1loT_ev)],
                                  inc="pe" if c == 7 else None)
                        x1loT_ev = kb.op("vector", lambda e, tt=tt: e.tensor_copy(
                            out=AP(tensor=x1loT, offset=tt * 128, ap=[[8 * S, 128], [S, 8], [1, 128]]),
                            in_=P6[:, 1]),
                            waits=[("pe", kb.now("pe"))], inc="dve")
                        xb_free[tt % 4] = kb.now("pe")
                        tp_dve = kb.now("dve")
                        for d in range(8):
                            kb.op("tensor", lambda e, tt=tt, d=d: e.matmul(
                                P3[:, 0:E], x1T[:, d, tt * 128:(tt + 1) * 128], wghi[:, d, :],
                                start=(d == 0), stop=False),
                                waits=[("dve", tp_dve)])
                            kb.op("tensor", lambda e, tt=tt, d=d: e.matmul(
                                P3[:, 0:E], x1T[:, d, tt * 128:(tt + 1) * 128], wglo[:, d, :],
                                start=False, stop=False))
                            kb.op("tensor", lambda e, tt=tt, d=d: e.matmul(
                                P3[:, 0:E], x1loT[:, d, tt * 128:(tt + 1) * 128], wghi[:, d, :],
                                start=False, stop=(d == 7)),
                                inc="pe" if d == 7 else None)
                        g_pe = kb.now("pe")
                        kb.op("vector", lambda e: e.tensor_copy(out=gsb[:], in_=P3[:, 0:E]), waits=[("pe", g_pe)], inc="dve")
                        kb.op("vector", lambda e: e.reduce_max(out=mx[:], in_=gsb[:], axis=AX.X), inc="dve")
                        kb.op("vector", lambda e: e.tensor_scalar(out=nmx[:], in0=mx[:], scalar1=-1.0, scalar2=None, op0=ALU.mult), inc="dve")
                        kb.op("vector", lambda e: e.tensor_scalar(out=eq[:], in0=gsb[:], scalar1=mx[:], scalar2=None, op0=ALU.is_equal), inc="dve")
                        kb.op("vector", lambda e: e.reduce_sum(out=cnt2[:], in_=eq[:], axis=AX.X), inc="dve")
                        kb.op("vector", lambda e: e.scalar_tensor_tensor(out=gwo[:], in0=eq[:], scalar=-1e9, in1=gsb[:], op0=ALU.mult, op1=ALU.add), inc="dve")
                        kb.op("vector", lambda e: e.reduce_max(out=second[:], in_=gwo[:], axis=AX.X), inc="dve")
                        kb.op("vector", lambda e: e.tensor_scalar(out=c2[:], in0=cnt2[:], scalar1=1.5, scalar2=None, op0=ALU.is_ge), inc="dve")
                        kb.op("vector", lambda e: e.tensor_scalar(out=c2i[:], in0=c2[:], scalar1=-1.0, scalar2=1.0, op0=ALU.mult, op1=ALU.add), inc="dve")
                        kb.op("vector", lambda e: e.tensor_mul(out=ktha[:], in0=c2[:], in1=mx[:]), inc="dve")
                        kb.op("vector", lambda e: e.tensor_mul(out=kthb[:], in0=c2i[:], in1=second[:]), inc="dve")
                        kb.op("vector", lambda e: e.tensor_add(out=kthv[:], in0=ktha[:], in1=kthb[:]), inc="dve")
                        kb.op("vector", lambda e: e.tensor_scalar(out=selt[:], in0=gsb[:], scalar1=kthv[:], scalar2=None, op0=ALU.is_ge), inc="dve")
                        ae = kb.op("scalar", lambda e: e.activation(out=emt[:], in_=gsb[:], func=AF.Exp, bias=nmx[:]),
                                   waits=[("dve", kb.now("dve"))], inc="act")
                        kb.op("vector", lambda e: e.tensor_mul(out=emt[:], in0=emt[:], in1=selt[:]), waits=[("act", ae)], inc="dve")
                        kb.op("vector", lambda e: e.reduce_sum(out=z8[:], in_=emt[:], axis=AX.X), inc="dve")
                        kb.op("vector", lambda e: e.reciprocal(out=rz8[:], in_=z8[:]), inc="dve")
                        m_dve = kb.op("vector", lambda e, tt=tt: e.tensor_scalar(out=m_all[:, tt, :], in0=emt[:], scalar1=rz8[:], scalar2=1.0 / E, op0=ALU.mult, op1=ALU.mult), inc="dve")
                        kb.op("vector", lambda e, tt=tt: e.tensor_copy(out=x1b[:, tt % 4, DM:DM + 8], in_=m_all[:, tt, :]), inc="dve")
                        kb.op("vector", lambda e, tt=tt: e.tensor_copy(out=mtmp[:], in_=x1b[:, tt % 4, DM:DM + 8]), inc="dve")
                        mlo_dve = kb.op("vector", lambda e, tt=tt: e.tensor_sub(out=x1b[:, tt % 4, DM + 8:DM + 16], in0=m_all[:, tt, :], in1=mtmp[:]), inc="dve")
                        x1b_st[tt % 4] = kb.op("sync", lambda e, tt=tt: e.dma_start(out=d_x1b.ap()[tt * 128:(tt + 1) * 128, :], in_=x1b[:, tt % 4]),
                              waits=[("act", ca), ("dve", mlo_dve), ("pe", kb.now("pe"))], inc="dX1B", inck=16)
                        kb.op("vector", lambda e, tt=tt: e.tensor_scalar(out=selt2[:], in0=m_all[:, tt, :], scalar1=0.0, scalar2=None, op0=ALU.is_gt),
                              waits=[("dve", gcon_dve)], inc="dve")
                        sel_dve = kb.now("dve")
                        if tt == 0:
                            cr = kb.op("vector", lambda e: e.memset(carrow[:], 0.0), inc="dve")
                            tot_ev_prev = 0
                        kb.op("tensor", lambda e: e.matmul(P3[:, 0:E], utri[:], selt2[:], start=True, stop=False),
                              waits=[("dve", sel_dve), ("pe", kb.now("pe"))])
                        pos_pe = kb.op("tensor", lambda e: e.matmul(P3[:, 0:E], ones1f[:], carrow[:], start=False, stop=True),
                                       waits=[("dve", cr)], inc="pe")
                        tot_pe = kb.op("tensor", lambda e: e.matmul(P4t[0:1, 512:512 + E], onescol[:], selt2[:],
                                                                    start=True, stop=True),
                                       waits=[("dve", max(cr, sel_dve, tot_ev_prev))], inc="pe")
                        tot_ev_prev = kb.op("vector", lambda e: e.tensor_copy(out=totrow[:], in_=P4t[0:1, 512:512 + E]),
                                            waits=[("pe", tot_pe)], inc="dve")
                        cr = kb.op("vector", lambda e: e.tensor_add(out=carrow[:], in0=carrow[:], in1=totrow[:]),
                                   waits=[("pe", pos_pe)], inc="dve")
                        pv2 = kb.op("vector", lambda e: e.tensor_copy(out=possb[:], in_=P3[:, 0:E]),
                                    waits=[("pe", pos_pe)], inc="dve")
                        kb.op("vector", lambda e: e.tensor_add(out=slf[:], in0=possb[:], in1=eoff[:]), inc="dve")
                        kb.op("vector", lambda e: e.tensor_scalar(out=slf[:], in0=slf[:], scalar1=BIGF, scalar2=None, op0=ALU.add), inc="dve")
                        kb.op("vector", lambda e: e.scalar_tensor_tensor(out=slf[:], in0=selt2[:], scalar=-BIGF, in1=slf[:], op0=ALU.mult, op1=ALU.add), inc="dve")
                        kb.op("vector", lambda e: e.tensor_scalar(out=selt3[:], in0=possb[:], scalar1=float(CAP) - 0.5, scalar2=BIGF, op0=ALU.is_ge, op1=ALU.mult), inc="dve")
                        kb.op("vector", lambda e: e.tensor_add(out=slf[:], in0=slf[:], in1=selt3[:]), inc="dve")
                        kb.op("vector", lambda e: e.scalar_tensor_tensor(out=a1sb[:], in0=selt2[:], scalar=-BIGF, in1=ecolB[:], op0=ALU.mult, op1=ALU.add), inc="dve")
                        kb.op("vector", lambda e: e.tensor_reduce(out=e1c[:], in_=a1sb[:], axis=AX.X, op=ALU.min), inc="dve")
                        kb.op("vector", lambda e: e.tensor_scalar(out=ohsb[:], in0=a1sb[:], scalar1=e1c[:], scalar2=None, op0=ALU.is_equal), inc="dve")
                        kb.op("vector", lambda e: e.tensor_mul(out=tsel[:], in0=slf[:], in1=ohsb[:]), inc="dve")
                        kb.op("vector", lambda e: e.reduce_sum(out=slout[:, 0:1], in_=tsel[:], axis=AX.X), inc="dve")
                        kb.op("vector", lambda e: e.scalar_tensor_tensor(out=a1sb[:], in0=ohsb[:], scalar=BIGF, in1=a1sb[:], op0=ALU.mult, op1=ALU.add), inc="dve")
                        kb.op("vector", lambda e: e.tensor_reduce(out=e1c[:], in_=a1sb[:], axis=AX.X, op=ALU.min), inc="dve")
                        kb.op("vector", lambda e: e.tensor_scalar(out=ohsb[:], in0=a1sb[:], scalar1=e1c[:], scalar2=None, op0=ALU.is_equal), inc="dve")
                        kb.op("vector", lambda e: e.tensor_mul(out=tsel[:], in0=slf[:], in1=ohsb[:]), inc="dve")
                        kb.op("vector", lambda e: e.reduce_sum(out=slout[:, 1:2], in_=tsel[:], axis=AX.X), inc="dve")
                        kb.op("vector", lambda e, tt=tt: e.tensor_scalar(out=slout[:, 0:1], in0=slout[:, 0:1], scalar1=float(E * CAP + tt * 2), scalar2=None, op0=ALU.min), inc="dve")
                        kb.op("vector", lambda e, tt=tt: e.tensor_scalar(out=slout[:, 1:2], in0=slout[:, 1:2], scalar1=float(E * CAP + tt * 2 + 1), scalar2=None, op0=ALU.min), inc="dve")
                        kb.op("vector", lambda e, tt=tt: e.tensor_copy(out=slint[:, tt % 2], in_=slout[:]),
                              waits=([("dIX%d" % (tt % 2), (tt // 2) * 2 * 16)] if tt >= 2 else []), inc="dve")
                        kb.op("vector", lambda e, tt=tt: e.tensor_scalar(out=tokft[:], in0=tokf[:], scalar1=float(tt * 128), scalar2=None, op0=ALU.add), inc="dve")
                        ti_dve = kb.op("vector", lambda e, tt=tt: e.tensor_copy(out=tokint[:, tt % 2], in_=tokft[:]), inc="dve")
                        for e_x in range(2):
                            kb.op("gpsimd", lambda e, e_x=e_x, tt=tt: e.indirect_dma_start(
                                out=d_idx.ap(),
                                out_offset=IndirectOffsetOnAxis(slint[:, tt % 2, e_x:e_x + 1], 0),
                                in_=tokint[:, tt % 2], in_offset=None),
                                waits=[("dve", ti_dve), ("dIDF", 16)], inc="dIX%d" % (tt % 2), inck=16, unordered=True)
                        kb.op("vector", lambda e, tt=tt: e.tensor_copy(out=mbf[:], in_=m_all[:, tt, :]), inc="dve")
                        mb_dve = kb.now("dve")
                        kb.op("tensor", lambda e, tt=tt: e.transpose(P6[0:8, 0, 0], mbf[:], ident[:]),
                              waits=[("dve", mb_dve)], inc="pe")
                        kb.op("vector", lambda e, tt=tt: e.tensor_copy(out=mT8[:, tt * 128:(tt + 1) * 128], in_=P6[0:8, 0, 0]),
                              waits=[("pe", kb.now("pe"))], inc="dve")
                        x1T_ev = kb.now("dve")
                    mask_dve = kb.now("dve")

                    b2ev = 0
                    for tt in range(NT):
                        for dd in range(2):
                            bm = kb.op("tensor", lambda e, tt=tt, dd=dd: e.matmul(
                                P4, mT8[:, tt * 128:(tt + 1) * 128], b2sb[:, dd * 512:(dd + 1) * 512],
                                start=True, stop=True),
                                waits=[("dve", max(mask_dve, b2ev)), ("dB2", b2_gld)], inc="pe")
                            b2ev = kb.op("vector", lambda e, tt=tt, dd=dd: e.tensor_add(
                                out=x1[:, tt, dd * 512:(dd + 1) * 512], in0=x1[:, tt, dd * 512:(dd + 1) * 512], in1=P4),
                                waits=[("pe", bm)], inc="dve")
                    b2_dve = kb.now("dve")

                    yzm = kb.op("vector", lambda e: e.memset(episb[:], 0.0), inc="dve")
                    w_init = kb.now("dve")
                    for tt in range(NT):
                        kb.op("sync", lambda e, tt=tt: e.dma_start(out=d_y.ap()[tt * 128:(tt + 1) * 128, :], in_=x1[:, tt, :]),
                              waits=[("dve", w_init)], inc="dYZ", inck=16, unordered=True)
                    kb.op("sync", lambda e: e.dma_start(out=d_y.ap()[S:S + 128, :], in_=episb[:]),
                          waits=[("dve", yzm)], inc="dYZ", inck=16, unordered=True)

                    for e_i2 in range(E):
                        for sbk in range(8):
                            kb.op("sync", lambda e, e_i2=e_i2, sbk=sbk: e.dma_start(
                                out=idxall[:, e_i2, sbk:sbk + 1],
                                in_=AP(tensor=d_idx, offset=e_i2 * CAP + sbk * 128, ap=[[1, 128], [1, 1]])),
                                waits=[("dIX0", (NT // 2) * 2 * 16), ("dIX1", (NT // 2) * 2 * 16)], inc="dIL", inck=16, unordered=True)

                _maybe_stop("gate")
                kb.fence()
                x1s.close()
                fx = es.enter_context(contextlib.ExitStack())
                hT = sb(fx, "hT", [128, 32, CAP], BF16)
                w1sb = sb(fx, "w1sb", [128, 4, 8, 256], BF16)
                w2sb = sb(fx, "w2sb", [128, 4, 8, 512], BF16)
                xg = sb(fx, "xg", [128, 8, DM16], BF16)
                xgT = sb(fx, "xgT", [128, 8, CAP], BF16)
                mgcol = sb(fx, "mgcol", [128, 2, 8], F32)
                idxsc = sb(fx, "idxsc", [128, 8], I32)
                idxeq = sb(fx, "idxeq", [128, 8], I32)
                ctrash = sb(fx, "ctrash", [128, 8], I32)
                ysc = sb(fx, "ysc", [128, 8, DM], F32)
                psY = [P1[:, 0:512], P1[:, 512:1024], P2[:, 0:512], P2[:, 512:1024],
                       P3[:, 0:512], P3[:, 512:1024], P4t[:, 0:512], P4t[:, 512:1024]]

                idx_done = (NT // 2) * 2 * 16
                x1b_done = kb.now("dX1B")
                yz_done = kb.now("dYZ")

                il_total = kb.now("dIL")

                w1_free = [0, 0, 0, 0]
                w2_free = [0, 0, 0, 0]
                psh_ev = [("dve", 0), ("dve", 0)]
                y_ev = 0
                yev_bank = [0, 0]
                yev_first4 = 0
                mg_war = [0, 0]
                relu_alt = 0
                w1_last_pe = 0
                tp_done_prev = 0
                gx_total = [0] * E
                ys_prev_expert = {}

                def issue_xgathers(e_i, war_pe, war_dve):
                    for sbk in range(8):
                        kb.op("gpsimd", lambda e, e_i=e_i, sbk=sbk: e.indirect_dma_start(
                            out=xg[:, sbk], out_offset=None,
                            in_=d_x1b.ap(), in_offset=IndirectOffsetOnAxis(idxall[:, e_i, sbk:sbk + 1], 0)),
                            waits=[("dIL", il_total), ("dX1B", x1b_done), ("pe", war_pe), ("dve", war_dve)],
                            inc="dGX%d" % (e_i % 2), inck=16, unordered=True)
                    return kb.now("dGX%d" % (e_i % 2))

                kb.op("gpsimd", lambda e: e.iota(ctrash[:], pattern=[[1, 8]], base=0, channel_multiplier=0), inc="gpc")
                ct_gpc = kb.now("gpc")
                gx_total[0] = issue_xgathers(0, 0, 0)
                for e_i in range(E):
                    ep = e_i % 2
                    kb.op("vector", lambda e, e_i=e_i: e.tensor_copy(out=idxsc[:], in_=idxall[:, e_i, :]),
                          waits=[("gpc", ct_gpc), ("dIL", il_total)] + [(sn, cv) for sn, cv in ys_prev_expert.items()], inc="dve")
                    kb.op("vector", lambda e, e_i=e_i: e.tensor_scalar(out=idxeq[:], in0=idxall[:, e_i, :], scalar1=S, scalar2=None, op0=ALU.is_equal),
                          inc="dve")
                    kb.op("vector", lambda e: e.tensor_mul(out=idxeq[:], in0=idxeq[:], in1=ctrash[:]), inc="dve")
                    idxsc_dve = kb.op("vector", lambda e: e.tensor_add(out=idxsc[:], in0=idxsc[:], in1=idxeq[:]), inc="dve")
                    xgt_evs = [0, 0]
                    for sbk in range(8):
                        for c in range(8):
                            kb.op("tensor", lambda e, sbk=sbk, c=c, e_i=e_i: e.transpose(P6[:, sbk % 2, c], xg[:, sbk, c * 128:(c + 1) * 128], ident[:]),
                                  waits=[("dGX%d" % (e_i % 2), gx_total[e_i]), ("dve", xgt_evs[sbk % 2]), ("dve", yev_bank[sbk % 2])],
                                  inc="pe" if c == 7 else None)
                        xgt_evs[sbk % 2] = kb.op("vector", lambda e, sbk=sbk: e.tensor_copy(
                            out=AP(tensor=xgT, offset=sbk * 128, ap=[[8 * CAP, 128], [CAP, 8], [1, 128]]),
                            in_=P6[:, sbk % 2]),
                            waits=[("pe", kb.now("pe")), ("pe", w1_last_pe)], inc="dve")
                    for sbk in range(8):
                        kb.op("vector", lambda e, e_i=e_i, sbk=sbk, ep=ep: e.tensor_tensor(
                            out=mgcol[:, ep, sbk:sbk + 1], in0=xg[:, sbk, DM + e_i:DM + e_i + 1],
                            in1=xg[:, sbk, DM + 8 + e_i:DM + 9 + e_i], op=ALU.add),
                            waits=[("dGX%d" % (e_i % 2), gx_total[e_i])], inc="dve")
                    xgt_dve = kb.now("dve")
                    tp_done = kb.now("pe")
                    if e_i + 1 < E:
                        gx_total[e_i + 1] = issue_xgathers(e_i + 1, tp_done, xgt_dve)
                    for jg in range(16):
                        w1_ld = kb.op("sync", lambda e, e_i=e_i, jg=jg: e.dma_start(
                            out=w1sb[:, jg % 4],
                            in_=AP(tensor=d_w1b, offset=e_i * DM * HF + jg * 256,
                                   ap=[[HF, 128], [128 * HF, 8], [1, 256]])),
                            waits=[("pe", w1_free[jg % 4]), ("dWCo", (PC_TOTAL // 2) * 16)],
                            inc="dW1", inck=16)
                        for js in range(2):
                            jc = jg * 2 + js
                            pp = jc % 2
                            psH = (P1, P2)[pp]
                            for th in range(2):
                                for d in range(8):
                                    kb.op("tensor", lambda e, jg=jg, js=js, d=d, th=th, psH=psH: e.matmul(
                                        psH[:, th * 512:(th + 1) * 512],
                                        w1sb[:, jg % 4, d, js * 128:(js + 1) * 128],
                                        xgT[:, d, th * 512:(th + 1) * 512],
                                        start=(d == 0), stop=(d == 7)),
                                        waits=[("dW1", w1_ld), ("dve", max(xgt_dve, yev_first4)), psh_ev[pp]],
                                        inc="pe" if (d == 7 and th == 1) else None)
                            h_pe = kb.now("pe")
                            if relu_alt == 0:
                                m = kb.op("scalar", lambda e, e_i=e_i, jc=jc, psH=psH: e.activation(
                                    out=hT[:, jc, :], in_=psH[:], func=AF.Relu, bias=b1sb[:, e_i, jc:jc + 1]),
                                    waits=[("pe", h_pe)], inc="act")
                                psh_ev[pp] = ("act", m)
                            else:
                                m = kb.op("vector", lambda e, e_i=e_i, jc=jc, psH=psH: e.tensor_scalar(
                                    out=hT[:, jc, :], in0=psH[:],
                                    scalar1=b1sb[:, e_i, jc:jc + 1], scalar2=0.0,
                                    op0=ALU.add, op1=ALU.max),
                                    waits=[("pe", h_pe)], inc="dve")
                                psh_ev[pp] = ("dve", m)
                            relu_alt ^= 1
                        w1_free[jg % 4] = kb.now("pe")
                    w1_last_pe = kb.now("pe")
                    h_act = kb.now("act")
                    h_dve = kb.now("dve")
                    def emit_w2_load(k2, e_i=e_i):
                        ddh, jcg = divmod(k2, 4)
                        return kb.op("scalar", lambda e, e_i=e_i, jcg=jcg, ddh=ddh, k2=k2: e.dma_start(
                            out=w2sb[:, k2 % 4],
                            in_=AP(tensor=d_w2b, offset=e_i * HF * DM + jcg * 8 * 128 * DM + ddh * 512,
                                   ap=[[DM, 128], [128 * DM, 8], [1, 512]])),
                            waits=[("pe", w2_free[k2 % 4]), ("dWCo", PC_TOTAL * 16)],
                            inc="dW2", inck=16)
                    w2_marks = {}
                    for k2 in range(3):
                        w2_marks[k2] = emit_w2_load(k2)
                    for ddh in range(2):
                        for jcg in range(4):
                            k2 = ddh * 4 + jcg
                            if k2 + 3 < 8:
                                w2_marks[k2 + 3] = emit_w2_load(k2 + 3)
                            w2_ld = w2_marks[k2]
                            for ji in range(8):
                                jc = jcg * 8 + ji
                                for ts in range(8):
                                    kb.op("tensor", lambda e, jc=jc, ji=ji, k2=k2, ts=ts: e.matmul(
                                        psY[ts], hT[:, jc, ts * 128:(ts + 1) * 128],
                                        w2sb[:, k2 % 4, ji, :],
                                        start=(jc == 0), stop=(jc == 31)),
                                        waits=[("dW2", w2_ld), ("act", h_act), ("dve", h_dve)] +
                                              ([("dve", y_ev)] if jc == 0 else []),
                                        inc="pe" if ts == 7 else None)
                            w2_free[k2 % 4] = kb.now("pe")
                        y_pe = kb.now("pe")
                        for ts in range(8):
                            sname = "dY%d" % ts
                            y_ev = kb.op("vector", lambda e, ts=ts, ep=ep, ddh=ddh: e.tensor_scalar(
                                out=ysc[:, ts, ddh * 512:(ddh + 1) * 512], in0=psY[ts],
                                scalar1=mgcol[:, ep, ts:ts + 1], scalar2=None, op0=ALU.mult),
                                waits=[("pe", y_pe)] + ([(sname, ys_prev_expert.get(sname, 0))] if ddh == 0 else []), inc="dve")
                            if ddh == 1:
                                kb.op("gpsimd", lambda e, ts=ts: e.indirect_dma_start(
                                    out=d_y.ap(), out_offset=IndirectOffsetOnAxis(idxsc[:, ts:ts + 1], 0),
                                    in_=ysc[:, ts, :], in_offset=None,
                                    compute_op=ALU.add),
                                    waits=[("dve", y_ev), ("dYZ", yz_done)] +
                                          [(sn, cv) for sn, cv in ys_prev_expert.items()],
                                    inc=sname, inck=16)
                            if ddh == 1 and ts == 1:
                                yev_bank = [None, None]
                                yev_bank[0] = y_ev - 1
                                yev_bank[1] = y_ev
                            if ddh == 1 and ts == 3:
                                yev_first4 = y_ev
                    mg_war[ep] = y_ev
                    tp_done_prev = tp_done
                    ys_prev_expert = {("dY%d" % t): kb.now("dY%d" % t) for t in range(8)}
                ffn_dve = kb.now("dve")

                fx.close()
                kb.fence()
                outsb = sb(pr, "outsb", [128, 2, DM], F32)
                yld = sb(pr, "yld", [128, 4, DM], F32)
                meanall = sb(pr, "meanall", [128, NT], F32)
                msq = sb(pr, "msq", [128, NT], F32)
                ys_totals = [("dY%d" % t, kb.now("dY%d" % t)) for t in range(8)]
                YL = ["dYL", "dYL1", "dYL2", "dYL3"]
                war_d = [0, 0, 0, 0]
                war_a = [0, 0, 0, 0]
                for tt in range(NT):
                    ly = kb.op("sync", lambda e, tt=tt: e.dma_start(out=yld[:, tt % 4], in_=d_y.ap()[tt * 128:(tt + 1) * 128, :]),
                               waits=ys_totals + [("dve", war_d[tt % 4]), ("act", war_a[tt % 4])], inc=YL[tt % 4], inck=16)
                    war_d[tt % 4] = kb.op("vector", lambda e, tt=tt: e.reduce_sum(out=meanall[:, tt:tt + 1], in_=yld[:, tt % 4], axis=AX.X),
                                          waits=[(YL[tt % 4], ly)], inc="dve")
                    war_a[tt % 4] = kb.op("scalar", lambda e, tt=tt: e.activation(out=sqt[:], in_=yld[:, tt % 4], func=AF.Square, accum_out=varall[:, tt:tt + 1]),
                          waits=[(YL[tt % 4], ly), ("act", kb.now("act"))] + ([("dYZ", (NT + 1) * 16)] if tt == 0 else []), inc="act")
                kb.op("vector", lambda e: e.tensor_scalar(out=meanall[:], in0=meanall[:], scalar1=1.0 / DM, scalar2=None, op0=ALU.mult), inc="dve")
                kb.op("vector", lambda e: e.tensor_mul(out=msq[:], in0=meanall[:], in1=meanall[:]), inc="dve")
                kb.op("vector", lambda e: e.tensor_scalar(out=varall[:], in0=varall[:], scalar1=1.0 / DM, scalar2=None, op0=ALU.mult),
                      waits=[("act", kb.now("act"))], inc="dve")
                kb.op("vector", lambda e: e.tensor_sub(out=varall[:], in0=varall[:], in1=msq[:]), inc="dve")
                a2 = kb.op("scalar", lambda e: e.activation(out=rstd[:], in_=varall[:], func=AF.Ln, bias=1e-5),
                           waits=[("dve", kb.now("dve"))], inc="act")
                kb.op("scalar", lambda e: e.activation(out=rstd[:], in_=rstd[:], func=AF.Exp, scale=-0.5),
                      waits=[("act", a2)], inc="act")
                ln2_act = kb.now("act")
                OUTS = ["out", "out1"]
                out_marks = [0, 0]
                for tt in range(NT):
                    ly = kb.op("sync", lambda e, tt=tt: e.dma_start(out=yld[:, tt % 4], in_=d_y.ap()[tt * 128:(tt + 1) * 128, :]),
                               waits=[("act", ln2_act), ("dve", war_d[tt % 4])], inc=YL[tt % 4], inck=16)
                    od = kb.op("vector", lambda e, tt=tt: e.tensor_scalar(out=outsb[:, tt % 2], in0=yld[:, tt % 4],
                                                                          scalar1=meanall[:, tt:tt + 1], scalar2=rstd[:, tt:tt + 1],
                                                                          op0=ALU.subtract, op1=ALU.mult),
                               waits=[(YL[tt % 4], ly), (OUTS[tt % 2], out_marks[tt % 2])], inc="dve")
                    war_d[tt % 4] = od
                    out_marks[tt % 2] = kb.op("sync", lambda e, tt=tt: e.dma_start(out=d_out.ap()[tt * 128:(tt + 1) * 128, :], in_=outsb[:, tt % 2]),
                          waits=[("dve", od)], inc=OUTS[tt % 2], inck=16)
                kb.wait("sync", ("out", out_marks[0]), ("out1", out_marks[1]))


        except _Stop:
            pass

        blk = es.enter_context(nc.Block())
        kb.emit(blk)

    return nc


_NC_CACHE = None


def kernel(L0_attn_x, L0_attn_qkv_w, L0_attn_out_w, L0_attn_bias_p, L0_attn_bias_a,
           L0_attn_rope_freqs, L0_ffn_W_gate, L0_ffn_W1, L0_ffn_b1, L0_ffn_W2, L0_ffn_b2):
    global _NC_CACHE
    if _NC_CACHE is None:
        _NC_CACHE = build_program()
    nc = _NC_CACHE
    weights = {
        "qkvw": np.ascontiguousarray(np.asarray(L0_attn_qkv_w, dtype=np.float32)),
        "outw": np.ascontiguousarray(np.asarray(L0_attn_out_w, dtype=np.float32)),
        "bp": np.ascontiguousarray(np.asarray(L0_attn_bias_p, dtype=np.float32).reshape(H)),
        "ba": np.ascontiguousarray(np.asarray(L0_attn_bias_a, dtype=np.float32).reshape(H)),
        "freqs": np.ascontiguousarray(np.asarray(L0_attn_rope_freqs, dtype=np.float32)),
        "wg": np.ascontiguousarray(np.asarray(L0_ffn_W_gate, dtype=np.float32)),
        "w1": np.ascontiguousarray(np.asarray(L0_ffn_W1, dtype=np.float32)),
        "b1": np.ascontiguousarray(np.asarray(L0_ffn_b1, dtype=np.float32)),
        "w2": np.ascontiguousarray(np.asarray(L0_ffn_W2, dtype=np.float32)),
        "b2": np.ascontiguousarray(np.asarray(L0_ffn_b2, dtype=np.float32)),
    }
    x = np.asarray(L0_attn_x, dtype=np.float32)
    in_maps = []
    for i in range(8):
        m = {"x": np.ascontiguousarray(x[i])}
        m.update(weights)
        in_maps.append(m)
    res = run_bass_kernel_spmd(nc, in_maps, core_ids=list(range(8)))
    return np.stack([np.asarray(r["out"], dtype=np.float32) for r in res.results], axis=0)
